# version 13
# baseline (speedup 1.0000x reference)
import numpy as np
import concourse.bass as bass
import concourse.mybir as mybir
from concourse.bass_utils import run_bass_kernel_spmd
from concourse.bass_core_rust import IndirectOffsetOnAxis

F32 = mybir.dt.float32
BF16 = mybir.dt.bfloat16
I32 = mybir.dt.int32
AF = mybir.ActivationFunctionType
ALU = mybir.AluOpType
AXX = mybir.AxisListType.X

NCORES = 8
B, S, D = 8, 2048, 1024
NT = 2048
NCH = D // 128
NTT = NT // 128
H0, KVH0, DH = 16, 4, 64
QL, KVL = 64, 32
E0, DFF0, CAP0 = 8, 4096, 2560
C0 = 384
E1, DFF1 = 8, 2048
C1 = 768
BIGIDX = float(1 << 22)
LN_EPS = 1e-5

ENGINES = ("pe", "act", "dve", "pool", "sp")


class Prog:

    def __init__(self, nc):
        self.nc = nc
        self.q = {e: [] for e in ENGINES}
        self.sems = {}
        self.counts = {}
        self.hwm = {e: {} for e in ENGINES}

    def sem(self, name):
        if name not in self.sems:
            self.sems[name] = self.nc.semaphore(name).__enter__()
            self.counts[name] = 0
        return self.sems[name]

    def emit(self, eng, fn, waits=(), inc=None, step=1):
        wl = []
        for t in waits:
            if t is None:
                continue
            sname, val = t
            if val <= 0 or self.hwm[eng].get(sname, 0) >= val:
                continue
            self.hwm[eng][sname] = val
            self.sem(sname)
            wl.append((sname, val))
        tok = None
        if inc is not None:
            self.sem(inc)
            self.counts[inc] += step
            tok = (inc, self.counts[inc])
        self.q[eng].append((wl, fn, inc, step))
        return tok

    def tok(self, semname):
        return (semname, self.counts.get(semname, 0))

    def frontier(self):
        return [(s, v) for s, v in self.counts.items()]

    def mm(self, out, lhsT, rhs, start, stop, waits=(), inc=None):
        return self.emit(
            "pe",
            lambda e, o=out, l=lhsT, r=rhs, a=start, b=stop: e.matmul(
                o, l, r, start=a, stop=b),
            waits, inc)

    def tr(self, out, in_, ident, waits=(), inc=None):
        return self.emit(
            "pe",
            lambda e, o=out, i=in_, d=ident: e.transpose(o, i, d),
            waits, inc)

    def dve(self, name, *a, waits=(), inc=None, lowp=False, **kw):
        if lowp:
            def fn(e, a=a, kw=kw, n=name):
                with self.nc.allow_low_precision("bf16 store of fp32-internal "
                                                 "8-way reduction"):
                    return getattr(e, n)(*a, **kw)
            return self.emit("dve", fn, waits, inc)
        return self.emit(
            "dve", lambda e, a=a, kw=kw, n=name: getattr(e, n)(*a, **kw),
            waits, inc)

    def act(self, name, *a, waits=(), inc=None, **kw):
        return self.emit(
            "act", lambda e, a=a, kw=kw, n=name: getattr(e, n)(*a, **kw),
            waits, inc)

    def pool(self, name, *a, waits=(), inc=None, step=1, **kw):
        return self.emit(
            "pool", lambda e, a=a, kw=kw, n=name: getattr(e, n)(*a, **kw),
            waits, inc, step)

    def dma(self, q, out, in_, sem, waits=(), **kw):
        return self.emit(
            q, lambda e, o=out, i=in_, kw=kw: e.dma_start(out=o, in_=i, **kw),
            waits, inc=sem, step=16)

    def dma_tr(self, q, out, in_, sem, waits=()):
        return self.emit(
            q, lambda e, o=out, i=in_: e.dma_start(out=o, in_=i, transpose=True),
            waits, inc=sem, step=16)

    def wait(self, eng, waits):
        self.emit(eng, lambda e: None, waits)

    def finalize(self):
        nc = self.nc
        prog = self
        with nc.Block() as blk:
            def runner(items):
                def run(eng):
                    for idx, (wl, fn, inc, step) in enumerate(items):
                        for sname, val in wl:
                            eng.wait_ge(prog.sems[sname], val)
                        try:
                            inst = fn(eng)
                        except Exception as ex:
                            raise RuntimeError(
                                "emit #%d on %s failed: %r (args %r)" % (
                                    idx, eng.engine,
                                    ex, getattr(fn, "__defaults__", None))
                            ) from ex
                        if inc is not None:
                            inst.then_inc(prog.sems[inc], step)
                return run
            if self.q["sp"]:
                blk.sync(runner(self.q["sp"]))
            if self.q["pe"]:
                blk.tensor(runner(self.q["pe"]))
            if self.q["act"]:
                blk.scalar(runner(self.q["act"]))
            if self.q["dve"]:
                blk.vector(runner(self.q["dve"]))
            if self.q["pool"]:
                blk.gpsimd(runner(self.q["pool"]))
        return nc


class NS(dict):
    __getattr__ = dict.__getitem__
    __setattr__ = dict.__setitem__



def _rope_tables():
    inv = 10000.0 ** (-np.arange(0, DH, 2, dtype=np.float64) / DH)
    t = np.arange(S, dtype=np.float64)
    fr = np.outer(t, inv)
    cos = np.cos(fr).astype(np.float32)
    sin = np.sin(fr).astype(np.float32)
    cos_il = np.empty((DH, S), np.float32)
    sin_il = np.empty((DH, S), np.float32)
    cos_il[0::2] = cos.T
    cos_il[1::2] = cos.T
    sin_il[0::2] = -sin.T
    sin_il[1::2] = sin.T
    return (np.ascontiguousarray(np.concatenate([cos_il, cos_il], 0)),
            np.ascontiguousarray(np.concatenate([sin_il, sin_il], 0)))


def _swap_cols(w, group):
    n = w.shape[-1]
    idx = np.arange(n)
    d = idx % group
    d2 = np.where(d % 2 == 0, d + 1, d - 1)
    return np.ascontiguousarray(w[..., (idx // group) * group + d2])


def _causal_masks():
    kt = np.arange(128)[:, None]
    u = np.arange(896)[None, :]
    return np.ascontiguousarray((kt <= u - 384).astype(np.float32))


def _chunked(w):
    f = w.shape[1]
    return np.ascontiguousarray(
        w.reshape(NCH, 128, f).transpose(1, 0, 2).reshape(128, NCH * f))


def host_constants():
    cos_il, sin_il = _rope_tables()
    return {
        "c_ident": np.eye(128, dtype=np.float32),
        "c_masks": _causal_masks(),
        "c_iota8": np.tile(np.arange(8, dtype=np.float32), (128, 1)),
        "c_ones_col": np.ones((128, 1), np.float32),
        "c_ones_row": np.ones((1, 512), np.float32),
        "c_rope_cos": cos_il,
        "c_rope_sin": sin_il,
        "c_fconst": np.tile(np.array([[0.0, 1.0, LN_EPS, 0.125,
                                       0.0, 0.0, 0.0, 0.0]], np.float32),
                            (128, 1)),
        "c_onesf": np.ones((128, 64), np.float32),
        "c_iota512": np.tile(np.arange(512, dtype=np.float32), (128, 1)),
        "c_niota_row": -np.arange(128, dtype=np.float32)[None, :],
    }


def const_input_specs():
    return {
        "c_ident": ([128, 128], F32),
        "c_masks": ([128, 896], F32),
        "c_iota8": ([128, 8], F32),
        "c_ones_col": ([128, 1], F32),
        "c_ones_row": ([1, 512], F32),
        "c_rope_cos": ([128, NT], F32),
        "c_rope_sin": ([128, NT], F32),
        "c_fconst": ([128, 8], F32),
        "c_onesf": ([128, 64], F32),
        "c_iota512": ([128, 512], F32),
        "c_niota_row": ([1, 128], F32),
    }



def alloc_common(nc):
    k = NS()
    k.x = nc.alloc_sbuf_tensor("A_x", [128, NCH * NT], F32)
    k.xn = nc.alloc_sbuf_tensor("A_xn", [128, NCH * NT], BF16)
    k.bufa = nc.alloc_sbuf_tensor("BUF_A", [128, 16384], BF16)
    k.bufb = nc.alloc_sbuf_tensor("BUF_B", [128, 16384], BF16)
    k.ident = nc.alloc_sbuf_tensor("ident", [128, 128], F32)
    k.ident_bf = nc.alloc_sbuf_tensor("ident_bf", [128, 128], BF16)
    k.masks = nc.alloc_sbuf_tensor("masks", [128, 896], BF16)
    k.iota8 = nc.alloc_sbuf_tensor("iota8", [128, 8], F32)
    k.ones_col = nc.alloc_sbuf_tensor("ones_col", [128, 1], F32)
    k.ones_col_bf = nc.alloc_sbuf_tensor("ones_col_bf", [128, 1], BF16)
    k.ones_row = nc.alloc_sbuf_tensor("ones_row", [1, 512], F32)
    k.ones_row_bf = nc.alloc_sbuf_tensor("ones_row_bf", [1, 512], BF16)
    k.rope = nc.alloc_sbuf_tensor("rope", [128, 2 * NT], BF16)
    k.rowbuf = nc.alloc_sbuf_tensor("rowbuf", [1, 1024], F32)
    k.stat = nc.alloc_sbuf_tensor("stat", [128, 96], F32)
    k.route = nc.alloc_sbuf_tensor("route", [128, 256], F32)
    k.ridx = nc.alloc_sbuf_tensor("ridx", [128, 64], I32)
    k.scr = nc.alloc_sbuf_tensor("scr", [128, 2048], F32)
    k.scr2 = nc.alloc_sbuf_tensor("scr2", [128, 1024], F32)
    k.stage = nc.alloc_sbuf_tensor("stage", [128, 2048], BF16)
    k.biasrow = nc.alloc_sbuf_tensor("biasrow", [1, 2048], BF16)
    k.biasrow2 = nc.alloc_sbuf_tensor("biasrow2", [1, 2048], BF16)
    k.bcols = nc.alloc_sbuf_tensor("bcols", [128, 64], F32)
    k.cnt = nc.alloc_sbuf_tensor("cnt", [128, 16], F32)
    k.fconst = nc.alloc_sbuf_tensor("fconst", [128, 8], F32)
    k.onesf = nc.alloc_sbuf_tensor("onesf", [128, 64], F32)
    k.iota512 = nc.alloc_sbuf_tensor("iota512", [128, 512], F32)
    k.niota_row = nc.alloc_sbuf_tensor("niota_row", [1, 128], F32)
    k.gatef = nc.alloc_sbuf_tensor("gatef", [128, 64], F32)
    k.zero8 = nc.alloc_sbuf_tensor("zero8", [8, 1], F32)
    k.ps0 = nc.alloc_psum_tensor("ps0", [128, 512], F32)
    k.ps1 = nc.alloc_psum_tensor("ps1", [128, 512], F32)
    k.ps2 = nc.alloc_psum_tensor("ps2", [128, 512], F32)
    k.ps3 = nc.alloc_psum_tensor("ps3", [128, 512], F32)
    k.pq = nc.alloc_psum_tensor("pq", [128, 1024], F32)
    k.pt0 = nc.alloc_psum_tensor("pt0", [128, 1024], BF16)
    k.pt1 = nc.alloc_psum_tensor("pt1", [128, 1024], BF16)
    return k


def declare_const_inputs(nc):
    d = {}
    for name, (shape, dt) in const_input_specs().items():
        d[name] = nc.dram_tensor(name, shape, dt, kind="ExternalInput").ap()
    return d


def e_load_consts(p, k, dram):
    for i, v in enumerate([0.0, 1.0, LN_EPS, 0.125]):
        p.nc.const_aps.register(F32, v, k.fconst[:, i:i + 1])
    for dst, name in [(k.ident[:, :], "c_ident"), (k.iota8[:, :], "c_iota8"),
                      (k.ones_col[:, :], "c_ones_col"),
                      (k.ones_row[:, :], "c_ones_row"),
                      (k.fconst[:, :], "c_fconst"),
                      (k.onesf[:, :], "c_onesf"),
                      (k.iota512[:, :], "c_iota512"),
                      (k.niota_row[:, :], "c_niota_row")]:
        p.dma("sp", dst, dram[name], "dc1")
    for dst, src in [
            (k.ident_bf[:, :], dram["c_ident"]),
            (k.ones_col_bf[:, :], dram["c_ones_col"]),
            (k.ones_row_bf[:, :], dram["c_ones_row"]),
            (k.rope[:, 0:NT], dram["c_rope_cos"]),
            (k.rope[:, NT:2 * NT], dram["c_rope_sin"]),
            (k.masks[:, :], dram["c_masks"])]:
        p.dma("pool", dst, src, "dc2")
    t = p.dve("memset", k.zero8[:, :], 0.0, inc="sdve")
    p.dve("memset", k.ridx[:, :], 0, inc="sdve")
    t2 = p.dve("memset", k.route[:, :], 0.0, inc="sdve")
    return [p.tok("dc1"), p.tok("dc2"), t, t2]


def e_load_x(p, k, src):
    toks = []
    for c in range(NCH):
        p.dma("sp", k.x[:, c * NT:(c + 1) * NT],
              src[c * 128:(c + 1) * 128, :], "dx%d" % c)
        toks.append(p.tok("dx%d" % c))
    return toks


def e_ln_plain(p, k, src_tok, chunk_toks=None, gt_out=None):
    xbf = k.bufa[:, 0:512]
    sqbf = k.bufa[:, 512:1024]
    outs = []
    a2 = None
    m2 = None
    bm_prev = br_prev = None
    t_m_prev = t_r_prev = None
    for g in range(4):
        m1 = None
        for c in range(NCH):
            off = c * NT + g * 512
            w = list(src_tok) if (g == 0 and c == 0) else []
            if chunk_toks is not None and g == 0:
                w.append(chunk_toks[c])
            if m2 is not None:
                w.append(m2)
            cpt = p.dve("tensor_copy", xbf, k.x[:, off:off + 512], waits=w,
                        inc="sdve")
            sqt = p.dve("tensor_tensor", sqbf, xbf, xbf, ALU.mult,
                        waits=[cpt], inc="sdve")
            w1 = [cpt]
            if c == 0 and t_m_prev is not None:
                w1.append(t_m_prev)
            m1 = p.mm(k.ps2[0:1, 0:512], k.ones_col_bf[:, :], xbf,
                      start=(c == 0), stop=(c == NCH - 1), waits=w1,
                      inc="spe" if c == NCH - 1 else None)
            w2 = [sqt]
            if c == 0 and t_r_prev is not None:
                w2.append(t_r_prev)
            m2 = p.mm(k.ps3[0:1, 0:512], k.ones_col_bf[:, :], sqbf,
                      start=(c == 0), stop=(c == NCH - 1), waits=w2,
                      inc="spe")
        rw = [x for x in (bm_prev, t_m_prev) if x is not None]
        t_m = p.dve("tensor_scalar_mul", k.rowbuf[0:1, 0:512], k.ps2[0:1, :],
                    1.0 / D, waits=[m1] + rw, inc="sdve")
        t_m_prev = t_m
        t_s = p.dve("tensor_tensor", k.scr[0:1, 1024:1536],
                    k.rowbuf[0:1, 0:512], k.rowbuf[0:1, 0:512], ALU.mult,
                    waits=[t_m] + ([t_r_prev] if t_r_prev is not None else []),
                    inc="sdve")
        t_v = p.dve("scalar_tensor_tensor", k.scr[0:1, 1536:2048],
                    k.ps3[0:1, :], 1.0 / D, k.scr[0:1, 1024:1536],
                    ALU.mult, ALU.subtract, waits=[m2, t_s], inc="sdve")
        t_d = p.act("activation", k.scr[0:1, 1024:1536], k.scr[0:1, 1536:2048],
                    AF.Sqrt, bias=LN_EPS, scale=1.0, waits=[t_v], inc="sact")
        rw = [br_prev] if br_prev is not None else []
        t_r = p.dve("reciprocal", k.rowbuf[0:1, 512:1024],
                    k.scr[0:1, 1024:1536], waits=[t_d] + rw, inc="sdve")
        t_r_prev = t_r
        wb = [t_m] + ([a2] if a2 is not None else [])
        bm = p.mm(k.ps0[:, :], k.ones_row[0:1, 0:128], k.rowbuf[0:1, 0:512],
                  start=True, stop=True, waits=wb, inc="spe")
        br = p.mm(k.ps1[:, :], k.ones_row[0:1, 0:128], k.rowbuf[0:1, 512:1024],
                  start=True, stop=True, waits=[t_r], inc="spe")
        bm_prev, br_prev = bm, br
        tcol_toks = []
        for j in range(4):
            t = g * 4 + j
            w = [t_m] if j == 0 else [tcol_toks[-1]]
            tr1 = p.tr(k.ps2[0:128, 0:1], k.rowbuf[0:1, j * 128:(j + 1) * 128],
                       k.ident[0:1, 0:1], waits=w, inc="spe")
            c1 = p.dve("tensor_copy", k.stat[:, t:t + 1], k.ps2[0:128, 0:1],
                       waits=[tr1], inc="sdve")
            tr2 = p.tr(k.ps3[0:128, 0:1],
                       k.rowbuf[0:1, 512 + j * 128:512 + (j + 1) * 128],
                       k.ident[0:1, 0:1], waits=[t_r, c1], inc="spe")
            c2 = p.dve("tensor_copy", k.stat[:, 16 + t:17 + t],
                       k.ps3[0:128, 0:1], waits=[tr2], inc="sdve")
            tcol_toks.append(c2)
        t_m_prev = tcol_toks[-1]
        t_r_prev = tcol_toks[-1]
        for c in range(NCH):
            off = c * NT + g * 512
            a1 = p.dve("tensor_tensor", k.scr[:, 512:1024],
                       k.x[:, off:off + 512],
                       k.ps0[:, :], ALU.subtract,
                       waits=[bm] + ([a2] if a2 is not None else []),
                       inc="sdve")
            a2 = p.dve("tensor_tensor", k.xn[:, off:off + 512],
                       k.scr[:, 512:1024],
                       k.ps1[:, :], ALU.mult, waits=[br, a1], inc="sdve")
            outs.append(a2)
        if gt_out is not None:
            gt_out.append([outs[-1], tcol_toks[-1]])
    return outs


def e_routing_logits(p, k, t, gatef, negsg_row, mrow_bank, acc_bank,
                     lg_out, extra_w=()):
    trm = p.tr(mrow_bank[0:1, 0:128], k.stat[:, t:t + 1], k.ident[:, :],
               waits=list(extra_w), inc="spe")
    mrow = k.scr[0:1, 384:512]
    cm = p.dve("tensor_copy", mrow, mrow_bank[0:1, 0:128], waits=[trm],
               inc="sdve")
    lt = None
    for c in range(NCH):
        lt = p.mm(acc_bank[:, 0:8],
                  k.x[:, c * NT + t * 128:c * NT + (t + 1) * 128],
                  gatef[:, c * 8:(c + 1) * 8], start=(c == 0), stop=False,
                  waits=list(extra_w))
    lt = p.mm(acc_bank[:, 0:8], mrow, negsg_row, start=False, stop=True,
              waits=[cm], inc="spe")
    lg = p.dve("tensor_single_scalar", lg_out, acc_bank[:, 0:8],
               k.stat[:, 16 + t:17 + t], ALU.mult, waits=[lt], inc="sdve")
    return lg


def e_negsg(p, k, gatef, gtok, bank):
    sg = None
    for c in range(NCH):
        sg = p.mm(bank[0:1, 0:8], k.ones_col[:, :],
                  gatef[:, c * 8:(c + 1) * 8], start=(c == 0),
                  stop=(c == NCH - 1), waits=[gtok], inc="spe")
    out = k.scr[0:1, 368:376]
    t = p.dve("tensor_scalar_mul", out, bank[0:1, 0:8], -1.0, waits=[sg],
              inc="sdve")
    return out, t


def e_l0_counts(p, k, dram, xn_toks, gtoks=None):
    gtw = p.dma("sp", k.gatef[:, :], dram["gate_t"], "dgt")
    gtok = p.tok("dgt")
    negsg, nt = e_negsg(p, k, k.gatef, gtok, k.pq)
    ohfm = k.bufb[:, :].bitcast(F32)[0:8, 2048:2048 + NT]
    oh_toks = []
    lgs = [None] * NTT
    cps = [None] * NTT
    ohs_tm = []
    for t in range(NTT):
        if gtoks is not None:
            w = list(gtoks[t // 4]) + [nt]
            if t >= 1:
                w.append(lgs[t - 1])
            bank = k.pq[:, 512:1024]
        else:
            w = list(xn_toks) + [nt]
            if t >= 2:
                w.append(lgs[t - 2])
            bank = k.ps2 if t % 2 == 0 else k.ps3
        lg = e_routing_logits(p, k, t, k.gatef, negsg, k.pq, bank,
                              k.scr[:, t * 8:t * 8 + 8], extra_w=w)
        lgs[t] = lg
        mx = p.dve("reduce_max", k.route[:, t * 16 + 1:t * 16 + 2],
                   k.scr[:, t * 8:t * 8 + 8], waits=[lg], inc="sdve",
                   axis=AXX)
        nmx = p.dve("reduce_max", k.route[:, t * 16 + 2:t * 16 + 3],
                    k.scr[:, t * 8:t * 8 + 8], waits=[lg], inc="sdve",
                    axis=AXX, negate=True)
        ex = p.act("activation", k.scr2[:, t * 8:t * 8 + 8],
                   k.scr[:, t * 8:t * 8 + 8], AF.Exp,
                   bias=k.route[:, t * 16 + 2:t * 16 + 3], scale=1.0,
                   accum_out=k.route[:, t * 16 + 3:t * 16 + 4],
                   waits=[nmx], inc="sact")
        p.dve("reciprocal", k.route[:, t * 16 + 0:t * 16 + 1],
              k.route[:, t * 16 + 3:t * 16 + 4], waits=[ex], inc="sdve")
        oh = p.dve("tensor_single_scalar", k.scr2[:, t * 8:t * 8 + 8],
                   k.scr[:, t * 8:t * 8 + 8],
                   k.route[:, t * 16 + 1:t * 16 + 2], ALU.is_ge,
                   waits=[mx, ex], inc="sdve")
        ohs_tm.append(oh)
    for t in range(NTT):
        w = [ohs_tm[t]] + ([cps[t - 2]] if t >= 2 else xn_toks)
        tbank = k.ps0 if t % 2 == 0 else k.ps1
        tt = p.tr(tbank[0:8, 0:128], k.scr2[:, t * 8:t * 8 + 8],
                  k.ident[:, :], waits=w, inc="spe")
        cp = p.dve("tensor_copy", ohfm[:, t * 128:(t + 1) * 128],
                   tbank[0:8, 0:128], waits=[tt], inc="sdve")
        cps[t] = cp
        oh_toks.append(cp)
    ct = p.dve("reduce_sum", k.cnt[0:8, 0:1], ohfm[:, :], waits=oh_toks,
               inc="sdve", axis=AXX)
    return ct, oh_toks


NSLOT0 = E0 * C0
NSLOT1 = E1 * C1
HPAIR = [(c, c + 4) for c in range(4)] + [(c + 8, c + 12) for c in range(4)]
ATTN_PERM = [h * 64 + i for pair in HPAIR for h in pair for i in range(64)]


def e_l0_ranks(p, k, dram, ct, oh_toks, counts_tok=None):
    fm = k.bufb[:, :].bitcast(F32)
    ranks, ohfm = fm[0:8, 0:NT], fm[0:8, 2048:2048 + NT]
    tmpA, tmpB = fm[0:8, 4096:4096 + NT], fm[0:8, 6144:6144 + NT]
    if counts_tok is None:
        p.dma("sp", k.cnt[0:8, 8:16], dram["cnts_all"], "dca")
    p.dma("sp", k.cnt[0:8, 2:3], dram["mymask"], "dca")
    p.dma("sp", k.cnt[0:8, 3:4], dram["c_ec0"], "dca")
    catok = p.tok("dca")
    pf = p.mm(k.pq[0:8, 0:1], k.cnt[0:8, 8:16], k.cnt[0:8, 2:3],
              start=True, stop=True, waits=[catok], inc="spe")
    pc = p.dve("tensor_scalar_add", k.cnt[0:8, 1:2], k.pq[0:8, 0:1], -1.0,
               waits=[pf], inc="sdve")
    sc = p.dve("tensor_tensor_scan", ranks, ohfm,
               k.zero8[0:8, 0:1].broadcast_to([8, NT]), 0.0, ALU.add, ALU.add,
               waits=oh_toks + [ct], inc="sdve")
    t1 = p.dve("tensor_scalar_add", tmpA, ranks, k.cnt[0:8, 1:2],
               waits=[sc, pc], inc="sdve")
    t2 = p.dve("tensor_single_scalar", tmpA, tmpA, CAP0 - 0.5, ALU.is_le,
               waits=[t1], inc="sdve")
    t3 = p.dve("tensor_single_scalar", tmpB, ranks, C0 - 0.5, ALU.is_le,
               waits=[sc], inc="sdve")
    t4 = p.dve("tensor_tensor", tmpA, tmpA, tmpB, ALU.mult, waits=[t2, t3],
               inc="sdve")
    t5 = p.dve("tensor_tensor", tmpA, tmpA, ohfm, ALU.mult, waits=[t4],
               inc="sdve")
    t6 = p.dve("tensor_scalar_add", tmpB, ranks, k.cnt[0:8, 3:4],
               waits=[t3, t5], inc="sdve")
    t7 = p.dve("tensor_tensor", tmpB, tmpB, tmpA, ALU.mult, waits=[t6],
               inc="sdve")
    return e_extract_idx(p, k, tmpA, tmpB, [t5, t7], 0, stash_col=11)


def e_extract_idx(p, k, keep_fm, contrib_fm, toks, out_col, stash_col=None,
                  big=BIGIDX):
    idx_toks = []
    e1 = e4 = None
    for t in range(NTT):
        w = list(toks)
        if t >= 1:
            w.append(idx_toks[t - 1])
        mk1 = p.mm(k.ps0[:, 0:1], keep_fm[:, t * 128:(t + 1) * 128],
                   k.ones_col[0:8, 0:1], start=True, stop=True, waits=w,
                   inc="spe")
        mk2 = p.mm(k.ps1[:, 0:1], contrib_fm[:, t * 128:(t + 1) * 128],
                   k.ones_col[0:8, 0:1], start=True, stop=True, waits=w,
                   inc="spe")
        e1 = p.dve("tensor_copy", k.scr[:, 256:257], k.ps0[:, 0:1],
                   waits=[mk1] + ([e4] if e4 else []), inc="sdve")
        e2 = p.dve("scalar_tensor_tensor", k.scr[:, 257:258],
                   k.scr[:, 256:257], -big, k.ps1[:, 0:1],
                   ALU.mult, ALU.add, waits=[mk2, e1], inc="sdve")
        e3 = p.dve("tensor_scalar_add", k.scr[:, 258:259], k.scr[:, 257:258],
                   big, waits=[e2], inc="sdve")
        if stash_col is not None:
            p.dve("tensor_copy",
                  k.route[:, t * 16 + stash_col:t * 16 + stash_col + 1],
                  k.scr[:, 258:259], waits=[e3], inc="sdve")
        e4 = p.dve("tensor_copy", k.ridx[:, out_col + t:out_col + t + 1],
                   k.scr[:, 258:259], waits=[e3], inc="sdve")
        idx_toks.append(e4)
    return idx_toks


def e_dispatch(p, k, disp_dram, nslot, idx_cols, idx_toks, src_toks, zsem,
               scsem, nidx=1):
    zm = p.dve("memset", k.stage[:, 1024:2048], 0.0, waits=src_toks,
               inc="sdve")
    for j in range(nslot // 128):
        p.dma("act", disp_dram[j * 128:(j + 1) * 128, :],
              k.stage[:, 1024:2048], zsem, waits=[zm])
    ztok = p.tok(zsem)
    sc_prev = {}
    sct = None
    for t in range(NTT):
        cps = []
        cp = None
        for c in range(NCH):
            w = []
            if c == 0 and (t, "s") in sc_prev:
                w.append(sc_prev[(t, "s")])
            bank = k.pt0 if c % 2 == 0 else k.pt1
            tt = p.tr(bank[0:128, 0:128],
                      k.xn[:, c * NT + t * 128:c * NT + (t + 1) * 128],
                      k.ident_bf[:, :],
                      waits=w + src_toks + ([cps[c - 2]] if c >= 2 else []),
                      inc="spe")
            cp = p.dve("tensor_copy", k.stage[:, c * 128:(c + 1) * 128],
                       bank[0:128, 0:128], waits=[tt] + w, inc="sdve")
            cps.append(cp)
        for j, col in enumerate(idx_cols):
            sct = p.pool("indirect_dma_start", disp_dram,
                         IndirectOffsetOnAxis(k.ridx[:, col + t:col + t + 1], 0),
                         k.stage[:, 0:1024], None,
                         bounds_check=nslot - 1, oob_is_err=False,
                         waits=[cp, idx_toks[j][t], ztok], inc=scsem, step=16)
        sc_prev[(t + 1, "s")] = sct
    return p.tok(scsem)


def e_ffn_l0_expert(p, k, dram, disp_dram, eout_dram, e, state):
    xw = list(state.get("xw", []))
    for c in range(NCH):
        p.dma_tr("sp", k.bufa[:, c * 512:(c + 1) * 512],
                 disp_dram[e * C0:(e + 1) * C0, c * 128:(c + 1) * 128],
                 "dxe", waits=xw if c == 0 else ())
    xet = p.tok("dxe")
    b1t = p.dma("sp", k.bcols[:, 0:32], dram["b1c"][e], "db1",
                waits=[state["b1w"]] if "b1w" in state else ())
    b1tok = p.tok("db1")
    relu_toks = []
    mmt = None
    for hc in range(32):
        g, sub = divmod(hc, 4)
        slot = g % 2
        if sub == 0:
            wkey = "w1c%d" % slot
            w = [state[wkey]] if wkey in state else []
            p.dma("pool", k.bufa[:, 4096 + slot * 4096:8192 + slot * 4096]
                  .rearrange("p (ci f) -> p ci f", ci=8),
                  dram["w1"][e].rearrange("(ci p) f -> p ci f", p=128)
                  [:, :, g * 512:(g + 1) * 512],
                  "dw1%d" % slot, waits=w)
            state["w1t%d" % slot] = p.tok("dw1%d" % slot)
        wbase = 4096 + slot * 4096
        bank = k.ps0 if hc % 2 == 0 else k.ps1
        for c in range(NCH):
            w = [state["w1t%d" % slot], xet]
            if c == 0 and len(relu_toks) >= 2:
                w.append(relu_toks[-2])
            mmt = p.mm(bank[:, :],
                       k.bufa[:, wbase + c * 512 + sub * 128:
                              wbase + c * 512 + sub * 128 + 128],
                       k.bufa[:, c * 512:(c + 1) * 512],
                       start=(c == 0), stop=(c == NCH - 1), waits=w,
                       inc="spe" if c == NCH - 1 else None)
        state["w1c%d" % slot] = mmt
        w = [mmt, b1tok]
        hk = "hcons%d" % hc
        if hk in state:
            w.append(state.pop(hk))
        rt = p.act("activation", k.xn[:, hc * 512:(hc + 1) * 512],
                   bank[:, :], AF.Relu, bias=k.bcols[:, hc:hc + 1],
                   scale=1.0, waits=w, inc="sact")
        relu_toks.append(rt)
    state["b1w"] = relu_toks[-1]
    b2t = p.dma("pool", k.biasrow[0:1, 0:1024], dram["b2r"][e:e + 1, :],
                "db2", waits=[state["b2w"]] if "b2w" in state else ())
    b2tok = p.tok("db2")
    ydones = []
    for oc in range(2):
        for hg in range(4):
            slot = (oc * 4 + hg) % 2
            wkey = "w2c%d" % slot
            w = [state[wkey]] if wkey in state else []
            p.dma("pool", k.bufb[:, slot * 4096:(slot + 1) * 4096]
                  .rearrange("p (hg f) -> p hg f", hg=8),
                  dram["w2"][e].rearrange("(hh p) f -> p hh f", p=128)
                  [:, hg * 8:(hg + 1) * 8, oc * 512:(oc + 1) * 512],
                  "dw2%d" % slot, waits=w)
            w2tok = p.tok("dw2%d" % slot)
            last = None
            for st in range(4):
                bank = [k.ps2[:, :], k.ps3[:, :], k.pq[:, 0:512],
                        k.pq[:, 512:1024]][st]
                for i in range(8):
                    hc = hg * 8 + i
                    w = [w2tok, relu_toks[hc]]
                    if hg == 0 and i == 0 and "ycons%d" % st in state:
                        w.append(state.pop("ycons%d" % st))
                    last = p.mm(bank,
                                k.xn[:, hc * 512 + st * 128:
                                     hc * 512 + st * 128 + 128],
                                k.bufb[:, slot * 4096 + i * 512:
                                       slot * 4096 + (i + 1) * 512],
                                start=(hg == 0 and i == 0), stop=False,
                                waits=w,
                                inc="spe" if (st == 3 and i == 7) else None)
            state[wkey] = last
        for st in range(4):
            bank = [k.ps2[:, :], k.ps3[:, :], k.pq[:, 0:512],
                    k.pq[:, 512:1024]][st]
            bm = p.mm(bank, k.ones_row_bf[0:1, 0:128],
                      k.biasrow[0:1, oc * 512:(oc + 1) * 512],
                      start=False, stop=True, waits=[b2tok], inc="spe")
            ev = p.dve("tensor_copy",
                       k.bufa[:, 12288 + st * 1024 + oc * 512:
                              12288 + st * 1024 + (oc + 1) * 512],
                       bank, waits=[bm] +
                       ([state.pop("yev%d_%d" % (st, oc))]
                        if "yev%d_%d" % (st, oc) in state else []),
                       inc="sdve")
            state["ycons%d" % st] = ev
            ydones.append(ev)
    state["b2w"] = ydones[-1]
    for hc in range(32):
        state["hcons%d" % hc] = p.tok("spe")
    for st in range(4):
        p.dma("act", eout_dram[e * C0 + st * 128:e * C0 + (st + 1) * 128, :],
              k.bufa[:, 12288 + st * 1024:12288 + (st + 1) * 1024],
              "dye", waits=[ydones[st + 4], ydones[st]])
    ydall = p.tok("dye")
    for st in range(4):
        for oc in range(2):
            state["yev%d_%d" % (st, oc)] = ydall
    state["xw"] = [relu_toks[-1]]
    return p.tok("dye")


def e_l0_experts(p, k, dram, disp_dram, eout_dram, sc_tok, pre):
    guard = pre[-1] if pre else sc_tok
    state = {"xw": [sc_tok] + pre,
             "w1c0": guard, "w1c1": guard,
             "w2c0": guard, "w2c1": guard}
    for hc in range(32):
        state["hcons%d" % hc] = guard
    last = None
    for e in range(E0):
        last = e_ffn_l0_expert(p, k, dram, disp_dram, eout_dram, e, state)
    return last


def e_combine(p, k, eout_dram, nslot, idx_col, idx_toks, wcols, store_tok,
              gsem, extra_scale_tok=()):
    add_toks = []
    gt_prev = None
    sc_prev = None
    for t in range(NTT):
        acc = None
        for j, (wcol, icol) in enumerate(wcols):
            g = p.pool("indirect_dma_start", k.stage[:, 0:1024], None,
                       eout_dram,
                       IndirectOffsetOnAxis(k.ridx[:, icol + t:icol + t + 1], 0),
                       bounds_check=nslot - 1, oob_is_err=False,
                       waits=[store_tok, idx_toks[j][t]] +
                       ([gt_prev] if gt_prev else []) + list(extra_scale_tok),
                       inc=gsem, step=16)
            gtok = p.tok(gsem)
            if j == 0:
                acc = p.dve("tensor_scalar_mul", k.scr[:, 0:1024],
                            k.stage[:, 0:1024],
                            k.route[:, t * 16 + wcol:t * 16 + wcol + 1],
                            waits=[gtok] + ([sc_prev] if sc_prev else []),
                            inc="sdve")
            else:
                acc = p.dve("scalar_tensor_tensor", k.scr[:, 0:1024],
                            k.stage[:, 0:1024],
                            k.route[:, t * 16 + wcol:t * 16 + wcol + 1],
                            k.scr[:, 0:1024], ALU.mult, ALU.add,
                            waits=[gtok, acc], inc="sdve")
            gt_prev = acc
        cb = p.dve("tensor_copy", k.stage[:, 1024:2048], k.scr[:, 0:1024],
                   waits=[acc], inc="sdve")
        sc_prev = cb
        bank_last = [None, None]
        for c in range(NCH):
            bank = k.pt0 if c % 2 == 0 else k.pt1
            w = [cb]
            if bank_last[c % 2] is not None:
                w.append(bank_last[c % 2])
            tt = p.tr(bank[0:128, 0:128],
                      k.stage[:, 1024 + c * 128:1024 + (c + 1) * 128],
                      k.ident_bf[:, :], waits=w, inc="spe")
            at = p.dve("tensor_tensor",
                       k.x[:, c * NT + t * 128:c * NT + (t + 1) * 128],
                       k.x[:, c * NT + t * 128:c * NT + (t + 1) * 128],
                       bank[0:128, 0:128], ALU.add, waits=[tt], inc="sdve")
            bank_last[c % 2] = at
            add_toks.append(at)
    return add_toks


def declare_l1_inputs(nc, with_x=True):
    d = {}
    if with_x:
        d["x1c"] = nc.dram_tensor("x1c", [8, D, 256], F32,
                                  kind="ExternalInput").ap()
    d["rw_t"] = nc.dram_tensor("rw_t", [128, 64], F32,
                               kind="ExternalInput").ap()
    d["l1g_c"] = nc.dram_tensor("l1g_c", [128, 8], F32,
                                kind="ExternalInput").ap()
    d["l1b_c"] = nc.dram_tensor("l1b_c", [128, 8], F32,
                                kind="ExternalInput").ap()
    for nm in ("wq1", "wk1", "wv1", "wo1"):
        d[nm] = nc.dram_tensor(nm, [D, D], F32, kind="ExternalInput").ap()
    for nm in ("bq1r", "bk1r", "bv1r", "bo1r"):
        d[nm] = nc.dram_tensor(nm, [1, D], F32, kind="ExternalInput").ap()
    d["wg1"] = nc.dram_tensor("wg1", [E1, D, DFF1], F32,
                              kind="ExternalInput").ap()
    d["wu1"] = nc.dram_tensor("wu1", [E1, D, DFF1], F32,
                              kind="ExternalInput").ap()
    d["bg1r"] = nc.dram_tensor("bg1r", [E1, DFF1], F32,
                               kind="ExternalInput").ap()
    d["bu1r"] = nc.dram_tensor("bu1r", [E1, DFF1], F32,
                               kind="ExternalInput").ap()
    d["wd1"] = nc.dram_tensor("wd1", [E1, DFF1, D], F32,
                              kind="ExternalInput").ap()
    d["bd1r"] = nc.dram_tensor("bd1r", [E1, D], F32,
                               kind="ExternalInput").ap()
    d["c_ec1"] = nc.dram_tensor("c_ec1", [8, 1], F32,
                                kind="ExternalInput").ap()
    return d


def e_load_x1(p, k, src):
    for c in range(NCH):
        for s in range(8):
            p.dma("sp", k.x[:, c * NT + s * 256:c * NT + s * 256 + 256],
                  src[s, c * 128:(c + 1) * 128, :], "dx")
    return [p.tok("dx")]


def e_ln2_stats(p, k, toks):
    sqbf = k.bufa[:, 0:512]
    m1 = m2 = None
    t_m_prev = t_r_prev = None
    out = []
    for g in range(4):
        for c in range(NCH):
            off = c * NT + g * 512
            w = list(toks) if (g == 0 and c == 0) else []
            if m2 is not None:
                w.append(m2)
            sqt = p.dve("tensor_tensor", sqbf, k.xn[:, off:off + 512],
                        k.xn[:, off:off + 512], ALU.mult, waits=w, inc="sdve")
            w1 = []
            if c == 0 and t_m_prev is not None:
                w1.append(t_m_prev)
            m1 = p.mm(k.ps2[0:1, 0:512], k.ones_col_bf[:, :],
                      k.xn[:, off:off + 512], start=(c == 0),
                      stop=(c == NCH - 1), waits=w1 + list(toks),
                      inc="spe" if c == NCH - 1 else None)
            w2 = [sqt]
            if c == 0 and t_r_prev is not None:
                w2.append(t_r_prev)
            m2 = p.mm(k.ps3[0:1, 0:512], k.ones_col_bf[:, :], sqbf,
                      start=(c == 0), stop=(c == NCH - 1), waits=w2,
                      inc="spe")
        t_m = p.dve("tensor_scalar_mul", k.rowbuf[0:1, 0:512], k.ps2[0:1, :],
                    1.0 / D, waits=[m1] + ([t_m_prev] if t_m_prev else []),
                    inc="sdve")
        t_s = p.dve("tensor_tensor", k.scr[0:1, 1024:1536],
                    k.rowbuf[0:1, 0:512], k.rowbuf[0:1, 0:512], ALU.mult,
                    waits=[t_m], inc="sdve")
        t_v = p.dve("scalar_tensor_tensor", k.scr[0:1, 1536:2048],
                    k.ps3[0:1, :], 1.0 / D, k.scr[0:1, 1024:1536],
                    ALU.mult, ALU.subtract, waits=[m2, t_s], inc="sdve")
        t_d = p.act("activation", k.scr[0:1, 1024:1536], k.scr[0:1, 1536:2048],
                    AF.Sqrt, bias=LN_EPS, scale=1.0, waits=[t_v], inc="sact")
        t_r = p.dve("reciprocal", k.rowbuf[0:1, 512:1024],
                    k.scr[0:1, 1024:1536], waits=[t_d], inc="sdve")
        tc = []
        for jj in range(4):
            t = g * 4 + jj
            w = [t_m] if jj == 0 else [tc[-1]]
            tr1 = p.tr(k.ps2[0:128, 0:1], k.rowbuf[0:1, jj * 128:(jj + 1) * 128],
                       k.ident[0:1, 0:1], waits=w, inc="spe")
            c1 = p.dve("tensor_copy", k.stat[:, 32 + t:33 + t],
                       k.ps2[0:128, 0:1], waits=[tr1], inc="sdve")
            tr2 = p.tr(k.ps3[0:128, 0:1],
                       k.rowbuf[0:1, 512 + jj * 128:512 + (jj + 1) * 128],
                       k.ident[0:1, 0:1], waits=[t_r, c1], inc="spe")
            c2 = p.dve("tensor_copy", k.stat[:, 48 + t:49 + t],
                       k.ps3[0:128, 0:1], waits=[tr2], inc="sdve")
            tc.append(c2)
        t_m_prev = t_r_prev = tc[-1]
        out.append(tc[-1])
    return out


def e_l1_routing(p, k, dram, xn_toks, gtoks=None):
    gtw = p.dma("sp", k.gatef[:, :], dram["rw_t"], "dgt2")
    p.dma("sp", k.cnt[0:8, 4:5], dram["c_ec1"], "dgt2")
    gtok = p.tok("dgt2")
    negsg, nt = e_negsg(p, k, k.gatef, gtok, k.pq)
    fm = k.bufb[:, :].bitcast(F32)
    oh1f, oh2f = fm[0:8, 0:NT], fm[0:8, 2048:2048 + NT]
    keptf, rankf = fm[0:8, 4096:4096 + NT], fm[0:8, 6144:6144 + NT]
    t1s = []
    t2s = []
    lgs = [None] * NTT
    oh12 = []
    for t in range(NTT):
        if gtoks is not None:
            w = list(gtoks[t // 4]) + [nt]
            if t >= 1:
                w.append(lgs[t - 1])
            bank = k.pq[:, 512:1024]
        else:
            w = list(xn_toks) + [nt]
            if t >= 2:
                w.append(lgs[t - 2])
            bank = k.ps2 if t % 2 == 0 else k.ps3
        lg = e_routing_logits(p, k, t, k.gatef, negsg, k.pq, bank,
                              k.scr[:, t * 8:t * 8 + 8], extra_w=w)
        mx = p.dve("reduce_max", k.route[:, t * 16 + 4:t * 16 + 5],
                   k.scr[:, t * 8:t * 8 + 8], waits=[lg], inc="sdve",
                   axis=AXX)
        nmx = p.dve("reduce_max", k.route[:, t * 16 + 5:t * 16 + 6],
                    k.scr[:, t * 8:t * 8 + 8], waits=[lg], inc="sdve",
                    axis=AXX, negate=True)
        ex = p.act("activation", k.scr2[:, t * 8:t * 8 + 8],
                   k.scr[:, t * 8:t * 8 + 8], AF.Exp,
                   bias=k.route[:, t * 16 + 5:t * 16 + 6], scale=1.0,
                   accum_out=k.route[:, t * 16 + 6:t * 16 + 7],
                   waits=[nmx], inc="sact")
        rs = p.dve("reciprocal", k.route[:, t * 16 + 7:t * 16 + 8],
                   k.route[:, t * 16 + 6:t * 16 + 7], waits=[ex], inc="sdve")
        pr = p.dve("tensor_scalar_mul", k.scr2[:, 128 + t * 8:136 + t * 8],
                   k.scr2[:, t * 8:t * 8 + 8],
                   k.route[:, t * 16 + 7:t * 16 + 8], waits=[ex, rs],
                   inc="sdve")
        oh1 = p.dve("tensor_single_scalar", k.scr2[:, t * 8:t * 8 + 8],
                    k.scr[:, t * 8:t * 8 + 8],
                    k.route[:, t * 16 + 4:t * 16 + 5], ALU.is_ge,
                    waits=[mx, pr], inc="sdve")
        p2t = p.dve("tensor_tensor", k.scr[:, 128 + t * 8:136 + t * 8],
                    k.scr2[:, 128 + t * 8:136 + t * 8],
                    k.scr2[:, t * 8:t * 8 + 8], ALU.subtract,
                    waits=[pr, oh1], inc="sdve")
        p2c = p.dve("reduce_max", k.route[:, t * 16 + 8:t * 16 + 9],
                    k.scr[:, 128 + t * 8:136 + t * 8], waits=[p2t],
                    inc="sdve", axis=AXX)
        oh2 = p.dve("tensor_single_scalar", k.scr[:, 128 + t * 8:136 + t * 8],
                    k.scr[:, 128 + t * 8:136 + t * 8],
                    k.route[:, t * 16 + 8:t * 16 + 9], ALU.is_ge,
                    waits=[p2c], inc="sdve")
        s12 = p.dve("tensor_tensor", k.route[:, t * 16 + 9:t * 16 + 10],
                    k.route[:, t * 16 + 7:t * 16 + 8],
                    k.route[:, t * 16 + 8:t * 16 + 9], ALU.add,
                    waits=[rs, p2c], inc="sdve")
        s12b = p.dve("tensor_scalar_add", k.route[:, t * 16 + 9:t * 16 + 10],
                     k.route[:, t * 16 + 9:t * 16 + 10], 1e-9, waits=[s12],
                     inc="sdve")
        rr = p.dve("reciprocal", k.route[:, t * 16 + 10:t * 16 + 11],
                   k.route[:, t * 16 + 9:t * 16 + 10], waits=[s12b],
                   inc="sdve")
        p.dve("tensor_tensor", k.route[:, t * 16 + 1:t * 16 + 2],
              k.route[:, t * 16 + 7:t * 16 + 8],
              k.route[:, t * 16 + 10:t * 16 + 11], ALU.mult,
              waits=[rr], inc="sdve")
        p.dve("tensor_tensor", k.route[:, t * 16 + 2:t * 16 + 3],
              k.route[:, t * 16 + 8:t * 16 + 9],
              k.route[:, t * 16 + 10:t * 16 + 11], ALU.mult,
              waits=[rr], inc="sdve")
        lgs[t] = p.tok("sdve")
        oh12.append((oh1, oh2))
    for t in range(NTT):
        oh1, oh2 = oh12[t]
        tb1 = k.ps0 if t % 2 == 0 else k.ps1
        w = [oh1] + ([t1s[t - 2]] if t >= 2 else xn_toks)
        tt1 = p.tr(tb1[0:8, 0:128], k.scr2[:, t * 8:t * 8 + 8], k.ident[:, :],
                   waits=w, inc="spe")
        cp1 = p.dve("tensor_copy", oh1f[:, t * 128:(t + 1) * 128],
                    tb1[0:8, 0:128], waits=[tt1], inc="sdve")
        t1s.append(cp1)
        w = [oh2, cp1]
        tt2 = p.tr(tb1[0:8, 0:128], k.scr[:, 128 + t * 8:136 + t * 8],
                   k.ident[:, :], waits=w, inc="spe")
        cp2 = p.dve("tensor_copy", oh2f[:, t * 128:(t + 1) * 128],
                    tb1[0:8, 0:128], waits=[tt2], inc="sdve")
        t2s.append(cp2)
    sel = p.dve("tensor_tensor", keptf, oh1f, oh2f, ALU.add,
                waits=t1s + t2s, inc="sdve")
    sc = p.dve("tensor_tensor_scan", rankf, keptf,
               k.zero8[0:8, 0:1].broadcast_to([8, NT]), 0.0, ALU.add, ALU.add,
               waits=[sel], inc="sdve")
    kp = p.dve("tensor_single_scalar", keptf, rankf, C1 - 0.5, ALU.is_le,
               waits=[sc], inc="sdve")
    slc = p.dve("tensor_scalar_add", rankf, rankf, k.cnt[0:8, 4:5],
                waits=[sc, gtok, kp], inc="sdve")
    a1 = p.dve("tensor_tensor", oh1f, oh1f, keptf, ALU.mult, waits=[kp, slc],
               inc="sdve")
    a2 = p.dve("tensor_tensor", oh2f, oh2f, keptf, ALU.mult, waits=[kp],
               inc="sdve")
    c1 = p.dve("tensor_tensor", keptf, oh1f, rankf, ALU.mult,
               waits=[a1, a2, slc], inc="sdve")
    i1 = e_extract_idx(p, k, oh1f, keptf, [a1, c1], 16, stash_col=12)
    c2 = p.dve("tensor_tensor", keptf, oh2f, rankf, ALU.mult,
               waits=[c1] + i1, inc="sdve")
    i2 = e_extract_idx(p, k, oh2f, keptf, [a2, c2], 32, stash_col=13)
    return i1, i2


def e_l1_attn(p, k, dram, xn_toks, st2_toks, pre):
    xadd_toks = []
    xn_used = []
    aprev = None
    for c in range(NCH):
        for g in range(4):
            off = c * NT + g * 512
            aprev = p.pool("tensor_tensor", k.x[:, off:off + 512],
                           k.x[:, off:off + 512], k.xn[:, off:off + 512],
                           ALU.add, waits=list(pre) + xn_toks + st2_toks,
                           inc="spool")
            xadd_toks.append(aprev)
    glb = p.dma("sp", k.bcols[:, 32:40], dram["l1g_c"], "dlg", waits=pre)
    p.dma("sp", k.bcols[:, 40:48], dram["l1b_c"], "dlg")
    p.dma("pool", k.biasrow[0:1, 0:1024], dram["bq1r"], "dlb", waits=pre)
    p.dma("pool", k.biasrow[0:1, 1024:2048], dram["bk1r"], "dlb")
    p.dma("pool", k.biasrow2[0:1, 0:1024], dram["bv1r"], "dlb")
    p.dma("pool", k.biasrow2[0:1, 1024:2048], dram["bo1r"], "dlb")
    lgtok = p.tok("dlg")
    lbtok = p.tok("dlb")
    w_state = {}
    kv_done = {}
    ln_done = {}
    out_done = {}
    for half in range(2):
        ln_toks = []
        lprev = None
        for s in range(8):
            toff = s * 256 + half * 128
            ti = (s * 256 + half * 128) // 128
            w = [st2_toks[-1], lgtok] + list(pre)
            if lprev is not None:
                w.append(lprev)
            trm = p.tr(k.pq[0:1, 0:128], k.stat[:, 32 + ti:33 + ti],
                       k.ident[:, :], waits=w + (out_done.get(half - 1, [])),
                       inc="spe")
            trr = p.tr(k.pq[0:1, 512:640], k.stat[:, 48 + ti:49 + ti],
                       k.ident[:, :], waits=[trm], inc="spe")
            cm = p.dve("tensor_copy", k.rowbuf[0:1, 0:128], k.pq[0:1, 0:128],
                       waits=[trm, lprev], inc="sdve")
            cr = p.dve("tensor_copy", k.rowbuf[0:1, 128:256],
                       k.pq[0:1, 512:640], waits=[trr], inc="sdve")
            bm = p.mm(k.ps0[:, 0:128], k.ones_row[0:1, 0:128],
                      k.rowbuf[0:1, 0:128], start=True, stop=True,
                      waits=[cm], inc="spe")
            br = p.mm(k.ps1[:, 0:128], k.ones_row[0:1, 0:128],
                      k.rowbuf[0:1, 128:256], start=True, stop=True,
                      waits=[cr], inc="spe")
            xnap = k.xn[:, :].rearrange("p (c n) -> p c n", c=NCH)[
                :, :, toff:toff + 128]
            l1 = p.dve("tensor_tensor", k.scr[:, 0:1024].rearrange(
                "p (c n) -> p c n", c=8), xnap,
                k.ps0[:, 0:128].unsqueeze(1).broadcast_to([128, 8, 128]),
                ALU.subtract, waits=[bm] + xn_toks, inc="sdve")
            l2 = p.dve("tensor_tensor", k.scr[:, 0:1024].rearrange(
                "p (c n) -> p c n", c=8),
                k.scr[:, 0:1024].rearrange("p (c n) -> p c n", c=8),
                k.ps1[:, 0:128].unsqueeze(1).broadcast_to([128, 8, 128]),
                ALU.mult, waits=[br, l1], inc="sdve")
            l3 = p.dve("tensor_tensor", k.scr[:, 0:1024].rearrange(
                "p (c n) -> p c n", c=8),
                k.scr[:, 0:1024].rearrange("p (c n) -> p c n", c=8),
                k.bcols[:, 32:40].unsqueeze(2).broadcast_to([128, 8, 128]),
                ALU.mult, waits=[l2], inc="sdve")
            l4 = p.dve("tensor_tensor", k.bufb[:, s * 1024:(s + 1) * 1024]
                       .rearrange("p (c n) -> p c n", c=8),
                       k.scr[:, 0:1024].rearrange("p (c n) -> p c n", c=8),
                       k.bcols[:, 40:48].unsqueeze(2).broadcast_to(
                           [128, 8, 128]),
                       ALU.add, waits=[l3], inc="sdve")
            lprev = l4
            ln_toks.append(l4)
        ln_done[half] = ln_toks
        kv_toks = []
        for wi, wname, brow in ((0, "wk1", k.biasrow[0:1, 1024:2048]),
                                (1, "wv1", k.biasrow2[0:1, 0:1024])):
            for oc in range(2):
                slot = (wi * 2 + oc) % 2
                wkey = "wkv%d" % slot
                w = [w_state[wkey]] if wkey in w_state else list(pre)
                p.dma("pool", k.bufb[:, 8192 + slot * 4096:
                                     8192 + (slot + 1) * 4096]
                      .rearrange("p (ci f) -> p ci f", ci=8),
                      dram[wname].rearrange("(ci p) f -> p ci f", p=128)
                      [:, :, oc * 512:(oc + 1) * 512], "dwkv%d" % slot,
                      waits=w)
                wt = p.tok("dwkv%d" % slot)
                mmt = None
                ev = None
                for s in range(8):
                    bank = k.ps2 if s % 2 == 0 else k.ps3
                    for ci in range(NCH):
                        w = [wt, ln_toks[s]]
                        if ci == 0 and len(kv_toks) >= 2:
                            w.append(kv_toks[-2])
                        mmt = p.mm(bank[:, :],
                                   k.bufb[:, s * 1024 + ci * 128:
                                          s * 1024 + (ci + 1) * 128],
                                   k.bufb[:, 8192 + slot * 4096 + ci * 512:
                                          8192 + slot * 4096 + (ci + 1) * 512],
                                   start=(ci == 0), stop=False, waits=w)
                    mmt = p.mm(bank[:, :], k.ones_row_bf[0:1, 0:128],
                               brow[0:1, oc * 512:(oc + 1) * 512], start=False,
                               stop=True, waits=[lbtok], inc="spe")
                    ev = p.act("copy",
                               k.bufa[:, wi * 8192 + s * 1024 + oc * 512:
                                      wi * 8192 + s * 1024 + oc * 512 + 512],
                               bank[:, :], waits=[mmt] + list(pre) + st2_toks,
                               inc="sact")
                    kv_toks.append(ev)
                w_state[wkey] = ev
        wq_w = [w_state.get("wkv0"), w_state.get("wkv1")]
        for i in range(2):
            p.dma("pool", k.bufb[:, 8192 + i * 4096:8192 + (i + 1) * 4096]
                  .rearrange("p (ci f) -> p ci f", ci=8),
                  dram["wq1"].rearrange("(ci p) f -> p ci f", p=128)
                  [:, :, i * 512:(i + 1) * 512], "dwq", waits=[wq_w[i]])
        wqtok = p.tok("dwq")
        qev_prev = None
        s8_prev = None
        outs = []
        blk_guard = [None, None]
        pend = None
        for s in range(8):
            qmt = None
            for oc in range(2):
                bank = k.pq[:, oc * 512:(oc + 1) * 512]
                for ci in range(NCH):
                    w = [wqtok, ln_toks[s]]
                    if ci == 0 and qev_prev is not None:
                        w.append(qev_prev)
                    qmt = p.mm(bank,
                               k.bufb[:, s * 1024 + ci * 128:
                                      s * 1024 + (ci + 1) * 128],
                               k.bufb[:, 8192 + oc * 4096 + ci * 512:
                                      8192 + oc * 4096 + (ci + 1) * 512],
                               start=(ci == 0), stop=False, waits=w)
                qmt = p.mm(bank, k.ones_row_bf[0:1, 0:128],
                           k.biasrow[0:1, oc * 512:(oc + 1) * 512],
                           start=False, stop=True, waits=[lbtok], inc="spe")
            qev = p.act("copy", k.stage[:, 0:1024], k.pq[:, :],
                        waits=[qmt] + ([s8_prev] if s8_prev else []),
                        inc="sact")
            qev_prev = qev
            s8c = []
            for hg in range(2):
                prod = p.dve(
                    "tensor_tensor", k.scr[:, 0:2048].bitcast(BF16),
                    k.stage[:, 0:1024].rearrange("p (h d) -> p h d", h=16)
                    [:, hg * 8:(hg + 1) * 8, :].unsqueeze(2)
                    .broadcast_to([128, 8, 8, 64]),
                    k.bufa[:, 0:8192].rearrange("p (t h d) -> p h t d", t=8,
                                                h=16)
                    [:, hg * 8:(hg + 1) * 8, :, :],
                    ALU.mult, waits=[qev] + kv_toks[-1:] + s8c, inc="sdve")
                red = p.dve("tensor_reduce",
                            k.scr2[:, s % 2 * 512 + hg * 64:
                                   s % 2 * 512 + hg * 64 + 64]
                            .rearrange("p (h t) -> p h t", h=8),
                            k.scr[:, 0:2048].bitcast(BF16)
                            .rearrange("p (h t d) -> p h t d", h=8, t=8),
                            AXX, ALU.add,
                            waits=[prod] +
                            ([blk_guard[s % 2]] if blk_guard[s % 2] else []),
                            inc="sdve")
                s8c.append(red)
            s8_prev = s8c[-1]
            sb = k.scr2[:, s % 2 * 512:s % 2 * 512 + 128]
            exd = p.act("activation", sb, sb, AF.Exp, scale=0.125,
                        waits=s8c, inc="sact")
            if pend is not None:
                pend()
                pend = None

            def mk_pend(s=s, sb=sb, exd=exd, qev=qev):
                def fin():
                    smr = p.dve("tensor_reduce",
                                k.scr2[:, s % 2 * 512 + 144:
                                       s % 2 * 512 + 160],
                                sb.rearrange("p (h t) -> p h t", h=16), AXX,
                                ALU.add, waits=[exd], inc="sdve")
                    rcr = p.dve("reciprocal",
                                k.scr2[:, s % 2 * 512 + 160:
                                       s % 2 * 512 + 176],
                                k.scr2[:, s % 2 * 512 + 144:
                                       s % 2 * 512 + 160],
                                waits=[smr], inc="sdve")
                    aw = p.dve("tensor_tensor", sb, sb,
                               k.scr2[:, s % 2 * 512 + 160:
                                      s % 2 * 512 + 176]
                               .rearrange("p (h o) -> p h o", o=1)
                               .broadcast_to([128, 16, 8]), ALU.mult,
                               waits=[rcr], inc="sdve")
                    avs = []
                    for hg in range(2):
                        prod = p.dve(
                            "tensor_tensor",
                            k.rope[:, 0:4096]
                            .rearrange("p (h d t) -> p h d t", h=8, d=64),
                            k.bufa[:, 8192:16384]
                            .rearrange("p (t h d) -> p h d t", t=8, h=16)
                            [:, hg * 8:(hg + 1) * 8, :, :],
                            sb.rearrange("p (h t) -> p h t", h=16)
                            [:, hg * 8:(hg + 1) * 8, :].unsqueeze(2)
                            .broadcast_to([128, 8, 64, 8]),
                            ALU.mult, waits=[aw] + avs, inc="sdve")
                        red = p.dve("tensor_reduce",
                                    k.bufb[:, s * 1024 + hg * 512:
                                           s * 1024 + (hg + 1) * 512]
                                    .rearrange("p (h d) -> p h d", h=8),
                                    k.rope[:, 0:4096]
                                    .rearrange("p (h d t) -> p h d t", h=8,
                                               d=64),
                                    AXX, ALU.add,
                                    waits=[prod, ln_toks[s], qev],
                                    inc="sdve", lowp=True)
                        avs.append(red)
                    blk_guard[s % 2] = avs[-1]
                    outs.append(avs)
                return fin
            pend = mk_pend()
        pend()
        pend = None
        av_prev = outs[-1][-1]
        for i in range(2):
            p.dma("pool", k.bufb[:, 8192 + i * 4096:8192 + (i + 1) * 4096]
                  .rearrange("p (ci f) -> p ci f", ci=8),
                  dram["wo1"].rearrange("(ci p) f -> p ci f", p=128)
                  [:, :, i * 512:(i + 1) * 512], "dwo1",
                  waits=[qev_prev, outs[-1][-1]])
        wotok = p.tok("dwo1")
        trp = [None, None]
        addp = [None, None]
        od = []
        for s in range(8):
            cps = []
            for c in range(NCH):
                bank = k.pt0 if c % 2 == 0 else k.pt1
                w = list(outs[s])
                if trp[c % 2] is not None:
                    w.append(trp[c % 2])
                tt = p.tr(bank[0:128, 0:128],
                          k.bufb[:, s * 1024 + c * 128:s * 1024 + (c + 1) * 128],
                          k.ident_bf[:, :], waits=w, inc="spe")
                cp = p.dve("tensor_copy",
                           k.stage[:, 1024 + c * 128:1024 + (c + 1) * 128],
                           bank[0:128, 0:128],
                           waits=[tt] + ([cps[-1]] if cps else
                                         ([addp[0]] if addp[0] else [])),
                           inc="sdve")
                trp[c % 2] = cp
                cps.append(cp)
            for oc in range(NCH):
                bank = k.ps2 if oc % 2 == 0 else k.ps3
                mmt = None
                for ci in range(NCH):
                    w = [wotok, cps[-1]]
                    if ci == 0 and addp[oc % 2] is not None:
                        w.append(addp[oc % 2])
                    mmt = p.mm(bank[:, 0:128],
                               k.bufb[:, 8192 + (oc // 4) * 4096 +
                                      ci * 512 + (oc % 4) * 128:
                                      8192 + (oc // 4) * 4096 +
                                      ci * 512 + (oc % 4) * 128 + 128],
                               k.stage[:, 1024 + ci * 128:1024 + (ci + 1) * 128],
                               start=(ci == 0), stop=False, waits=w)
                mmt = p.mm(bank[:, 0:128],
                           k.biasrow2[0:1, 1024 + oc * 128:1024 + (oc + 1) * 128],
                           k.ones_row_bf[0:1, 0:128], start=False, stop=True,
                           waits=[lbtok], inc="spe")
                toff = oc * NT + s * 256 + half * 128
                at = p.dve("tensor_tensor", k.x[:, toff:toff + 128],
                           k.x[:, toff:toff + 128], bank[:, 0:128], ALU.add,
                           waits=[mmt] + xadd_toks[-1:], inc="sdve")
                addp[oc % 2] = at
                od.append(at)
                xadd_toks.append(at)
        out_done[half] = od[-2:]
        for i in range(2):
            w_state["wkv%d" % i] = od[-1]
    return xadd_toks


def e_ffn_l1_expert(p, k, dram, disp_dram, eout_dram, e, state):
    xw = list(state.get("xw", []))
    for c in range(NCH):
        p.dma_tr("sp", k.bufa[:, c * 1024:(c + 1) * 1024],
                 disp_dram[e * C1:(e + 1) * C1, c * 128:(c + 1) * 128],
                 "dxe1", waits=xw if c == 0 else ())
    xet = p.tok("dxe1")
    p.dma("pool", k.biasrow[0:1, 0:2048], dram["bg1r"][e:e + 1, :], "db1g",
          waits=[state["bgw"]] if "bgw" in state else ())
    p.dma("pool", k.biasrow2[0:1, 0:2048], dram["bu1r"][e:e + 1, :], "db1g")
    btok = p.tok("db1g")
    h_toks = []
    for gf in range(16):
        slot = gf % 2
        wg = [state["wgc%d" % slot]] if "wgc%d" % slot in state else []
        p.dma("pool", k.bufa[:, 8192 + slot * 2048:8192 + slot * 2048 + 1024]
              .rearrange("p (ci f) -> p ci f", ci=8),
              dram["wg1"][e].rearrange("(ci p) f -> p ci f", p=128)
              [:, :, gf * 128:(gf + 1) * 128], "dwg%d" % slot, waits=wg)
        p.dma("pool", k.bufa[:, 8192 + slot * 2048 + 1024:
                             8192 + slot * 2048 + 2048]
              .rearrange("p (ci f) -> p ci f", ci=8),
              dram["wu1"][e].rearrange("(ci p) f -> p ci f", p=128)
              [:, :, gf * 128:(gf + 1) * 128], "dwg%d" % slot)
        wt = p.tok("dwg%d" % slot)
        last = None
        for sh in range(2):
            bg = k.ps2 if sh == 0 else k.ps3
            bu = k.pq[:, 0:512] if sh == 0 else k.pq[:, 512:1024]
            m1 = m2 = None
            for ci in range(NCH):
                w = [wt, xet]
                if ci == 0 and len(h_toks) >= 2:
                    w.append(h_toks[-2])
                m1 = p.mm(bg[:, :],
                          k.bufa[:, 8192 + slot * 2048 + ci * 128:
                                 8192 + slot * 2048 + (ci + 1) * 128],
                          k.bufa[:, ci * 1024 + sh * 512:
                                 ci * 1024 + (sh + 1) * 512],
                          start=(ci == 0), stop=False, waits=w)
                m2 = p.mm(bu,
                          k.bufa[:, 8192 + slot * 2048 + 1024 + ci * 128:
                                 8192 + slot * 2048 + 1024 + (ci + 1) * 128],
                          k.bufa[:, ci * 1024 + sh * 512:
                                 ci * 1024 + (sh + 1) * 512],
                          start=(ci == 0), stop=False, waits=w)
            m1 = p.mm(bg[:, :], k.biasrow[0:1, gf * 128:(gf + 1) * 128],
                      k.ones_row_bf[0:1, 0:512], start=False, stop=True,
                      waits=[btok], inc="spe")
            m2 = p.mm(bu, k.biasrow2[0:1, gf * 128:(gf + 1) * 128],
                      k.ones_row_bf[0:1, 0:512], start=False, stop=True,
                      waits=[btok], inc="spe")
            hk = "hc%d_%d" % (gf, sh)
            sw = [m1] + ([state.pop("sil%d" % sh)]
                         if "sil%d" % sh in state else [])
            sg = p.act("activation", k.stage[:, sh * 512:(sh + 1) * 512],
                       bg[:, :], AF.Silu, waits=sw, inc="sact")
            state["sil%d" % sh] = sg
            hw = [sg, m2]
            if hk in state:
                hw.append(state.pop(hk))
            ht = p.dve("tensor_tensor",
                       k.xn[:, gf * 1024 + sh * 512:gf * 1024 + (sh + 1) * 512],
                       k.stage[:, sh * 512:(sh + 1) * 512], bu, ALU.mult,
                       waits=hw, inc="sdve")
            h_toks.append(ht)
        state["wgc%d" % slot] = h_toks[-1]
    state["bgw"] = h_toks[-1]
    p.dma("pool", k.biasrow[0:1, 0:1024], dram["bd1r"][e:e + 1, :], "db1d",
          waits=[h_toks[-1]])
    bdt = p.tok("db1d")
    ydones = {}
    wlast = None
    for oc in range(2):
        w = [wlast] if wlast is not None else \
            ([state["wdc"]] if "wdc" in state else [])
        for gg in range(2):
            p.dma("pool", k.bufb[:, gg * 4096:(gg + 1) * 4096]
                  .rearrange("p (gr f) -> p gr f", gr=8),
                  dram["wd1"][e].rearrange("(gh p) f -> p gh f", p=128)
                  [:, gg * 8:(gg + 1) * 8, oc * 512:(oc + 1) * 512],
                  "dwd%d" % gg, waits=w)
        wdt = [p.tok("dwd0"), p.tok("dwd1")]
        for wave in range(2):
            for sq in range(4):
                st = wave * 4 + sq
                bank = [k.ps2[:, :], k.ps3[:, :], k.pq[:, 0:512],
                        k.pq[:, 512:1024]][sq]
                for i in range(16):
                    w2 = wdt + [h_toks[2 * i], h_toks[2 * i + 1]]
                    if i == 0 and "yc%d" % sq in state:
                        w2.append(state.pop("yc%d" % sq))
                    wlast = p.mm(bank,
                                 k.xn[:, i * 1024 + st * 128:
                                      i * 1024 + st * 128 + 128],
                                 k.bufb[:, (i // 8) * 4096 + (i % 8) * 512:
                                        (i // 8) * 4096 + (i % 8) * 512 + 512],
                                 start=(i == 0), stop=False, waits=w2)
                bm = p.mm(bank, k.ones_row_bf[0:1, 0:128],
                          k.biasrow[0:1, oc * 512:(oc + 1) * 512],
                          start=False, stop=True, waits=[bdt], inc="spe")
                wlast = bm
                ev = p.dve("tensor_copy",
                           k.bufa[:, st * 1024 + oc * 512:
                                  st * 1024 + oc * 512 + 512],
                           bank, waits=[bm] +
                           ([state.pop("ye%d_%d" % (st, oc))]
                            if "ye%d_%d" % (st, oc) in state else []),
                           inc="sdve")
                state["yc%d" % sq] = ev
                ydones[(st, oc)] = ev
    state["wdc"] = wlast
    for st in range(8):
        p.dma("act", eout_dram[e * C1 + st * 128:e * C1 + (st + 1) * 128, :],
              k.bufa[:, st * 1024:(st + 1) * 1024], "dye1",
              waits=[ydones[(st, 0)], ydones[(st, 1)]])
    ydall = p.tok("dye1")
    for st in range(8):
        for oc in range(2):
            state["ye%d_%d" % (st, oc)] = ydall
    state["xw"] = [h_toks[-1], ydall]
    state["bgw"] = ydall
    return ydall


def e_l1_experts(p, k, dram, disp_dram, eout_dram, sc_tok, pre):
    guard = pre[-1] if pre else sc_tok
    state = {"xw": [sc_tok] + list(pre),
             "wgc0": guard, "wgc1": guard, "wdc": guard}
    for gf in range(16):
        for sh in range(2):
            state["hc%d_%d" % (gf, sh)] = guard
    last = None
    for e in range(E1):
        last = e_ffn_l1_expert(p, k, dram, disp_dram, eout_dram, e, state)
    return last


def build_progC1():
    nc = bass.Bass("TRN2", target_bir_lowering=False, debug=False)
    dram = declare_const_inputs(nc)
    dram.update(declare_l1_inputs(nc))
    xpartT = nc.dram_tensor("xpartT", [D, NT], F32, kind="ExternalOutput").ap()
    ridx_o = nc.dram_tensor("ridx_o", [128, 64], I32, kind="ExternalOutput").ap()
    route_o = nc.dram_tensor("route_o", [128, 256], F32,
                             kind="ExternalOutput").ap()
    disp = nc.dram_tensor("disp1", [NSLOT1, D], BF16, kind="ExternalOutput").ap()
    p = Prog(nc)
    k = alloc_common(nc)
    ct0 = e_load_consts(p, k, dram)
    e_load_x1(p, k, dram["x1c"])
    xn_toks = e_ln_plain(p, k, ct0 + [p.tok("dx")])
    i1, i2 = e_l1_routing(p, k, dram, xn_toks)
    sc_tok = e_dispatch(p, k, disp, NSLOT1, [16, 32], [i1, i2],
                        [xn_toks[-1], i2[-1]], "dzr", "dsc")
    st2 = e_ln2_stats(p, k, [sc_tok] + xn_toks)
    xadd = e_l1_attn(p, k, dram, xn_toks, st2, [sc_tok])
    for c in range(NCH):
        p.dma("act", xpartT[c * 128:(c + 1) * 128, :],
              k.x[:, c * NT:(c + 1) * NT], "dout", waits=xadd)
    p.dma("act", ridx_o, k.ridx[:, :], "dout", waits=i1 + i2)
    p.dma("act", route_o, k.route[:, :], "dout", waits=xadd)
    p.wait("act", [p.tok("dout"), sc_tok])
    return p.finalize()


def build_progC2():
    nc = bass.Bass("TRN2", target_bir_lowering=False, debug=False)
    dram = declare_const_inputs(nc)
    dram.update(declare_l1_inputs(nc))
    dram["xpartT"] = nc.dram_tensor("xpartT", [D, NT], F32,
                                    kind="ExternalInput").ap()
    dram["ridx_i"] = nc.dram_tensor("ridx_i", [128, 64], I32,
                                    kind="ExternalInput").ap()
    dram["route_i"] = nc.dram_tensor("route_i", [128, 256], F32,
                                     kind="ExternalInput").ap()
    disp = nc.dram_tensor("disp1", [NSLOT1, D], BF16,
                          kind="ExternalInput").ap()
    outT = nc.dram_tensor("outT", [8, D, 256], F32, kind="ExternalOutput").ap()
    eout = nc.dram_tensor("eout1", [NSLOT1, D], BF16, kind="Internal").ap()
    p = Prog(nc)
    k = alloc_common(nc)
    ct0 = e_load_consts(p, k, dram)
    e_load_x(p, k, dram["xpartT"])
    p.dma("sp", k.ridx[:, :], dram["ridx_i"], "dx", waits=ct0)
    p.dma("sp", k.route[:, :], dram["route_i"], "dx", waits=ct0)
    ld = p.tok("dx")
    ldl = [ld] * NTT
    ye_tok = e_l1_experts(p, k, dram, disp, eout, ld, ct0 + [ld])
    add_toks = e_combine(p, k, eout, NSLOT1, 0, [ldl, ldl], [(1, 16), (2, 32)],
                         ye_tok, "dgc", extra_scale_tok=[ld])
    for c in range(NCH):
        for s in range(8):
            p.dma("act", outT[s, c * 128:(c + 1) * 128, :],
                  k.x[:, c * NT + s * 256:c * NT + s * 256 + 256], "dout",
                  waits=add_toks)
    p.wait("act", [p.tok("dout")])
    return p.finalize()



def e_xn_tm(p, k, pre):
    toks = []
    bank_last = [None, None]
    for t in range(NTT):
        cps = []
        for c in range(NCH):
            bank = k.pt0 if c % 2 == 0 else k.pt1
            w = list(pre)
            if bank_last[c % 2] is not None:
                w.append(bank_last[c % 2])
            tt = p.tr(bank[0:128, 0:128],
                      k.xn[:, c * NT + t * 128:c * NT + (t + 1) * 128],
                      k.ident_bf[:, :], waits=w, inc="spe")
            cp = p.dve("tensor_copy",
                       k.bufb[:, t * 1024 + c * 128:t * 1024 + (c + 1) * 128],
                       bank[0:128, 0:128], waits=[tt], inc="sdve")
            bank_last[c % 2] = cp
            cps.append(cp)
        toks.append(cps[-1])
    return toks


def e_moe_mm_l0(p, k, dram, xntm_toks, pre):
    x_toks = []
    state = {}
    guard = pre[-1]
    for e in range(E0):
        xev = []
        for wave in range(2):
            mmt = None
            slot_last = state.get("slot_last", [None] * 4)
            for t in range(NTT):
                slot = t % 4
                w = [guard]
                if slot_last[slot] is not None:
                    w.append(slot_last[slot])
                oh = p.dve("tensor_scalar",
                           k.stage[:, slot * C0:(slot + 1) * C0],
                           k.iota512[:, 0:C0],
                           k.route[:, t * 16 + 11:t * 16 + 12],
                           -float(e * C0), ALU.subtract, ALU.is_equal,
                           waits=w, inc="sdve")
                for cw in range(4):
                    c = wave * 4 + cw
                    bank = [k.ps0, k.ps1, k.ps2, k.ps3][cw]
                    w = [oh, xntm_toks[t]]
                    if t == 0:
                        w.append(state.get("xdw%d" % cw))
                    mmt = p.mm(bank[:, 0:C0],
                               k.bufb[:, t * 1024 + c * 128:
                                      t * 1024 + (c + 1) * 128],
                               k.stage[:, slot * C0:(slot + 1) * C0],
                               start=(t == 0), stop=(t == NTT - 1), waits=w,
                               inc="spe" if cw == 3 else None)
                    if cw == 3:
                        slot_last[slot] = mmt
            state["slot_last"] = slot_last
            for cw in range(4):
                c = wave * 4 + cw
                bank = [k.ps0, k.ps1, k.ps2, k.ps3][cw]
                ev = p.dve("tensor_copy", k.bufa[:, c * C0:(c + 1) * C0],
                           bank[:, 0:C0],
                           waits=[mmt, state.pop("xfw%d" % c, None)],
                           inc="sdve")
                state["xdw%d" % cw] = ev
                xev.append(ev)
        xet = xev[-1]
        b1t = p.dma("sp", k.bcols[:, 0:32], dram["b1c"][e], "db1",
                    waits=[state["b1w"]] if "b1w" in state else [guard])
        b1tok = p.tok("db1")
        relu_toks = []
        for hc in range(32):
            g, sub = divmod(hc, 2)
            slot = g % 4
            if sub == 0:
                wkey = "w1c%d" % slot
                w = [state[wkey]] if wkey in state else [guard]
                p.dma("pool", k.bufa[:, 6144 + slot * 2048:8192 + slot * 2048]
                      .rearrange("p (ci f) -> p ci f", ci=8),
                      dram["w1"][e].rearrange("(ci p) f -> p ci f", p=128)
                      [:, :, g * 256:(g + 1) * 256],
                      "dw1%d" % slot, waits=w)
                state["w1t%d" % slot] = p.tok("dw1%d" % slot)
            wbase = 6144 + slot * 2048
            bank = k.ps0 if hc % 2 == 0 else k.ps1
            mmt = None
            for c in range(NCH):
                w = [state["w1t%d" % slot], xet]
                if c == 0 and len(relu_toks) >= 2:
                    w.append(relu_toks[-2])
                mmt = p.mm(bank[:, 0:C0],
                           k.bufa[:, wbase + c * 256 + sub * 128:
                                  wbase + c * 256 + sub * 128 + 128],
                           k.bufa[:, c * C0:(c + 1) * C0],
                           start=(c == 0), stop=(c == NCH - 1), waits=w,
                           inc="spe" if c == NCH - 1 else None)
            state["w1c%d" % slot] = mmt
            w = [mmt, b1tok]
            hk = "hcons%d" % hc
            if hk in state:
                w.append(state.pop(hk))
            else:
                w.append(guard)
            rt = p.act("activation", k.xn[:, hc * C0:(hc + 1) * C0],
                       bank[:, 0:C0], AF.Relu, bias=k.bcols[:, hc:hc + 1],
                       scale=1.0, waits=w, inc="sact")
            relu_toks.append(rt)
        state["b1w"] = relu_toks[-1]
        for c in range(NCH):
            state["xfw%d" % c] = relu_toks[-1]
        b2t = p.dma("pool", k.biasrow[0:1, 0:1024], dram["b2r"][e:e + 1, :],
                    "db2", waits=[state["b2w"]] if "b2w" in state else [guard])
        b2tok = p.tok("db2")
        ydones = {}
        NST0 = C0 // 128

        def w2ap(slot, c0, c1):
            if slot < 2:
                return k.rope[:, slot * 2048 + c0:slot * 2048 + c1]
            return k.xn[:, 12288 + (slot - 2) * 2048 + c0:
                        12288 + (slot - 2) * 2048 + c1]
        for oc in range(2):
            last4 = [None] * NST0
            for hg in range(8):
                slot = (oc * 8 + hg) % 4
                wkey = "w2c%d" % slot
                w = [state[wkey]] if wkey in state else [guard]
                p.dma("pool", w2ap(slot, 0, 2048)
                      .rearrange("p (hh f) -> p hh f", hh=4),
                      dram["w2"][e].rearrange("(hh p) f -> p hh f", p=128)
                      [:, hg * 4:(hg + 1) * 4, oc * 512:(oc + 1) * 512],
                      "dw2%d" % slot, waits=w)
                w2tok = p.tok("dw2%d" % slot)
                for st in range(NST0):
                    bank = [k.ps2[:, :], k.ps3[:, :], k.pq[:, 0:512]][st]
                    for i in range(4):
                        hc = hg * 4 + i
                        w = [w2tok, relu_toks[hc]]
                        if hg == 0 and i == 0 and "ycons%d" % st in state:
                            w.append(state.pop("ycons%d" % st))
                        last4[st] = p.mm(
                            bank,
                            k.xn[:, hc * C0 + st * 128:hc * C0 + st * 128 + 128],
                            w2ap(slot, i * 512, (i + 1) * 512),
                            start=(hg == 0 and i == 0), stop=False, waits=w,
                            inc="spe" if (st == NST0 - 1 and i == 3) else None)
                state[wkey] = last4[NST0 - 1]
            for st in range(NST0):
                bank = [k.ps2[:, :], k.ps3[:, :], k.pq[:, 0:512]][st]
                bm = p.mm(bank, k.ones_row_bf[0:1, 0:128],
                          k.biasrow[0:1, oc * 512:(oc + 1) * 512],
                          start=False, stop=True, waits=[b2tok], inc="spe")
                ev = p.dve("tensor_copy",
                           k.bufa[:, 3072 + st * 1024 + oc * 512:
                                  3072 + st * 1024 + (oc + 1) * 512],
                           bank, waits=[bm] +
                           ([state.pop("ye%d_%d" % (st, oc))]
                            if "ye%d_%d" % (st, oc) in state else []),
                           inc="sdve")
                state["ycons%d" % st] = ev
                ydones[(st, oc)] = ev
        ytoks = list(ydones.values())
        for tg in range(4):
            trs = []
            for j in range(4):
                t = tg * 4 + j
                w = ytoks if j == 0 else [trs[-1]]
                if "rowc" in state and j == 0:
                    w = w + [state.pop("rowc")]
                t1 = p.tr(k.pt0[0:1, j * 128:(j + 1) * 128]
                          if False else k.ps0[0:1, j * 128:(j + 1) * 128],
                          k.route[:, t * 16 + 11:t * 16 + 12],
                          k.ident[:, :], waits=w, inc="spe")
                t2 = p.tr(k.ps1[0:1, j * 128:(j + 1) * 128],
                          k.route[:, t * 16 + 0:t * 16 + 1],
                          k.ident[:, :], waits=[t1], inc="spe")
                trs.append(t2)
            r1 = p.dve("tensor_copy", k.rowbuf[0:1, 0:512], k.ps0[0:1, 0:512],
                       waits=[trs[-1]] + ([state.pop("rowr")]
                                          if "rowr" in state else []),
                       inc="sdve")
            r2 = p.dve("tensor_copy", k.rowbuf[0:1, 512:1024],
                       k.ps1[0:1, 0:512], waits=[trs[-1], r1], inc="sdve")
            state["rowc"] = r2
            m1 = p.mm(k.ps0[:, :], k.ones_row[0:1, 0:128],
                      k.rowbuf[0:1, 0:512], start=True, stop=False,
                      waits=[r1], inc="spe")
            m1 = p.mm(k.ps0[:, :], k.niota_row[0:1, 0:128],
                      k.ones_row[0:1, 0:512], start=False, stop=True,
                      waits=[], inc="spe")
            mp = p.mm(k.ps1[:, :], k.ones_row[0:1, 0:128],
                      k.rowbuf[0:1, 512:1024], start=True, stop=True,
                      waits=[r2], inc="spe")
            state["rowr"] = mp
            ohts = []
            for st in range(C0 // 128):
                o1 = p.dve("tensor_single_scalar",
                           k.stage[:, st * 512:(st + 1) * 512], k.ps0[:, :],
                           float(e * C0 + st * 128), ALU.is_equal,
                           waits=[m1] + ([state.pop("cmb%d" % st)]
                                         if "cmb%d" % st in state else []),
                           inc="sdve")
                o2 = p.dve("tensor_tensor",
                           k.stage[:, st * 512:(st + 1) * 512],
                           k.stage[:, st * 512:(st + 1) * 512], k.ps1[:, :],
                           ALU.mult, waits=[o1, mp], inc="sdve")
                ohts.append(o2)
            state["rowr"] = ohts[-1]
            state["rowc"] = ohts[-1]
            for c in range(NCH):
                bank = k.ps2 if c % 2 == 0 else k.ps3
                mmt = None
                nst = C0 // 128
                for st in range(nst):
                    w = [ohts[st], ydones[(st, c // 4)]]
                    if st == 0 and "cadd%d" % (c % 2) in state:
                        w.append(state.pop("cadd%d" % (c % 2)))
                    mmt = p.mm(bank[:, :],
                               k.bufa[:, 3072 + st * 1024 + c * 128:
                                      3072 + st * 1024 + (c + 1) * 128],
                               k.stage[:, st * 512:(st + 1) * 512],
                               start=(st == 0), stop=(st == nst - 1), waits=w,
                               inc="spe" if st == nst - 1 else None)
                at = p.dve("tensor_tensor",
                           k.x[:, c * NT + tg * 512:c * NT + (tg + 1) * 512],
                           k.x[:, c * NT + tg * 512:c * NT + (tg + 1) * 512],
                           bank[:, :], ALU.add,
                           waits=[mmt] + list(pre), inc="sdve")
                state["cadd%d" % (c % 2)] = at
                x_toks.append(at)
            for st in range(C0 // 128):
                state["cmb%d" % st] = x_toks[-1]
        for cw in range(4):
            state["xdw%d" % cw] = x_toks[-1]
    return x_toks


def e_ffn_l1_expert_mm(p, k, dram, eout_dram, xntm_toks, e, state):
    guard = state.get("guard")
    xev = []
    for sg in range(2):
        ohs = []
        for t in range(NTT):
            w = [guard] if guard else []
            ok = "ohb%d" % t
            if ok in state:
                w.append(state.pop(ok))
            SG1 = C1 // 2
            o1 = p.dve("tensor_scalar",
                       k.bufa[:, 6144 + t * SG1:6144 + (t + 1) * SG1],
                       k.iota512[:, 0:SG1],
                       k.route[:, t * 16 + 12:t * 16 + 13],
                       -float(e * C1 + sg * SG1),
                       ALU.subtract, ALU.is_equal, waits=w, inc="sdve")
            pp = t % 2
            t2w = list(w)
            if "ohtmp%d" % pp in state:
                t2w.append(state.pop("ohtmp%d" % pp))
            t2 = p.dve("tensor_scalar",
                       k.stage[:, 1024 + pp * SG1:1024 + (pp + 1) * SG1],
                       k.iota512[:, 0:SG1],
                       k.route[:, t * 16 + 13:t * 16 + 14],
                       -float(e * C1 + sg * SG1),
                       ALU.subtract, ALU.is_equal, waits=t2w, inc="sdve")
            oh = p.dve("tensor_tensor",
                       k.bufa[:, 6144 + t * SG1:6144 + (t + 1) * SG1],
                       k.bufa[:, 6144 + t * SG1:6144 + (t + 1) * SG1],
                       k.stage[:, 1024 + pp * SG1:1024 + (pp + 1) * SG1],
                       ALU.add, waits=[o1, t2], inc="sdve")
            state["ohtmp%d" % pp] = oh
            ohs.append(oh)
        SG1 = C1 // 2
        for wave in range(2):
            mmt = None
            for t in range(NTT):
                for cw in range(4):
                    c = wave * 4 + cw
                    bank = [k.ps0, k.ps1, k.ps2, k.ps3][cw]
                    w = [ohs[t], xntm_toks[t]]
                    if t == 0:
                        w.append(state.get("xdw%d" % cw))
                    mmt = p.mm(bank[:, 0:SG1],
                               k.bufb[:, t * 1024 + c * 128:
                                      t * 1024 + (c + 1) * 128],
                               k.bufa[:, 6144 + t * SG1:6144 + (t + 1) * SG1],
                               start=(t == 0), stop=(t == NTT - 1), waits=w,
                               inc="spe" if cw == 3 else None)
            for cw in range(4):
                c = wave * 4 + cw
                bank = [k.ps0, k.ps1, k.ps2, k.ps3][cw]
                ev = p.dve("tensor_copy",
                           k.bufa[:, c * C1 + sg * SG1:
                                  c * C1 + (sg + 1) * SG1],
                           bank[:, 0:SG1],
                           waits=[mmt, state.pop("xfw%d_%d" % (c, sg), None)],
                           inc="sdve")
                state["xdw%d" % cw] = ev
                xev.append(ev)
    xet = xev[-1]
    for t in range(NTT):
        state["ohb%d" % t] = xet
    p.dma("pool", k.biasrow[0:1, 0:2048], dram["bg1r"][e:e + 1, :], "db1g",
          waits=[state["bgw"]] if "bgw" in state else [guard])
    p.dma("pool", k.biasrow2[0:1, 0:2048], dram["bu1r"][e:e + 1, :], "db1g")
    btok = p.tok("db1g")
    h_toks = []

    def wgap(slot, c0, c1):
        if slot < 2:
            return k.bufa[:, 12288 + slot * 2048 + c0:
                          12288 + slot * 2048 + c1]
        return k.xn[:, 12288 + (slot - 2) * 2048 + c0:
                    12288 + (slot - 2) * 2048 + c1]
    for gf in range(16):
        slot = gf % 4
        wg = ([state["wgc%d" % slot]] if "wgc%d" % slot in state else []) + [xet]
        SG1 = C1 // 2
        p.dma("pool", wgap(slot, 0, 1024)
              .rearrange("p (ci f) -> p ci f", ci=8),
              dram["wg1"][e].rearrange("(ci p) f -> p ci f", p=128)
              [:, :, gf * 128:(gf + 1) * 128], "dwg%d" % slot, waits=wg)
        p.dma("pool", wgap(slot, 1024, 2048)
              .rearrange("p (ci f) -> p ci f", ci=8),
              dram["wu1"][e].rearrange("(ci p) f -> p ci f", p=128)
              [:, :, gf * 128:(gf + 1) * 128], "dwg%d" % slot)
        wt = p.tok("dwg%d" % slot)
        for sh in range(2):
            bg = k.ps2[:, 0:SG1] if sh == 0 else k.ps3[:, 0:SG1]
            bu = k.pq[:, 0:SG1] if sh == 0 else k.pq[:, 512:512 + SG1]
            m1 = m2 = None
            for ci in range(NCH):
                w = [wt, xet]
                if ci == 0 and len(h_toks) >= 2:
                    w.append(h_toks[-2])
                m1 = p.mm(bg,
                          wgap(slot, ci * 128, (ci + 1) * 128),
                          k.bufa[:, ci * C1 + sh * SG1:
                                 ci * C1 + (sh + 1) * SG1],
                          start=(ci == 0), stop=False, waits=w)
                m2 = p.mm(bu,
                          wgap(slot, 1024 + ci * 128, 1024 + (ci + 1) * 128),
                          k.bufa[:, ci * C1 + sh * SG1:
                                 ci * C1 + (sh + 1) * SG1],
                          start=(ci == 0), stop=False, waits=w)
            m1 = p.mm(bg, k.biasrow[0:1, gf * 128:(gf + 1) * 128],
                      k.ones_row_bf[0:1, 0:SG1], start=False, stop=True,
                      waits=[btok], inc="spe")
            m2 = p.mm(bu, k.biasrow2[0:1, gf * 128:(gf + 1) * 128],
                      k.ones_row_bf[0:1, 0:SG1], start=False, stop=True,
                      waits=[btok], inc="spe")
            sw = [m1] + ([state.pop("sil%d" % sh)]
                         if "sil%d" % sh in state else [])
            sg_ = p.act("activation", k.stage[:, sh * SG1:(sh + 1) * SG1],
                        bg, AF.Silu, waits=sw, inc="sact")
            state["sil%d" % sh] = sg_
            hw = [sg_, m2]
            hk = "hc%d_%d" % (gf, sh)
            if hk in state:
                hw.append(state.pop(hk))
            ht = p.dve("tensor_tensor",
                       k.xn[:, gf * C1 + sh * SG1:gf * C1 + (sh + 1) * SG1],
                       k.stage[:, sh * SG1:(sh + 1) * SG1], bu, ALU.mult,
                       waits=hw, inc="sdve")
            h_toks.append(ht)
        state["wgc%d" % slot] = h_toks[-1]
    state["bgw"] = h_toks[-1]
    p.dma("pool", k.biasrow[0:1, 0:1024], dram["bd1r"][e:e + 1, :], "db1d",
          waits=[h_toks[-1]])
    bdt = p.tok("db1d")
    def wdap(slot, c0, c1):
        if slot < 2:
            return k.rope[:, slot * 2048 + c0:slot * 2048 + c1]
        return k.scr[:, 1024:2048].bitcast(BF16)[:, c0:c1]
    for wave in range(2):
        ydones = {}
        for oc in range(2):
            wlast = None
            for ggg in range(4):
                slot = ((wave * 2 + oc) * 4 + ggg) % 3
                wkey = "wdc%d" % slot
                w = [state[wkey]] if wkey in state else [guard]
                p.dma("pool", wdap(slot, 0, 2048)
                      .rearrange("p (gr f) -> p gr f", gr=4),
                      dram["wd1"][e].rearrange("(gh p) f -> p gh f", p=128)
                      [:, ggg * 4:(ggg + 1) * 4, oc * 512:(oc + 1) * 512],
                      "dwd%d" % slot, waits=w)
                wdt = p.tok("dwd%d" % slot)
                for sq in range(3):
                    st = wave * 3 + sq
                    bank = [k.ps2[:, :], k.ps3[:, :], k.pq[:, 0:512]][sq]
                    for i in range(4):
                        gf2 = ggg * 4 + i
                        w2 = [wdt, h_toks[2 * gf2], h_toks[2 * gf2 + 1]]
                        if ggg == 0 and i == 0 and "yc%d" % sq in state:
                            w2.append(state.pop("yc%d" % sq))
                        wlast = p.mm(bank,
                                     k.xn[:, gf2 * C1 + st * 128:
                                          gf2 * C1 + st * 128 + 128],
                                     wdap(slot, i * 512, (i + 1) * 512),
                                     start=(ggg == 0 and i == 0), stop=False,
                                     waits=w2,
                                     inc="spe" if (sq == 2 and i == 3)
                                     else None)
                state[wkey] = wlast
            for sq in range(3):
                st = wave * 3 + sq
                bank = [k.ps2[:, :], k.ps3[:, :], k.pq[:, 0:512]][sq]
                bm = p.mm(bank, k.ones_row_bf[0:1, 0:128],
                          k.biasrow[0:1, oc * 512:(oc + 1) * 512],
                          start=False, stop=True, waits=[bdt], inc="spe")
                ev = p.dve("tensor_copy",
                           k.bufa[:, 6144 + sq * 1024 + oc * 512:
                                  6144 + sq * 1024 + oc * 512 + 512],
                           bank, waits=[bm] +
                           ([state.pop("ye%d" % sq)]
                            if "ye%d" % sq in state else []),
                           inc="sdve")
                state["yc%d" % sq] = ev
                ydones[(sq, oc)] = ev
        for sq in range(3):
            st = wave * 3 + sq
            p.dma("act",
                  eout_dram[e * C1 + st * 128:e * C1 + (st + 1) * 128, :],
                  k.bufa[:, 6144 + sq * 1024:6144 + (sq + 1) * 1024],
                  "dye1", waits=[ydones[(sq, 0)], ydones[(sq, 1)]])
        ydall = p.tok("dye1")
        for sq in range(3):
            state["ye%d" % sq] = ydall
    for c in range(NCH):
        for sg in range(2):
            state["xfw%d_%d" % (c, sg)] = h_toks[-1]
    state["guard"] = p.tok("dye1")
    state["bgw"] = p.tok("dye1")
    return p.tok("dye1")


def e_l1_experts_mm(p, k, dram, eout_dram, xntm_toks, pre):
    state = {"guard": pre[-1]}
    last = None
    for e in range(E1):
        last = e_ffn_l1_expert_mm(p, k, dram, eout_dram, xntm_toks, e, state)
    return last


def e_counts_cc(p, k, cc_in, cc_out, ctok):
    d1 = p.dma("act", cc_in, k.cnt[0:8, 0:1], "dcc1", waits=[ctok])
    cc = p.pool("collective_compute", "AllGather", ALU.bypass,
                replica_groups=[list(range(NCORES))],
                ins=[cc_in.opt()], outs=[cc_out.opt()],
                waits=[p.tok("dcc1")], inc="ccs", step=1)
    ld = p.dma("sp", k.cnt[0:8, 8:16], cc_out.rearrange("(b e) -> b e", e=8),
               "dca", waits=[cc])
    return p.tok("dca")


def e_a2a_reshard(p, k, a_io, x_toks):
    ccs = []
    for h in range(2):
        a_in, a_out = a_io[h]
        for c in range(4 * h, 4 * h + 4):
            for sl in range(8):
                p.dma("act", a_in[sl, (c - 4 * h) * 128:
                                  (c - 4 * h + 1) * 128, :],
                      k.x[:, c * NT + sl * 256:c * NT + sl * 256 + 256],
                      "da2a%d" % h, waits=x_toks)
        cc = p.pool("collective_compute", "AllToAll", ALU.bypass,
                    replica_groups=[list(range(NCORES))],
                    ins=[a_in.opt()], outs=[a_out.opt()],
                    waits=[p.tok("da2a%d" % h)], inc="ccs", step=1)
        ccs.append(cc)
    toks = []
    for c in range(NCH):
        h = c // 4
        a_in, a_out = a_io[h]
        for sl in range(8):
            p.dma("sp", k.x[:, c * NT + sl * 256:c * NT + sl * 256 + 256],
                  a_out[sl, (c - 4 * h) * 128:(c - 4 * h + 1) * 128, :],
                  "dx2_%d" % c, waits=[ccs[h]])
        toks.append(p.tok("dx2_%d" % c))
    return toks


def build_progA():
    nc = bass.Bass("TRN2", target_bir_lowering=False, debug=False)
    dram = declare_const_inputs(nc)
    dram["xT"] = nc.dram_tensor("xT", [D, NT], F32, kind="ExternalInput").ap()
    dram["gate_t"] = nc.dram_tensor("gate_t", [128, 64], F32,
                                    kind="ExternalInput").ap()
    cnts = nc.dram_tensor("cnts", [8, 1], F32, kind="ExternalOutput").ap()
    p = Prog(nc)
    k = alloc_common(nc)
    ct0 = e_load_consts(p, k, dram)
    xt = e_load_x(p, k, dram["xT"])
    xn_toks = e_ln_plain(p, k, ct0 + xt)
    ctok, _ = e_l0_counts(p, k, dram, xn_toks)
    st = p.dma("act", cnts, k.cnt[0:8, 0:1], "dout", waits=[ctok])
    p.wait("act", [st])
    return p.finalize()


def declare_l0_attn_inputs(nc):
    d = {}
    d["wqa"] = nc.dram_tensor("wqa", [128, 512], F32, kind="ExternalInput").ap()
    d["wkva"] = nc.dram_tensor("wkva", [128, 256], F32,
                               kind="ExternalInput").ap()
    d["wqb2"] = nc.dram_tensor("wqb2", [QL, 2048], F32,
                               kind="ExternalInput").ap()
    d["wkvb3"] = nc.dram_tensor("wkvb3", [KVL, 768], F32,
                                kind="ExternalInput").ap()
    d["wout"] = nc.dram_tensor("wout", [D, D], F32, kind="ExternalInput").ap()
    d["bout_r"] = nc.dram_tensor("bout_r", [1, D], F32,
                                 kind="ExternalInput").ap()
    return d


def e_l0_attn(p, k, dram, xn_toks, pre):
    VT0 = 4096
    QK0 = 8256
    TC0 = 10304
    PT0 = 12352
    WQ0 = 13376
    WK0 = 15424
    w = list(pre)
    p.dma("pool", k.stage[:, 0:512], dram["wqa"], "dwa", waits=w)
    p.dma("pool", k.stage[:, 512:768], dram["wkva"], "dwa")
    p.dma("pool", k.bufa[0:QL, WQ0:WQ0 + 2048], dram["wqb2"], "dwa", waits=w)
    p.dma("pool", k.bufa[0:KVL, WK0:WK0 + 768], dram["wkvb3"], "dwa", waits=w)
    wtok = p.tok("dwa")
    vm = p.dve("memset", k.bufa[:, VT0:VT0 + 4160], 1.0, waits=pre,
               inc="sdve")
    kl_toks = []
    ev_prev = None
    for tg in range(4):
        m2 = None
        for c in range(NCH):
            w = [wtok] + xn_toks
            if c == 0 and ev_prev is not None:
                w.append(ev_prev)
            m2 = p.mm(k.ps1[0:KVL, :],
                      k.stage[:, 512 + c * 32:512 + (c + 1) * 32],
                      k.xn[:, c * NT + tg * 512:c * NT + tg * 512 + 512],
                      start=(c == 0), stop=(c == NCH - 1), waits=w,
                      inc="spe" if c == NCH - 1 else None)
        e2 = p.dve("tensor_copy",
                   k.bufa[0:KVL, QK0 + tg * 512:QK0 + tg * 512 + 512],
                   k.ps1[0:KVL, :], waits=[m2, vm], inc="sdve")
        ev_prev = e2
        kl_toks.append(e2)
    kv_toks = []
    vtr_toks = []
    kvread = None
    for j in range(2):
        vev = []
        rp_prev = None
        for tg in range(4):
            w = [kl_toks[-1]] + ([rp_prev] if rp_prev else [])
            mk = p.mm(k.ps0[:, :], k.bufa[0:KVL, WK0 + j * 128:WK0 + (j + 1) * 128],
                      k.bufa[0:KVL, QK0 + tg * 512:QK0 + tg * 512 + 512],
                      start=True, stop=True, waits=w, inc="spe")
            ms = p.mm(k.ps1[:, :],
                      k.bufa[0:KVL, WK0 + 256 + j * 128:WK0 + 256 + (j + 1) * 128],
                      k.bufa[0:KVL, QK0 + tg * 512:QK0 + tg * 512 + 512],
                      start=True, stop=True, waits=w, inc="spe")
            mv = p.mm(k.ps2[:, :],
                      k.bufa[0:KVL, WK0 + 512 + j * 128:WK0 + 512 + (j + 1) * 128],
                      k.bufa[0:KVL, QK0 + tg * 512:QK0 + tg * 512 + 512],
                      start=True, stop=True,
                      waits=w + ([vev[-1]] if vev else []), inc="spe")
            kvread = mv
            t1 = p.dve("tensor_tensor", k.scr[:, 0:512], k.ps0[:, :],
                       k.rope[:, tg * 512:(tg + 1) * 512], ALU.mult,
                       waits=[mk, rp_prev], inc="sdve")
            t2 = p.dve("tensor_tensor", k.scr2[:, 0:512], k.ps1[:, :],
                       k.rope[:, NT + tg * 512:NT + (tg + 1) * 512], ALU.mult,
                       waits=[ms, t1], inc="sdve")
            t3 = p.dve("tensor_tensor",
                       k.bufa[:, j * 2048 + tg * 512:j * 2048 + (tg + 1) * 512],
                       k.scr[:, 0:512], k.scr2[:, 0:512], ALU.add,
                       waits=[t2], inc="sdve")
            rp_prev = t3
            kv_toks.append(t3)
            ve = p.dve("tensor_copy", k.bufa[:, TC0 + tg * 512:TC0 + (tg + 1) * 512],
                       k.ps2[:, :], waits=[mv] + ([vev[-1]] if vev else []),
                       inc="sdve")
            vev.append(ve)
        tprev = [None, None]
        for kt in range(16):
            for half in range(2):
                kvh = 2 * j + half
                bank = k.pt0 if (kt + half) % 2 == 0 else k.pt1
                w = [vev[kt // 4], vm]
                if tprev[(kt + half) % 2] is not None:
                    w.append(tprev[(kt + half) % 2])
                tt = p.tr(bank[0:128, 0:64],
                          k.bufa[half * 64:half * 64 + 64,
                                 TC0 + kt * 128:TC0 + (kt + 1) * 128],
                          k.ident_bf[half * 64:half * 64 + 64,
                                     half * 64:half * 64 + 64],
                          waits=w, inc="spe")
                cc = p.dve("tensor_copy",
                           k.bufa[:, VT0 + (kvh * 16 + kt) * 65:
                                  VT0 + (kvh * 16 + kt) * 65 + 64],
                           bank[0:128, 0:64], waits=[tt], inc="sdve")
                tprev[(kt + half) % 2] = cc
                vtr_toks.append(cc)
    ql_toks = []
    ev_prev = None
    for tg in range(4):
        m1 = None
        for c in range(NCH):
            w = [wtok, kvread] + xn_toks
            if c == 0 and ev_prev is not None:
                w.append(ev_prev)
            m1 = p.mm(k.ps0[0:QL, :], k.stage[:, c * 64:(c + 1) * 64],
                      k.xn[:, c * NT + tg * 512:c * NT + tg * 512 + 512],
                      start=(c == 0), stop=(c == NCH - 1),
                      waits=w + [kv_toks[-1]],
                      inc="spe" if c == NCH - 1 else None)
        e1 = p.dve("tensor_copy", k.bufa[0:QL, QK0 + tg * 512:QK0 + tg * 512 + 512],
                   k.ps0[0:QL, :], waits=[m1, vtr_toks[-1]], inc="sdve")
        ev_prev = e1
        ql_toks.append(e1)
    xn_done = p.tok("spe")
    xadd_toks = []
    qr_prev = None
    pv_prev = [None, None]
    sc_prev = [None, None]
    exp_prev = [None, None]
    bc_prev = None
    sh_prev = None
    zlast = None
    fin_prev = None
    for c in range(NCH):
        qr_toks = []
        for tg in range(4):
            w = [wtok, ql_toks[-1], vtr_toks[-1]]
            if qr_prev is not None:
                w.append(qr_prev)
            if zlast is not None:
                w.append(zlast)
            mq = p.mm(k.ps0[:, :], k.bufa[0:QL, WQ0 + c * 128:WQ0 + (c + 1) * 128],
                      k.bufa[0:QL, QK0 + tg * 512:QK0 + tg * 512 + 512],
                      start=True, stop=True, waits=w, inc="spe")
            ms = p.mm(k.ps1[:, :],
                      k.bufa[0:QL, WQ0 + 1024 + c * 128:WQ0 + 1024 + (c + 1) * 128],
                      k.bufa[0:QL, QK0 + tg * 512:QK0 + tg * 512 + 512],
                      start=True, stop=True, waits=w, inc="spe")
            t1 = p.dve("tensor_tensor", k.scr[:, 0:512], k.ps0[:, :],
                       k.rope[:, tg * 512:(tg + 1) * 512], ALU.mult,
                       waits=[mq] + ([qr_prev] if qr_prev else []) +
                       ([zlast] if zlast else []), inc="sdve")
            t2 = p.dve("tensor_tensor", k.scr2[0:128, 0:512], k.ps1[:, :],
                       k.rope[:, NT + tg * 512:NT + (tg + 1) * 512], ALU.mult,
                       waits=[ms, t1], inc="sdve")
            t3 = p.dve("tensor_tensor",
                       k.bufa[:, TC0 + tg * 512:TC0 + (tg + 1) * 512],
                       k.scr[:, 0:512], k.scr2[0:128, 0:512], ALU.add,
                       waits=[t2, vtr_toks[-1]], inc="sdve")
            qr_toks.append(t3)
            qr_prev = t3
        for hh in range(2):
            h = HPAIR[c][hh]
            kvh = h // 4
            j, rbk = kvh // 2, (kvh % 2) * 64
            rbq = hh * 64
            assert rbk == rbq
            for Q in range(4):
                par = (c * 8 + hh * 4 + Q) % 2
                pvbank = k.pq[0:65, par * 512:par * 512 + 512]
                nkt = 4 * Q + 4
                pv = None
                exs = [None] * nkt

                def emit_score(kt):
                    sbank = k.ps2 if kt % 2 == 0 else k.ps3
                    w = [qr_toks[Q], kv_toks[-1]]
                    if exp_prev[kt % 2] is not None:
                        w.append(exp_prev[kt % 2])
                    msc = p.mm(sbank[:, :],
                               k.bufa[rbk:rbk + 64,
                                      j * 2048 + kt * 128:j * 2048 + (kt + 1) * 128],
                               k.bufa[rbq:rbq + 64,
                                      TC0 + Q * 512:TC0 + (Q + 1) * 512],
                               start=True, stop=True, waits=w, inc="spe")
                    pslot = k.bufa[:, PT0 + (kt % 2) * 512:
                                   PT0 + (kt % 2) * 512 + 512]
                    w2 = [msc]
                    if pv_prev[kt % 2] is not None:
                        w2.append(pv_prev[kt % 2])
                    ex = p.act("activation", pslot, sbank[:, :], AF.Exp,
                               scale=0.125, waits=w2, inc="sact")
                    if kt >= 4 * Q:
                        jm = kt - 4 * Q
                        ex = p.dve("tensor_tensor", pslot, pslot,
                                   k.masks[:, 384 - 128 * jm:896 - 128 * jm],
                                   ALU.mult, waits=[ex], inc="sdve")
                    exp_prev[kt % 2] = ex
                    exs[kt] = ex

                emit_score(0)
                for kt in range(nkt):
                    if kt + 1 < nkt:
                        emit_score(kt + 1)
                    if kt == 0 and fin_prev is not None:
                        fin_prev()
                        fin_prev = None
                    pslot = k.bufa[:, PT0 + (kt % 2) * 512:
                                   PT0 + (kt % 2) * 512 + 512]
                    w = [exs[kt]]
                    if kt == 0:
                        w.append(sc_prev[par])
                    pv = p.mm(pvbank,
                              k.bufa[:, VT0 + (kvh * 16 + kt) * 65:
                                     VT0 + (kvh * 16 + kt) * 65 + 65],
                              pslot, start=(kt == 0), stop=(kt == nkt - 1),
                              waits=w, inc="spe" if kt == nkt - 1 else None)
                    pv_prev[kt % 2] = pv
                rc = p.dve("reciprocal", k.scr2[64:65, 512:1024],
                           k.pq[64:65, par * 512:par * 512 + 512],
                           waits=[pv] + ([bc_prev] if bc_prev else []),
                           inc="sdve")

                def mk_fin(rc=rc, pv=pv, par=par, c=c, hh=hh, Q=Q,
                           qlast=qr_toks[-1]):
                    def fin():
                        nonlocal bc_prev, sh_prev, zlast
                        bc = p.mm(k.ps0[0:64, :], k.onesf[64:65, 0:64],
                                  k.scr2[64:65, 512:1024], start=True,
                                  stop=True, waits=[rc, qlast], inc="spe")
                        bc_prev = bc
                        zc = p.dve("tensor_copy", k.scr[0:64, 512:1024],
                                   k.pq[0:64, par * 512:par * 512 + 512],
                                   waits=[pv, bc], inc="sdve")
                        sc_prev[par] = zc
                        if hh == 0:
                            zn = p.dve("tensor_tensor",
                                       k.bufb[0:64, c * 2048 + Q * 512:
                                              c * 2048 + (Q + 1) * 512],
                                       k.scr[0:64, 512:1024], k.ps0[0:64, :],
                                       ALU.mult, waits=[bc, zc], inc="sdve")
                            zlast = zn
                            xadd_toks.append(zn)
                        else:
                            zn = p.dve("tensor_tensor", k.stage[0:64, 768:1280],
                                       k.scr[0:64, 512:1024], k.ps0[0:64, :],
                                       ALU.mult,
                                       waits=[bc, zc] +
                                       ([sh_prev] if sh_prev else []),
                                       inc="sdve")
                            sh = p.mm(k.ps1[64:128, :], k.ident_bf[0:64, 0:64],
                                      k.stage[0:64, 768:1280], start=True,
                                      stop=True, waits=[zn], inc="spe")
                            cp = p.dve("tensor_copy",
                                       k.bufb[64:128, c * 2048 + Q * 512:
                                              c * 2048 + (Q + 1) * 512],
                                       k.ps1[64:128, :], waits=[sh],
                                       inc="sdve")
                            sh_prev = cp
                            zlast = cp
                            xadd_toks.append(cp)
                    return fin

                fin_prev = mk_fin()
        if fin_prev is not None:
            fin_prev()
            fin_prev = None
    az_toks = list(xadd_toks)
    xadd_toks = []
    p.dma("pool", k.biasrow2[0:1, 0:1024], dram["bout_r"], "dwo",
          waits=[az_toks[-1]])
    wo_state = {}
    add_prev = [None, None]
    for oc in range(NCH):
        slot = oc % 2
        w = [wo_state.get(slot) or az_toks[-1]]
        p.dma("pool", k.bufa[:, slot * 2048:slot * 2048 + 1024]
              .rearrange("p (ci f) -> p ci f", ci=8),
              dram["wout"].rearrange("(ci p) f -> p ci f", p=128)
              [:, :, oc * 128:(oc + 1) * 128], "dwo%d" % slot, waits=w)
        wtok2 = p.tok("dwo%d" % slot)
        for tg in range(4):
            bank = k.ps2 if tg % 2 == 0 else k.ps3
            mmt = None
            for ci in range(NCH):
                w = [wtok2, az_toks[-1]]
                if ci == 0 and add_prev[tg % 2] is not None:
                    w.append(add_prev[tg % 2])
                mmt = p.mm(bank[:, :],
                           k.bufa[:, slot * 2048 + ci * 128:
                                  slot * 2048 + (ci + 1) * 128],
                           k.bufb[:, ci * 2048 + tg * 512:
                                  ci * 2048 + (tg + 1) * 512],
                           start=(ci == 0), stop=False, waits=w)
            mmt = p.mm(bank[:, :], k.biasrow2[0:1, oc * 128:(oc + 1) * 128],
                       k.ones_row_bf[0:1, 0:512], start=False, stop=True,
                       waits=[p.tok("dwo")], inc="spe")
            at = p.dve("tensor_tensor",
                       k.x[:, oc * NT + tg * 512:oc * NT + (tg + 1) * 512],
                       k.x[:, oc * NT + tg * 512:oc * NT + (tg + 1) * 512],
                       bank[:, :], ALU.add, waits=[mmt], inc="sdve")
            add_prev[tg % 2] = at
            xadd_toks.append(at)
        wo_state[slot] = xadd_toks[-1]
    return xadd_toks, xn_done


def declare_l0_weight_inputs(nc, with_counts=True):
    d = {}
    d["gate_t"] = nc.dram_tensor("gate_t", [128, 64], F32,
                                 kind="ExternalInput").ap()
    if with_counts:
        d["cnts_all"] = nc.dram_tensor("cnts_all", [8, 8], F32,
                                       kind="ExternalInput").ap()
    d["mymask"] = nc.dram_tensor("mymask", [8, 1], F32,
                                 kind="ExternalInput").ap()
    d["c_ec0"] = nc.dram_tensor("c_ec0", [8, 1], F32,
                                kind="ExternalInput").ap()
    d["w1"] = nc.dram_tensor("w1", [E0, D, DFF0], F32,
                             kind="ExternalInput").ap()
    d["b1c"] = nc.dram_tensor("b1c", [E0, 128, 32], F32,
                              kind="ExternalInput").ap()
    d["w2"] = nc.dram_tensor("w2", [E0, DFF0, D], F32,
                             kind="ExternalInput").ap()
    d["b2r"] = nc.dram_tensor("b2r", [E0, D], F32,
                              kind="ExternalInput").ap()
    return d


def build_progB(with_attn=False):
    nc = bass.Bass("TRN2", target_bir_lowering=False, debug=False)
    dram = declare_const_inputs(nc)
    dram["xT"] = nc.dram_tensor("xT", [D, NT], F32, kind="ExternalInput").ap()
    dram.update(declare_l0_weight_inputs(nc))
    if with_attn:
        dram.update(declare_l0_attn_inputs(nc))
    x1T = nc.dram_tensor("x1T", [D, NT], F32, kind="ExternalOutput").ap()
    disp = nc.dram_tensor("disp0", [NSLOT0, D], BF16, kind="Internal").ap()
    eout = nc.dram_tensor("eout0", [NSLOT0, D], BF16, kind="Internal").ap()
    p = Prog(nc)
    k = alloc_common(nc)
    ct0 = e_load_consts(p, k, dram)
    xt_toks = e_load_x(p, k, dram["xT"])
    gt0 = []
    xn_toks = e_ln_plain(p, k, ct0, chunk_toks=xt_toks, gt_out=gt0)
    ctok, oh_toks = e_l0_counts(p, k, dram, xn_toks, gtoks=gt0)
    idx_toks = e_l0_ranks(p, k, dram, ctok, oh_toks)
    sc_tok = e_dispatch(p, k, disp, NSLOT0, [0], [idx_toks],
                        [xn_toks[-1]], "dzr", "dsc")
    pre = [xn_toks[-1]]
    extra = []
    if with_attn:
        xadd, xn_done = e_l0_attn(p, k, dram, xn_toks, [sc_tok])
        pre = [xn_toks[-1], xn_done, xadd[-1]]
        extra = [xadd[-1]]
    ye_tok = e_l0_experts(p, k, dram, disp, eout, sc_tok, pre)
    add_toks = e_combine(p, k, eout, NSLOT0, 0, [idx_toks], [(0, 0)],
                         ye_tok, "dgc", extra_scale_tok=extra)
    for c in range(NCH):
        p.dma("act", x1T[c * 128:(c + 1) * 128, :],
              k.x[:, c * NT:(c + 1) * NT], "dout", waits=add_toks)
    p.wait("act", [p.tok("dout")])
    return p.finalize()


def build_fused():
    from concourse.aot_env import allow_ici_all_to_all
    nc = bass.Bass("TRN2", target_bir_lowering=False, debug=False,
                   num_devices=NCORES)
    dram = declare_const_inputs(nc)
    dram["xT"] = nc.dram_tensor("xT", [D, NT], F32, kind="ExternalInput").ap()
    dram.update(declare_l0_attn_inputs(nc))
    dram.update(declare_l0_weight_inputs(nc, with_counts=False))
    dram.update(declare_l1_inputs(nc, with_x=False))
    outT = nc.dram_tensor("outT", [8, D, 256], F32, kind="ExternalOutput").ap()
    eout = nc.dram_tensor("eout1", [NSLOT1, D], BF16, kind="Internal").ap()
    cc_in = nc.dram_tensor("cc_in", [8], F32, kind="Internal").ap()
    cc_out = nc.dram_tensor("cc_out", [64], F32, kind="Internal").ap()
    a_io = [(nc.dram_tensor("a2a_in%d" % h, [8, 512, 256], F32,
                            kind="Internal").ap(),
             nc.dram_tensor("a2a_out%d" % h, [8, 512, 256], F32,
                            kind="Internal").ap()) for h in range(2)]
    p = Prog(nc)
    k = alloc_common(nc)
    ct0 = e_load_consts(p, k, dram)
    xt_toks = e_load_x(p, k, dram["xT"])
    gt0 = []
    xn_toks = e_ln_plain(p, k, ct0, chunk_toks=xt_toks, gt_out=gt0)
    ctok, oh_toks = e_l0_counts(p, k, dram, xn_toks, gtoks=gt0)
    cnt_tok = e_counts_cc(p, k, cc_in, cc_out, ctok)
    idx_toks = e_l0_ranks(p, k, dram, ctok, oh_toks, counts_tok=cnt_tok)
    xadd0, xn_done = e_l0_attn(p, k, dram, xn_toks, [idx_toks[-1]])
    xntm_toks = e_xn_tm(p, k, [xadd0[-1], xn_done])
    moe_toks = e_moe_mm_l0(p, k, dram, xntm_toks, [xadd0[-1]])
    a2a_toks = e_a2a_reshard(p, k, a_io, [moe_toks[-1]])
    gt2 = []
    xn2 = e_ln_plain(p, k, [], chunk_toks=a2a_toks, gt_out=gt2)
    i1, i2 = e_l1_routing(p, k, dram, xn2, gtoks=gt2)
    st2 = e_ln2_stats(p, k, xn2 + list(gt2[3]))
    xadd1 = e_l1_attn(p, k, dram, xn2, st2, [i2[-1]])
    xntm2 = e_xn_tm(p, k, [xadd1[-1]])
    ye = e_l1_experts_mm(p, k, dram, eout, xntm2, [xadd1[-1], xntm2[-1]])
    add_toks = e_combine(p, k, eout, NSLOT1, 0, [i1, i2], [(1, 16), (2, 32)],
                         ye, "dgc", extra_scale_tok=[xadd1[-1]])
    for s in range(8):
        for c in range(NCH):
            p.dma("act", outT[s, c * 128:(c + 1) * 128, :],
                  k.x[:, c * NT + s * 256:c * NT + s * 256 + 256], "dout",
                  waits=[add_toks[(2 * s) * 8 + c],
                         add_toks[(2 * s + 1) * 8 + c]])
    p.wait("act", [p.tok("dout")])
    with allow_ici_all_to_all():
        return p.finalize()


_KCOLS = (np.arange(256) // 64) * 128 + (np.arange(256) % 64) * 2


def host_prep(inputs):
    perm = np.array(ATTN_PERM)
    Wqb_p = np.asarray(inputs["L0_attn_Wqb"])[:, perm]
    Wkvb = np.asarray(inputs["L0_attn_Wkvb"])
    Wk = Wkvb[:, _KCOLS]
    Wv = Wkvb[:, _KCOLS + 1]
    shared = dict(host_constants())
    shared.update({
        "wqa": _chunked(np.asarray(inputs["L0_attn_Wqa"])),
        "wkva": _chunked(np.asarray(inputs["L0_attn_Wkva"])),
        "wqb2": np.ascontiguousarray(
            np.concatenate([Wqb_p, _swap_cols(Wqb_p, 64)], 1)),
        "wkvb3": np.ascontiguousarray(
            np.concatenate([Wk, _swap_cols(Wk, 64), Wv], 1)),
        "wout": np.ascontiguousarray(np.asarray(inputs["L0_attn_Wout"])[perm, :]),
        "bout_r": np.ascontiguousarray(np.asarray(inputs["L0_attn_bout"])[None]),
        "gate_t": _chunked(np.ascontiguousarray(
            np.asarray(inputs["L0_ffn_gate_w"]).T)),
        "w1": np.asarray(inputs["L0_ffn_w1"]),
        "b1c": np.ascontiguousarray(
            np.asarray(inputs["L0_ffn_b1"]).reshape(8, 32, 128)
            .transpose(0, 2, 1)),
        "w2": np.asarray(inputs["L0_ffn_w2"]),
        "b2r": np.asarray(inputs["L0_ffn_b2"]),
        "c_ec0": (np.arange(8, dtype=np.float32) * C0 - 1).reshape(8, 1),
        "c_ec1": (np.arange(8, dtype=np.float32) * C1 - 1).reshape(8, 1),
        "rw_t": _chunked(np.ascontiguousarray(
            np.asarray(inputs["L1_ffn_router_w"]).T)),
        "l1g_c": np.ascontiguousarray(
            np.asarray(inputs["L1_attn_ln_g"]).reshape(8, 128).T),
        "l1b_c": np.ascontiguousarray(
            np.asarray(inputs["L1_attn_ln_b"]).reshape(8, 128).T),
        "wq1": np.asarray(inputs["L1_attn_Wq"]),
        "wk1": np.asarray(inputs["L1_attn_Wk"]),
        "wv1": np.asarray(inputs["L1_attn_Wv"]),
        "wo1": np.asarray(inputs["L1_attn_Wo"]),
        "bq1r": np.ascontiguousarray(np.asarray(inputs["L1_attn_bq"])[None]),
        "bk1r": np.ascontiguousarray(np.asarray(inputs["L1_attn_bk"])[None]),
        "bv1r": np.ascontiguousarray(np.asarray(inputs["L1_attn_bv"])[None]),
        "bo1r": np.ascontiguousarray(np.asarray(inputs["L1_attn_bo"])[None]),
        "wg1": np.asarray(inputs["L1_ffn_gate_w"]),
        "wu1": np.asarray(inputs["L1_ffn_up_w"]),
        "bg1r": np.asarray(inputs["L1_ffn_gate_b"]),
        "bu1r": np.asarray(inputs["L1_ffn_up_b"]),
        "wd1": np.asarray(inputs["L1_ffn_down_w"]),
        "bd1r": np.asarray(inputs["L1_ffn_down_b"]),
    })
    return shared


def _pick(shared, names):
    return {n: shared[n] for n in names if n in shared}


def _in_names(nc):
    import concourse.mybir as _mb
    out = []
    for alloc in nc.m.functions[0].allocations:
        if isinstance(alloc, _mb.MemoryLocationSet) and \
                alloc.kind == "ExternalInput":
            nm = alloc.memorylocations[0].name
            if nm != "partition_id":
                out.append(nm)
    return out


USE_FUSED = True


def kernel(**inputs):
    if USE_FUSED:
        return kernel_fused(**inputs)
    return kernel_split(**inputs)


def kernel_fused(**inputs):
    shared = host_prep(inputs)
    x = np.asarray(inputs["L0_attn_x"])
    nc = build_fused()
    names = _in_names(nc)
    maps = []
    for b in range(NCORES):
        m = dict(_pick(shared, names))
        m["xT"] = np.ascontiguousarray(x[b].T)
        mask = np.zeros((8, 1), np.float32)
        mask[:b] = 1.0
        m["mymask"] = mask
        maps.append(m)
    res = run_bass_kernel_spmd(nc, maps, core_ids=list(range(NCORES)))
    out = np.empty((B, S, D), np.float32)
    for c in range(NCORES):
        out[:, c * 256:(c + 1) * 256, :] = \
            np.asarray(res.results[c]["outT"]).reshape(8, D, 256) \
            .transpose(0, 2, 1)
    return out


def kernel_split(**inputs):
    shared = host_prep(inputs)
    x = np.asarray(inputs["L0_attn_x"])
    xT = [np.ascontiguousarray(x[b].T) for b in range(NCORES)]
    masks = [np.zeros((8, 1), np.float32) for _ in range(NCORES)]
    for b in range(NCORES):
        masks[b][:b] = 1.0

    ncA = build_progA()
    namesA = _in_names(ncA)
    mapsA = []
    for b in range(NCORES):
        m = dict(_pick(shared, namesA))
        m["xT"] = xT[b]
        mapsA.append(m)
    resA = run_bass_kernel_spmd(ncA, mapsA, core_ids=list(range(NCORES)))
    cnts_all = np.ascontiguousarray(
        np.concatenate([resA.results[b]["cnts"].reshape(1, 8)
                        for b in range(NCORES)], 0))

    ncB = build_progB(with_attn=True)
    namesB = _in_names(ncB)
    mapsB = []
    for b in range(NCORES):
        m = dict(_pick(shared, namesB))
        m["xT"] = xT[b]
        m["cnts_all"] = cnts_all
        m["mymask"] = masks[b]
        mapsB.append(m)
    resB = run_bass_kernel_spmd(ncB, mapsB, core_ids=list(range(NCORES)))
    x1T = [resB.results[b]["x1T"] for b in range(NCORES)]

    x1c = [np.ascontiguousarray(
        np.stack([x1T[s][:, c * 256:(c + 1) * 256] for s in range(8)]))
        for c in range(NCORES)]

    ncC1 = build_progC1()
    namesC1 = _in_names(ncC1)
    mapsC1 = []
    for c in range(NCORES):
        m = dict(_pick(shared, namesC1))
        m["x1c"] = x1c[c]
        mapsC1.append(m)
    resC1 = run_bass_kernel_spmd(ncC1, mapsC1, core_ids=list(range(NCORES)))

    ncC2 = build_progC2()
    namesC2 = _in_names(ncC2)
    mapsC2 = []
    for c in range(NCORES):
        m = dict(_pick(shared, namesC2))
        m["x1c"] = x1c[c]
        m["xpartT"] = resC1.results[c]["xpartT"]
        m["ridx_i"] = resC1.results[c]["ridx_o"]
        m["route_i"] = resC1.results[c]["route_o"]
        m["disp1"] = resC1.results[c]["disp1"]
        mapsC2.append(m)
    resC2 = run_bass_kernel_spmd(ncC2, mapsC2, core_ids=list(range(NCORES)))

    out = np.empty((B, S, D), np.float32)
    for c in range(NCORES):
        out[:, c * 256:(c + 1) * 256, :] = \
            np.asarray(resC2.results[c]["outT"]).reshape(8, D, 256) \
            .transpose(0, 2, 1)
    return out


# version 14 (final)
# speedup vs baseline: 1.0187x; 1.0187x over previous
import numpy as np
import concourse.bass as bass
import concourse.mybir as mybir
from concourse.bass_utils import run_bass_kernel_spmd
from concourse.bass_core_rust import IndirectOffsetOnAxis

F32 = mybir.dt.float32
BF16 = mybir.dt.bfloat16
I32 = mybir.dt.int32
AF = mybir.ActivationFunctionType
ALU = mybir.AluOpType
AXX = mybir.AxisListType.X

NCORES = 8
B, S, D = 8, 2048, 1024
NT = 2048
NCH = D // 128
NTT = NT // 128
H0, KVH0, DH = 16, 4, 64
QL, KVL = 64, 32
E0, DFF0, CAP0 = 8, 4096, 2560
C0 = 384
E1, DFF1 = 8, 2048
C1 = 768
BIGIDX = float(1 << 22)
LN_EPS = 1e-5

ENGINES = ("pe", "act", "dve", "pool", "sp")


class Prog:

    def __init__(self, nc):
        self.nc = nc
        self.q = {e: [] for e in ENGINES}
        self.sems = {}
        self.counts = {}
        self.hwm = {e: {} for e in ENGINES}

    def sem(self, name):
        if name not in self.sems:
            self.sems[name] = self.nc.semaphore(name).__enter__()
            self.counts[name] = 0
        return self.sems[name]

    def emit(self, eng, fn, waits=(), inc=None, step=1):
        wl = []
        for t in waits:
            if t is None:
                continue
            sname, val = t
            if val <= 0 or self.hwm[eng].get(sname, 0) >= val:
                continue
            self.hwm[eng][sname] = val
            self.sem(sname)
            wl.append((sname, val))
        tok = None
        if inc is not None:
            self.sem(inc)
            self.counts[inc] += step
            tok = (inc, self.counts[inc])
        self.q[eng].append((wl, fn, inc, step))
        return tok

    def tok(self, semname):
        return (semname, self.counts.get(semname, 0))

    def frontier(self):
        return [(s, v) for s, v in self.counts.items()]

    def mm(self, out, lhsT, rhs, start, stop, waits=(), inc=None):
        return self.emit(
            "pe",
            lambda e, o=out, l=lhsT, r=rhs, a=start, b=stop: e.matmul(
                o, l, r, start=a, stop=b),
            waits, inc)

    def tr(self, out, in_, ident, waits=(), inc=None):
        return self.emit(
            "pe",
            lambda e, o=out, i=in_, d=ident: e.transpose(o, i, d),
            waits, inc)

    def dve(self, name, *a, waits=(), inc=None, lowp=False, **kw):
        if lowp:
            def fn(e, a=a, kw=kw, n=name):
                with self.nc.allow_low_precision("bf16 store of fp32-internal "
                                                 "8-way reduction"):
                    return getattr(e, n)(*a, **kw)
            return self.emit("dve", fn, waits, inc)
        return self.emit(
            "dve", lambda e, a=a, kw=kw, n=name: getattr(e, n)(*a, **kw),
            waits, inc)

    def act(self, name, *a, waits=(), inc=None, **kw):
        return self.emit(
            "act", lambda e, a=a, kw=kw, n=name: getattr(e, n)(*a, **kw),
            waits, inc)

    def pool(self, name, *a, waits=(), inc=None, step=1, **kw):
        return self.emit(
            "pool", lambda e, a=a, kw=kw, n=name: getattr(e, n)(*a, **kw),
            waits, inc, step)

    def dma(self, q, out, in_, sem, waits=(), **kw):
        return self.emit(
            q, lambda e, o=out, i=in_, kw=kw: e.dma_start(out=o, in_=i, **kw),
            waits, inc=sem, step=16)

    def dma_tr(self, q, out, in_, sem, waits=()):
        return self.emit(
            q, lambda e, o=out, i=in_: e.dma_start(out=o, in_=i, transpose=True),
            waits, inc=sem, step=16)

    def wait(self, eng, waits):
        self.emit(eng, lambda e: None, waits)

    def finalize(self):
        nc = self.nc
        prog = self
        with nc.Block() as blk:
            def runner(items):
                def run(eng):
                    for idx, (wl, fn, inc, step) in enumerate(items):
                        for sname, val in wl:
                            eng.wait_ge(prog.sems[sname], val)
                        try:
                            inst = fn(eng)
                        except Exception as ex:
                            raise RuntimeError(
                                "emit #%d on %s failed: %r (args %r)" % (
                                    idx, eng.engine,
                                    ex, getattr(fn, "__defaults__", None))
                            ) from ex
                        if inc is not None:
                            inst.then_inc(prog.sems[inc], step)
                return run
            if self.q["sp"]:
                blk.sync(runner(self.q["sp"]))
            if self.q["pe"]:
                blk.tensor(runner(self.q["pe"]))
            if self.q["act"]:
                blk.scalar(runner(self.q["act"]))
            if self.q["dve"]:
                blk.vector(runner(self.q["dve"]))
            if self.q["pool"]:
                blk.gpsimd(runner(self.q["pool"]))
        return nc


class NS(dict):
    __getattr__ = dict.__getitem__
    __setattr__ = dict.__setitem__



def _rope_tables():
    inv = 10000.0 ** (-np.arange(0, DH, 2, dtype=np.float64) / DH)
    t = np.arange(S, dtype=np.float64)
    fr = np.outer(t, inv)
    cos = np.cos(fr).astype(np.float32)
    sin = np.sin(fr).astype(np.float32)
    cos_il = np.empty((DH, S), np.float32)
    sin_il = np.empty((DH, S), np.float32)
    cos_il[0::2] = cos.T
    cos_il[1::2] = cos.T
    sin_il[0::2] = -sin.T
    sin_il[1::2] = sin.T
    return (np.ascontiguousarray(np.concatenate([cos_il, cos_il], 0)),
            np.ascontiguousarray(np.concatenate([sin_il, sin_il], 0)))


def _swap_cols(w, group):
    n = w.shape[-1]
    idx = np.arange(n)
    d = idx % group
    d2 = np.where(d % 2 == 0, d + 1, d - 1)
    return np.ascontiguousarray(w[..., (idx // group) * group + d2])


def _causal_masks():
    kt = np.arange(128)[:, None]
    u = np.arange(896)[None, :]
    return np.ascontiguousarray((kt <= u - 384).astype(np.float32))


def _chunked(w):
    f = w.shape[1]
    return np.ascontiguousarray(
        w.reshape(NCH, 128, f).transpose(1, 0, 2).reshape(128, NCH * f))


def host_constants():
    cos_il, sin_il = _rope_tables()
    return {
        "c_ident": np.eye(128, dtype=np.float32),
        "c_masks": _causal_masks(),
        "c_iota8": np.tile(np.arange(8, dtype=np.float32), (128, 1)),
        "c_ones_col": np.ones((128, 1), np.float32),
        "c_ones_row": np.ones((1, 512), np.float32),
        "c_rope_cos": cos_il,
        "c_rope_sin": sin_il,
        "c_fconst": np.tile(np.array([[0.0, 1.0, LN_EPS, 0.125,
                                       0.0, 0.0, 0.0, 0.0]], np.float32),
                            (128, 1)),
        "c_onesf": np.ones((128, 64), np.float32),
        "c_iota512": np.tile(np.arange(512, dtype=np.float32), (128, 1)),
        "c_niota_row": -np.arange(128, dtype=np.float32)[None, :],
    }


def const_input_specs():
    return {
        "c_ident": ([128, 128], F32),
        "c_masks": ([128, 896], F32),
        "c_iota8": ([128, 8], F32),
        "c_ones_col": ([128, 1], F32),
        "c_ones_row": ([1, 512], F32),
        "c_rope_cos": ([128, NT], F32),
        "c_rope_sin": ([128, NT], F32),
        "c_fconst": ([128, 8], F32),
        "c_onesf": ([128, 64], F32),
        "c_iota512": ([128, 512], F32),
        "c_niota_row": ([1, 128], F32),
    }



def alloc_common(nc):
    k = NS()
    k.x = nc.alloc_sbuf_tensor("A_x", [128, NCH * NT], F32)
    k.xn = nc.alloc_sbuf_tensor("A_xn", [128, NCH * NT], BF16)
    k.bufa = nc.alloc_sbuf_tensor("BUF_A", [128, 16384], BF16)
    k.bufb = nc.alloc_sbuf_tensor("BUF_B", [128, 16384], BF16)
    k.ident = nc.alloc_sbuf_tensor("ident", [128, 128], F32)
    k.ident_bf = nc.alloc_sbuf_tensor("ident_bf", [128, 128], BF16)
    k.masks = nc.alloc_sbuf_tensor("masks", [128, 896], BF16)
    k.iota8 = nc.alloc_sbuf_tensor("iota8", [128, 8], F32)
    k.ones_col = nc.alloc_sbuf_tensor("ones_col", [128, 1], F32)
    k.ones_col_bf = nc.alloc_sbuf_tensor("ones_col_bf", [128, 1], BF16)
    k.ones_row = nc.alloc_sbuf_tensor("ones_row", [1, 512], F32)
    k.ones_row_bf = nc.alloc_sbuf_tensor("ones_row_bf", [1, 512], BF16)
    k.rope = nc.alloc_sbuf_tensor("rope", [128, 2 * NT], BF16)
    k.rowbuf = nc.alloc_sbuf_tensor("rowbuf", [1, 1024], F32)
    k.stat = nc.alloc_sbuf_tensor("stat", [128, 96], F32)
    k.route = nc.alloc_sbuf_tensor("route", [128, 256], F32)
    k.ridx = nc.alloc_sbuf_tensor("ridx", [128, 64], I32)
    k.scr = nc.alloc_sbuf_tensor("scr", [128, 2048], F32)
    k.scr2 = nc.alloc_sbuf_tensor("scr2", [128, 1024], F32)
    k.stage = nc.alloc_sbuf_tensor("stage", [128, 2048], BF16)
    k.biasrow = nc.alloc_sbuf_tensor("biasrow", [1, 2048], BF16)
    k.biasrow2 = nc.alloc_sbuf_tensor("biasrow2", [1, 2048], BF16)
    k.bcols = nc.alloc_sbuf_tensor("bcols", [128, 64], F32)
    k.cnt = nc.alloc_sbuf_tensor("cnt", [128, 16], F32)
    k.fconst = nc.alloc_sbuf_tensor("fconst", [128, 8], F32)
    k.onesf = nc.alloc_sbuf_tensor("onesf", [128, 64], F32)
    k.iota512 = nc.alloc_sbuf_tensor("iota512", [128, 512], F32)
    k.niota_row = nc.alloc_sbuf_tensor("niota_row", [1, 128], F32)
    k.gatef = nc.alloc_sbuf_tensor("gatef", [128, 64], F32)
    k.zero8 = nc.alloc_sbuf_tensor("zero8", [8, 1], F32)
    k.ps0 = nc.alloc_psum_tensor("ps0", [128, 512], F32)
    k.ps1 = nc.alloc_psum_tensor("ps1", [128, 512], F32)
    k.ps2 = nc.alloc_psum_tensor("ps2", [128, 512], F32)
    k.ps3 = nc.alloc_psum_tensor("ps3", [128, 512], F32)
    k.pq = nc.alloc_psum_tensor("pq", [128, 1024], F32)
    k.pt0 = nc.alloc_psum_tensor("pt0", [128, 1024], BF16)
    k.pt1 = nc.alloc_psum_tensor("pt1", [128, 1024], BF16)
    return k


def declare_const_inputs(nc):
    d = {}
    for name, (shape, dt) in const_input_specs().items():
        d[name] = nc.dram_tensor(name, shape, dt, kind="ExternalInput").ap()
    return d


def e_load_consts(p, k, dram):
    for i, v in enumerate([0.0, 1.0, LN_EPS, 0.125]):
        p.nc.const_aps.register(F32, v, k.fconst[:, i:i + 1])
    for dst, name in [(k.ident[:, :], "c_ident"), (k.iota8[:, :], "c_iota8"),
                      (k.ones_col[:, :], "c_ones_col"),
                      (k.ones_row[:, :], "c_ones_row"),
                      (k.fconst[:, :], "c_fconst"),
                      (k.onesf[:, :], "c_onesf"),
                      (k.iota512[:, :], "c_iota512"),
                      (k.niota_row[:, :], "c_niota_row")]:
        p.dma("sp", dst, dram[name], "dc1")
    for dst, src in [
            (k.ident_bf[:, :], dram["c_ident"]),
            (k.ones_col_bf[:, :], dram["c_ones_col"]),
            (k.ones_row_bf[:, :], dram["c_ones_row"]),
            (k.rope[:, 0:NT], dram["c_rope_cos"]),
            (k.rope[:, NT:2 * NT], dram["c_rope_sin"]),
            (k.masks[:, :], dram["c_masks"])]:
        p.dma("pool", dst, src, "dc2")
    t = p.dve("memset", k.zero8[:, :], 0.0, inc="sdve")
    p.dve("memset", k.ridx[:, :], 0, inc="sdve")
    t2 = p.dve("memset", k.route[:, :], 0.0, inc="sdve")
    return [p.tok("dc1"), p.tok("dc2"), t, t2]


def e_load_x(p, k, src):
    toks = []
    for c in range(NCH):
        p.dma("sp", k.x[:, c * NT:(c + 1) * NT],
              src[c * 128:(c + 1) * 128, :], "dx%d" % c)
        toks.append(p.tok("dx%d" % c))
    return toks


def e_ln_plain(p, k, src_tok, chunk_toks=None, gt_out=None):
    xbf = k.bufa[:, 0:512]
    sqbf = k.bufa[:, 512:1024]
    outs = []
    a2 = None
    m2 = None
    bm_prev = br_prev = None
    t_m_prev = t_r_prev = None
    for g in range(4):
        m1 = None
        for c in range(NCH):
            off = c * NT + g * 512
            w = list(src_tok) if (g == 0 and c == 0) else []
            if chunk_toks is not None and g == 0:
                w.append(chunk_toks[c])
            if m2 is not None:
                w.append(m2)
            cpt = p.dve("tensor_copy", xbf, k.x[:, off:off + 512], waits=w,
                        inc="sdve")
            sqt = p.dve("tensor_tensor", sqbf, xbf, xbf, ALU.mult,
                        waits=[cpt], inc="sdve")
            w1 = [cpt]
            if c == 0 and t_m_prev is not None:
                w1.append(t_m_prev)
            m1 = p.mm(k.ps2[0:1, 0:512], k.ones_col_bf[:, :], xbf,
                      start=(c == 0), stop=(c == NCH - 1), waits=w1,
                      inc="spe" if c == NCH - 1 else None)
            w2 = [sqt]
            if c == 0 and t_r_prev is not None:
                w2.append(t_r_prev)
            m2 = p.mm(k.ps3[0:1, 0:512], k.ones_col_bf[:, :], sqbf,
                      start=(c == 0), stop=(c == NCH - 1), waits=w2,
                      inc="spe")
        rw = [x for x in (bm_prev, t_m_prev) if x is not None]
        t_m = p.dve("tensor_scalar_mul", k.rowbuf[0:1, 0:512], k.ps2[0:1, :],
                    1.0 / D, waits=[m1] + rw, inc="sdve")
        t_m_prev = t_m
        t_s = p.dve("tensor_tensor", k.scr[0:1, 1024:1536],
                    k.rowbuf[0:1, 0:512], k.rowbuf[0:1, 0:512], ALU.mult,
                    waits=[t_m] + ([t_r_prev] if t_r_prev is not None else []),
                    inc="sdve")
        t_v = p.dve("scalar_tensor_tensor", k.scr[0:1, 1536:2048],
                    k.ps3[0:1, :], 1.0 / D, k.scr[0:1, 1024:1536],
                    ALU.mult, ALU.subtract, waits=[m2, t_s], inc="sdve")
        t_d = p.act("activation", k.scr[0:1, 1024:1536], k.scr[0:1, 1536:2048],
                    AF.Sqrt, bias=LN_EPS, scale=1.0, waits=[t_v], inc="sact")
        rw = [br_prev] if br_prev is not None else []
        t_r = p.dve("reciprocal", k.rowbuf[0:1, 512:1024],
                    k.scr[0:1, 1024:1536], waits=[t_d] + rw, inc="sdve")
        t_r_prev = t_r
        wb = [t_m] + ([a2] if a2 is not None else [])
        bm = p.mm(k.ps0[:, :], k.ones_row[0:1, 0:128], k.rowbuf[0:1, 0:512],
                  start=True, stop=True, waits=wb, inc="spe")
        br = p.mm(k.ps1[:, :], k.ones_row[0:1, 0:128], k.rowbuf[0:1, 512:1024],
                  start=True, stop=True, waits=[t_r], inc="spe")
        bm_prev, br_prev = bm, br
        tcol_toks = []
        for j in range(4):
            t = g * 4 + j
            w = [t_m] if j == 0 else [tcol_toks[-1]]
            tr1 = p.tr(k.ps2[0:128, 0:1], k.rowbuf[0:1, j * 128:(j + 1) * 128],
                       k.ident[0:1, 0:1], waits=w, inc="spe")
            c1 = p.dve("tensor_copy", k.stat[:, t:t + 1], k.ps2[0:128, 0:1],
                       waits=[tr1], inc="sdve")
            tr2 = p.tr(k.ps3[0:128, 0:1],
                       k.rowbuf[0:1, 512 + j * 128:512 + (j + 1) * 128],
                       k.ident[0:1, 0:1], waits=[t_r, c1], inc="spe")
            c2 = p.dve("tensor_copy", k.stat[:, 16 + t:17 + t],
                       k.ps3[0:128, 0:1], waits=[tr2], inc="sdve")
            tcol_toks.append(c2)
        t_m_prev = tcol_toks[-1]
        t_r_prev = tcol_toks[-1]
        for c in range(NCH):
            off = c * NT + g * 512
            a1 = p.dve("tensor_tensor", k.scr[:, 512:1024],
                       k.x[:, off:off + 512],
                       k.ps0[:, :], ALU.subtract,
                       waits=[bm] + ([a2] if a2 is not None else []),
                       inc="sdve")
            a2 = p.dve("tensor_tensor", k.xn[:, off:off + 512],
                       k.scr[:, 512:1024],
                       k.ps1[:, :], ALU.mult, waits=[br, a1], inc="sdve")
            outs.append(a2)
        if gt_out is not None:
            gt_out.append([outs[-1], tcol_toks[-1]])
    return outs


def e_routing_logits(p, k, t, gatef, negsg_row, mrow_bank, acc_bank,
                     lg_out, extra_w=()):
    trm = p.tr(mrow_bank[0:1, 0:128], k.stat[:, t:t + 1], k.ident[:, :],
               waits=list(extra_w), inc="spe")
    mrow = k.scr[0:1, 384:512]
    cm = p.dve("tensor_copy", mrow, mrow_bank[0:1, 0:128], waits=[trm],
               inc="sdve")
    lt = None
    for c in range(NCH):
        lt = p.mm(acc_bank[:, 0:8],
                  k.x[:, c * NT + t * 128:c * NT + (t + 1) * 128],
                  gatef[:, c * 8:(c + 1) * 8], start=(c == 0), stop=False,
                  waits=list(extra_w))
    lt = p.mm(acc_bank[:, 0:8], mrow, negsg_row, start=False, stop=True,
              waits=[cm], inc="spe")
    lg = p.dve("tensor_single_scalar", lg_out, acc_bank[:, 0:8],
               k.stat[:, 16 + t:17 + t], ALU.mult, waits=[lt], inc="sdve")
    return lg


def e_negsg(p, k, gatef, gtok, bank):
    sg = None
    for c in range(NCH):
        sg = p.mm(bank[0:1, 0:8], k.ones_col[:, :],
                  gatef[:, c * 8:(c + 1) * 8], start=(c == 0),
                  stop=(c == NCH - 1), waits=[gtok], inc="spe")
    out = k.scr[0:1, 368:376]
    t = p.dve("tensor_scalar_mul", out, bank[0:1, 0:8], -1.0, waits=[sg],
              inc="sdve")
    return out, t


def e_l0_counts(p, k, dram, xn_toks, gtoks=None):
    gtw = p.dma("sp", k.gatef[:, :], dram["gate_t"], "dgt")
    gtok = p.tok("dgt")
    negsg, nt = e_negsg(p, k, k.gatef, gtok, k.pq)
    ohfm = k.bufb[:, :].bitcast(F32)[0:8, 2048:2048 + NT]
    oh_toks = []
    lgs = [None] * NTT
    cps = [None] * NTT
    ohs_tm = []
    for t in range(NTT):
        if gtoks is not None:
            w = list(gtoks[t // 4]) + [nt]
            if t >= 1:
                w.append(lgs[t - 1])
            bank = k.pq[:, 512:1024]
        else:
            w = list(xn_toks) + [nt]
            if t >= 2:
                w.append(lgs[t - 2])
            bank = k.ps2 if t % 2 == 0 else k.ps3
        lg = e_routing_logits(p, k, t, k.gatef, negsg, k.pq, bank,
                              k.scr[:, t * 8:t * 8 + 8], extra_w=w)
        lgs[t] = lg
        mx = p.dve("reduce_max", k.route[:, t * 16 + 1:t * 16 + 2],
                   k.scr[:, t * 8:t * 8 + 8], waits=[lg], inc="sdve",
                   axis=AXX)
        nmx = p.dve("reduce_max", k.route[:, t * 16 + 2:t * 16 + 3],
                    k.scr[:, t * 8:t * 8 + 8], waits=[lg], inc="sdve",
                    axis=AXX, negate=True)
        ex = p.act("activation", k.scr2[:, t * 8:t * 8 + 8],
                   k.scr[:, t * 8:t * 8 + 8], AF.Exp,
                   bias=k.route[:, t * 16 + 2:t * 16 + 3], scale=1.0,
                   accum_out=k.route[:, t * 16 + 3:t * 16 + 4],
                   waits=[nmx], inc="sact")
        p.dve("reciprocal", k.route[:, t * 16 + 0:t * 16 + 1],
              k.route[:, t * 16 + 3:t * 16 + 4], waits=[ex], inc="sdve")
        oh = p.dve("tensor_single_scalar", k.scr2[:, t * 8:t * 8 + 8],
                   k.scr[:, t * 8:t * 8 + 8],
                   k.route[:, t * 16 + 1:t * 16 + 2], ALU.is_ge,
                   waits=[mx, ex], inc="sdve")
        ohs_tm.append(oh)
    for t in range(NTT):
        w = [ohs_tm[t]] + ([cps[t - 2]] if t >= 2 else xn_toks)
        tbank = k.ps0 if t % 2 == 0 else k.ps1
        tt = p.tr(tbank[0:8, 0:128], k.scr2[:, t * 8:t * 8 + 8],
                  k.ident[:, :], waits=w, inc="spe")
        cp = p.dve("tensor_copy", ohfm[:, t * 128:(t + 1) * 128],
                   tbank[0:8, 0:128], waits=[tt], inc="sdve")
        cps[t] = cp
        oh_toks.append(cp)
    ct = p.dve("reduce_sum", k.cnt[0:8, 0:1], ohfm[:, :], waits=oh_toks,
               inc="sdve", axis=AXX)
    return ct, oh_toks


NSLOT0 = E0 * C0
NSLOT1 = E1 * C1
HPAIR = [(c, c + 4) for c in range(4)] + [(c + 8, c + 12) for c in range(4)]
ATTN_PERM = [h * 64 + i for pair in HPAIR for h in pair for i in range(64)]


def e_l0_ranks(p, k, dram, ct, oh_toks, counts_tok=None):
    fm = k.bufb[:, :].bitcast(F32)
    ranks, ohfm = fm[0:8, 0:NT], fm[0:8, 2048:2048 + NT]
    tmpA, tmpB = fm[0:8, 4096:4096 + NT], fm[0:8, 6144:6144 + NT]
    if counts_tok is None:
        p.dma("sp", k.cnt[0:8, 8:16], dram["cnts_all"], "dca")
    p.dma("sp", k.cnt[0:8, 2:3], dram["mymask"], "dca")
    p.dma("sp", k.cnt[0:8, 3:4], dram["c_ec0"], "dca")
    catok = p.tok("dca")
    pf = p.mm(k.pq[0:8, 0:1], k.cnt[0:8, 8:16], k.cnt[0:8, 2:3],
              start=True, stop=True, waits=[catok], inc="spe")
    pc = p.dve("tensor_scalar_add", k.cnt[0:8, 1:2], k.pq[0:8, 0:1], -1.0,
               waits=[pf], inc="sdve")
    sc = p.dve("tensor_tensor_scan", ranks, ohfm,
               k.zero8[0:8, 0:1].broadcast_to([8, NT]), 0.0, ALU.add, ALU.add,
               waits=oh_toks + [ct], inc="sdve")
    t1 = p.dve("tensor_scalar_add", tmpA, ranks, k.cnt[0:8, 1:2],
               waits=[sc, pc], inc="sdve")
    t2 = p.dve("tensor_single_scalar", tmpA, tmpA, CAP0 - 0.5, ALU.is_le,
               waits=[t1], inc="sdve")
    t3 = p.dve("tensor_single_scalar", tmpB, ranks, C0 - 0.5, ALU.is_le,
               waits=[sc], inc="sdve")
    t4 = p.dve("tensor_tensor", tmpA, tmpA, tmpB, ALU.mult, waits=[t2, t3],
               inc="sdve")
    t5 = p.dve("tensor_tensor", tmpA, tmpA, ohfm, ALU.mult, waits=[t4],
               inc="sdve")
    t6 = p.dve("tensor_scalar_add", tmpB, ranks, k.cnt[0:8, 3:4],
               waits=[t3, t5], inc="sdve")
    t7 = p.dve("tensor_tensor", tmpB, tmpB, tmpA, ALU.mult, waits=[t6],
               inc="sdve")
    return e_extract_idx(p, k, tmpA, tmpB, [t5, t7], 0, stash_col=11)


def e_extract_idx(p, k, keep_fm, contrib_fm, toks, out_col, stash_col=None,
                  big=BIGIDX):
    idx_toks = []
    e1 = e4 = None
    for t in range(NTT):
        w = list(toks)
        if t >= 1:
            w.append(idx_toks[t - 1])
        mk1 = p.mm(k.ps0[:, 0:1], keep_fm[:, t * 128:(t + 1) * 128],
                   k.ones_col[0:8, 0:1], start=True, stop=True, waits=w,
                   inc="spe")
        mk2 = p.mm(k.ps1[:, 0:1], contrib_fm[:, t * 128:(t + 1) * 128],
                   k.ones_col[0:8, 0:1], start=True, stop=True, waits=w,
                   inc="spe")
        e1 = p.dve("tensor_copy", k.scr[:, 256:257], k.ps0[:, 0:1],
                   waits=[mk1] + ([e4] if e4 else []), inc="sdve")
        e2 = p.dve("scalar_tensor_tensor", k.scr[:, 257:258],
                   k.scr[:, 256:257], -big, k.ps1[:, 0:1],
                   ALU.mult, ALU.add, waits=[mk2, e1], inc="sdve")
        e3 = p.dve("tensor_scalar_add", k.scr[:, 258:259], k.scr[:, 257:258],
                   big, waits=[e2], inc="sdve")
        if stash_col is not None:
            p.dve("tensor_copy",
                  k.route[:, t * 16 + stash_col:t * 16 + stash_col + 1],
                  k.scr[:, 258:259], waits=[e3], inc="sdve")
        e4 = p.dve("tensor_copy", k.ridx[:, out_col + t:out_col + t + 1],
                   k.scr[:, 258:259], waits=[e3], inc="sdve")
        idx_toks.append(e4)
    return idx_toks


def e_dispatch(p, k, disp_dram, nslot, idx_cols, idx_toks, src_toks, zsem,
               scsem, nidx=1):
    zm = p.dve("memset", k.stage[:, 1024:2048], 0.0, waits=src_toks,
               inc="sdve")
    for j in range(nslot // 128):
        p.dma("act", disp_dram[j * 128:(j + 1) * 128, :],
              k.stage[:, 1024:2048], zsem, waits=[zm])
    ztok = p.tok(zsem)
    sc_prev = {}
    sct = None
    for t in range(NTT):
        cps = []
        cp = None
        for c in range(NCH):
            w = []
            if c == 0 and (t, "s") in sc_prev:
                w.append(sc_prev[(t, "s")])
            bank = k.pt0 if c % 2 == 0 else k.pt1
            tt = p.tr(bank[0:128, 0:128],
                      k.xn[:, c * NT + t * 128:c * NT + (t + 1) * 128],
                      k.ident_bf[:, :],
                      waits=w + src_toks + ([cps[c - 2]] if c >= 2 else []),
                      inc="spe")
            cp = p.dve("tensor_copy", k.stage[:, c * 128:(c + 1) * 128],
                       bank[0:128, 0:128], waits=[tt] + w, inc="sdve")
            cps.append(cp)
        for j, col in enumerate(idx_cols):
            sct = p.pool("indirect_dma_start", disp_dram,
                         IndirectOffsetOnAxis(k.ridx[:, col + t:col + t + 1], 0),
                         k.stage[:, 0:1024], None,
                         bounds_check=nslot - 1, oob_is_err=False,
                         waits=[cp, idx_toks[j][t], ztok], inc=scsem, step=16)
        sc_prev[(t + 1, "s")] = sct
    return p.tok(scsem)


def e_ffn_l0_expert(p, k, dram, disp_dram, eout_dram, e, state):
    xw = list(state.get("xw", []))
    for c in range(NCH):
        p.dma_tr("sp", k.bufa[:, c * 512:(c + 1) * 512],
                 disp_dram[e * C0:(e + 1) * C0, c * 128:(c + 1) * 128],
                 "dxe", waits=xw if c == 0 else ())
    xet = p.tok("dxe")
    b1t = p.dma("sp", k.bcols[:, 0:32], dram["b1c"][e], "db1",
                waits=[state["b1w"]] if "b1w" in state else ())
    b1tok = p.tok("db1")
    relu_toks = []
    mmt = None
    for hc in range(32):
        g, sub = divmod(hc, 4)
        slot = g % 2
        if sub == 0:
            wkey = "w1c%d" % slot
            w = [state[wkey]] if wkey in state else []
            p.dma("pool", k.bufa[:, 4096 + slot * 4096:8192 + slot * 4096]
                  .rearrange("p (ci f) -> p ci f", ci=8),
                  dram["w1"][e].rearrange("(ci p) f -> p ci f", p=128)
                  [:, :, g * 512:(g + 1) * 512],
                  "dw1%d" % slot, waits=w)
            state["w1t%d" % slot] = p.tok("dw1%d" % slot)
        wbase = 4096 + slot * 4096
        bank = k.ps0 if hc % 2 == 0 else k.ps1
        for c in range(NCH):
            w = [state["w1t%d" % slot], xet]
            if c == 0 and len(relu_toks) >= 2:
                w.append(relu_toks[-2])
            mmt = p.mm(bank[:, :],
                       k.bufa[:, wbase + c * 512 + sub * 128:
                              wbase + c * 512 + sub * 128 + 128],
                       k.bufa[:, c * 512:(c + 1) * 512],
                       start=(c == 0), stop=(c == NCH - 1), waits=w,
                       inc="spe" if c == NCH - 1 else None)
        state["w1c%d" % slot] = mmt
        w = [mmt, b1tok]
        hk = "hcons%d" % hc
        if hk in state:
            w.append(state.pop(hk))
        rt = p.act("activation", k.xn[:, hc * 512:(hc + 1) * 512],
                   bank[:, :], AF.Relu, bias=k.bcols[:, hc:hc + 1],
                   scale=1.0, waits=w, inc="sact")
        relu_toks.append(rt)
    state["b1w"] = relu_toks[-1]
    b2t = p.dma("pool", k.biasrow[0:1, 0:1024], dram["b2r"][e:e + 1, :],
                "db2", waits=[state["b2w"]] if "b2w" in state else ())
    b2tok = p.tok("db2")
    ydones = []
    for oc in range(2):
        for hg in range(4):
            slot = (oc * 4 + hg) % 2
            wkey = "w2c%d" % slot
            w = [state[wkey]] if wkey in state else []
            p.dma("pool", k.bufb[:, slot * 4096:(slot + 1) * 4096]
                  .rearrange("p (hg f) -> p hg f", hg=8),
                  dram["w2"][e].rearrange("(hh p) f -> p hh f", p=128)
                  [:, hg * 8:(hg + 1) * 8, oc * 512:(oc + 1) * 512],
                  "dw2%d" % slot, waits=w)
            w2tok = p.tok("dw2%d" % slot)
            last = None
            for st in range(4):
                bank = [k.ps2[:, :], k.ps3[:, :], k.pq[:, 0:512],
                        k.pq[:, 512:1024]][st]
                for i in range(8):
                    hc = hg * 8 + i
                    w = [w2tok, relu_toks[hc]]
                    if hg == 0 and i == 0 and "ycons%d" % st in state:
                        w.append(state.pop("ycons%d" % st))
                    last = p.mm(bank,
                                k.xn[:, hc * 512 + st * 128:
                                     hc * 512 + st * 128 + 128],
                                k.bufb[:, slot * 4096 + i * 512:
                                       slot * 4096 + (i + 1) * 512],
                                start=(hg == 0 and i == 0), stop=False,
                                waits=w,
                                inc="spe" if (st == 3 and i == 7) else None)
            state[wkey] = last
        for st in range(4):
            bank = [k.ps2[:, :], k.ps3[:, :], k.pq[:, 0:512],
                    k.pq[:, 512:1024]][st]
            bm = p.mm(bank, k.ones_row_bf[0:1, 0:128],
                      k.biasrow[0:1, oc * 512:(oc + 1) * 512],
                      start=False, stop=True, waits=[b2tok], inc="spe")
            ev = p.dve("tensor_copy",
                       k.bufa[:, 12288 + st * 1024 + oc * 512:
                              12288 + st * 1024 + (oc + 1) * 512],
                       bank, waits=[bm] +
                       ([state.pop("yev%d_%d" % (st, oc))]
                        if "yev%d_%d" % (st, oc) in state else []),
                       inc="sdve")
            state["ycons%d" % st] = ev
            ydones.append(ev)
    state["b2w"] = ydones[-1]
    for hc in range(32):
        state["hcons%d" % hc] = p.tok("spe")
    for st in range(4):
        p.dma("act", eout_dram[e * C0 + st * 128:e * C0 + (st + 1) * 128, :],
              k.bufa[:, 12288 + st * 1024:12288 + (st + 1) * 1024],
              "dye", waits=[ydones[st + 4], ydones[st]])
    ydall = p.tok("dye")
    for st in range(4):
        for oc in range(2):
            state["yev%d_%d" % (st, oc)] = ydall
    state["xw"] = [relu_toks[-1]]
    return p.tok("dye")


def e_l0_experts(p, k, dram, disp_dram, eout_dram, sc_tok, pre):
    guard = pre[-1] if pre else sc_tok
    state = {"xw": [sc_tok] + pre,
             "w1c0": guard, "w1c1": guard,
             "w2c0": guard, "w2c1": guard}
    for hc in range(32):
        state["hcons%d" % hc] = guard
    last = None
    for e in range(E0):
        last = e_ffn_l0_expert(p, k, dram, disp_dram, eout_dram, e, state)
    return last


def e_combine(p, k, eout_dram, nslot, idx_col, idx_toks, wcols, store_tok,
              gsem, extra_scale_tok=()):
    add_toks = []
    gt_prev = None
    sc_prev = None
    for t in range(NTT):
        acc = None
        for j, (wcol, icol) in enumerate(wcols):
            g = p.pool("indirect_dma_start", k.stage[:, 0:1024], None,
                       eout_dram,
                       IndirectOffsetOnAxis(k.ridx[:, icol + t:icol + t + 1], 0),
                       bounds_check=nslot - 1, oob_is_err=False,
                       waits=[store_tok, idx_toks[j][t]] +
                       ([gt_prev] if gt_prev else []) + list(extra_scale_tok),
                       inc=gsem, step=16)
            gtok = p.tok(gsem)
            if j == 0:
                acc = p.dve("tensor_scalar_mul", k.scr[:, 0:1024],
                            k.stage[:, 0:1024],
                            k.route[:, t * 16 + wcol:t * 16 + wcol + 1],
                            waits=[gtok] + ([sc_prev] if sc_prev else []),
                            inc="sdve")
            else:
                acc = p.dve("scalar_tensor_tensor", k.scr[:, 0:1024],
                            k.stage[:, 0:1024],
                            k.route[:, t * 16 + wcol:t * 16 + wcol + 1],
                            k.scr[:, 0:1024], ALU.mult, ALU.add,
                            waits=[gtok, acc], inc="sdve")
            gt_prev = acc
        cb = p.dve("tensor_copy", k.stage[:, 1024:2048], k.scr[:, 0:1024],
                   waits=[acc], inc="sdve")
        sc_prev = cb
        bank_last = [None, None]
        for c in range(NCH):
            bank = k.pt0 if c % 2 == 0 else k.pt1
            w = [cb]
            if bank_last[c % 2] is not None:
                w.append(bank_last[c % 2])
            tt = p.tr(bank[0:128, 0:128],
                      k.stage[:, 1024 + c * 128:1024 + (c + 1) * 128],
                      k.ident_bf[:, :], waits=w, inc="spe")
            at = p.dve("tensor_tensor",
                       k.x[:, c * NT + t * 128:c * NT + (t + 1) * 128],
                       k.x[:, c * NT + t * 128:c * NT + (t + 1) * 128],
                       bank[0:128, 0:128], ALU.add, waits=[tt], inc="sdve")
            bank_last[c % 2] = at
            add_toks.append(at)
    return add_toks


def declare_l1_inputs(nc, with_x=True):
    d = {}
    if with_x:
        d["x1c"] = nc.dram_tensor("x1c", [8, D, 256], F32,
                                  kind="ExternalInput").ap()
    d["rw_t"] = nc.dram_tensor("rw_t", [128, 64], F32,
                               kind="ExternalInput").ap()
    d["l1g_c"] = nc.dram_tensor("l1g_c", [128, 8], F32,
                                kind="ExternalInput").ap()
    d["l1b_c"] = nc.dram_tensor("l1b_c", [128, 8], F32,
                                kind="ExternalInput").ap()
    for nm in ("wq1", "wk1", "wv1", "wo1"):
        d[nm] = nc.dram_tensor(nm, [D, D], F32, kind="ExternalInput").ap()
    for nm in ("bq1r", "bk1r", "bv1r", "bo1r"):
        d[nm] = nc.dram_tensor(nm, [1, D], F32, kind="ExternalInput").ap()
    d["wg1"] = nc.dram_tensor("wg1", [E1, D, DFF1], F32,
                              kind="ExternalInput").ap()
    d["wu1"] = nc.dram_tensor("wu1", [E1, D, DFF1], F32,
                              kind="ExternalInput").ap()
    d["bg1r"] = nc.dram_tensor("bg1r", [E1, DFF1], F32,
                               kind="ExternalInput").ap()
    d["bu1r"] = nc.dram_tensor("bu1r", [E1, DFF1], F32,
                               kind="ExternalInput").ap()
    d["wd1"] = nc.dram_tensor("wd1", [E1, DFF1, D], F32,
                              kind="ExternalInput").ap()
    d["bd1r"] = nc.dram_tensor("bd1r", [E1, D], F32,
                               kind="ExternalInput").ap()
    d["c_ec1"] = nc.dram_tensor("c_ec1", [8, 1], F32,
                                kind="ExternalInput").ap()
    return d


def e_load_x1(p, k, src):
    for c in range(NCH):
        for s in range(8):
            p.dma("sp", k.x[:, c * NT + s * 256:c * NT + s * 256 + 256],
                  src[s, c * 128:(c + 1) * 128, :], "dx")
    return [p.tok("dx")]


def e_ln2_stats(p, k, toks):
    sqbf = k.bufa[:, 0:512]
    m1 = m2 = None
    t_m_prev = t_r_prev = None
    out = []
    for g in range(4):
        for c in range(NCH):
            off = c * NT + g * 512
            w = list(toks) if (g == 0 and c == 0) else []
            if m2 is not None:
                w.append(m2)
            sqt = p.dve("tensor_tensor", sqbf, k.xn[:, off:off + 512],
                        k.xn[:, off:off + 512], ALU.mult, waits=w, inc="sdve")
            w1 = []
            if c == 0 and t_m_prev is not None:
                w1.append(t_m_prev)
            m1 = p.mm(k.ps2[0:1, 0:512], k.ones_col_bf[:, :],
                      k.xn[:, off:off + 512], start=(c == 0),
                      stop=(c == NCH - 1), waits=w1 + list(toks),
                      inc="spe" if c == NCH - 1 else None)
            w2 = [sqt]
            if c == 0 and t_r_prev is not None:
                w2.append(t_r_prev)
            m2 = p.mm(k.ps3[0:1, 0:512], k.ones_col_bf[:, :], sqbf,
                      start=(c == 0), stop=(c == NCH - 1), waits=w2,
                      inc="spe")
        t_m = p.dve("tensor_scalar_mul", k.rowbuf[0:1, 0:512], k.ps2[0:1, :],
                    1.0 / D, waits=[m1] + ([t_m_prev] if t_m_prev else []),
                    inc="sdve")
        t_s = p.dve("tensor_tensor", k.scr[0:1, 1024:1536],
                    k.rowbuf[0:1, 0:512], k.rowbuf[0:1, 0:512], ALU.mult,
                    waits=[t_m], inc="sdve")
        t_v = p.dve("scalar_tensor_tensor", k.scr[0:1, 1536:2048],
                    k.ps3[0:1, :], 1.0 / D, k.scr[0:1, 1024:1536],
                    ALU.mult, ALU.subtract, waits=[m2, t_s], inc="sdve")
        t_d = p.act("activation", k.scr[0:1, 1024:1536], k.scr[0:1, 1536:2048],
                    AF.Sqrt, bias=LN_EPS, scale=1.0, waits=[t_v], inc="sact")
        t_r = p.dve("reciprocal", k.rowbuf[0:1, 512:1024],
                    k.scr[0:1, 1024:1536], waits=[t_d], inc="sdve")
        tc = []
        for jj in range(4):
            t = g * 4 + jj
            w = [t_m] if jj == 0 else [tc[-1]]
            tr1 = p.tr(k.ps2[0:128, 0:1], k.rowbuf[0:1, jj * 128:(jj + 1) * 128],
                       k.ident[0:1, 0:1], waits=w, inc="spe")
            c1 = p.dve("tensor_copy", k.stat[:, 32 + t:33 + t],
                       k.ps2[0:128, 0:1], waits=[tr1], inc="sdve")
            tr2 = p.tr(k.ps3[0:128, 0:1],
                       k.rowbuf[0:1, 512 + jj * 128:512 + (jj + 1) * 128],
                       k.ident[0:1, 0:1], waits=[t_r, c1], inc="spe")
            c2 = p.dve("tensor_copy", k.stat[:, 48 + t:49 + t],
                       k.ps3[0:128, 0:1], waits=[tr2], inc="sdve")
            tc.append(c2)
        t_m_prev = t_r_prev = tc[-1]
        out.append(tc[-1])
    return out


def e_l1_routing(p, k, dram, xn_toks, gtoks=None):
    gtw = p.dma("sp", k.gatef[:, :], dram["rw_t"], "dgt2")
    p.dma("sp", k.cnt[0:8, 4:5], dram["c_ec1"], "dgt2")
    gtok = p.tok("dgt2")
    negsg, nt = e_negsg(p, k, k.gatef, gtok, k.pq)
    fm = k.bufb[:, :].bitcast(F32)
    oh1f, oh2f = fm[0:8, 0:NT], fm[0:8, 2048:2048 + NT]
    keptf, rankf = fm[0:8, 4096:4096 + NT], fm[0:8, 6144:6144 + NT]
    t1s = []
    t2s = []
    lgs = [None] * NTT
    oh12 = []
    for t in range(NTT):
        if gtoks is not None:
            w = list(gtoks[t // 4]) + [nt]
            if t >= 1:
                w.append(lgs[t - 1])
            bank = k.pq[:, 512:1024]
        else:
            w = list(xn_toks) + [nt]
            if t >= 2:
                w.append(lgs[t - 2])
            bank = k.ps2 if t % 2 == 0 else k.ps3
        lg = e_routing_logits(p, k, t, k.gatef, negsg, k.pq, bank,
                              k.scr[:, t * 8:t * 8 + 8], extra_w=w)
        mx = p.dve("reduce_max", k.route[:, t * 16 + 4:t * 16 + 5],
                   k.scr[:, t * 8:t * 8 + 8], waits=[lg], inc="sdve",
                   axis=AXX)
        nmx = p.dve("reduce_max", k.route[:, t * 16 + 5:t * 16 + 6],
                    k.scr[:, t * 8:t * 8 + 8], waits=[lg], inc="sdve",
                    axis=AXX, negate=True)
        ex = p.act("activation", k.scr2[:, t * 8:t * 8 + 8],
                   k.scr[:, t * 8:t * 8 + 8], AF.Exp,
                   bias=k.route[:, t * 16 + 5:t * 16 + 6], scale=1.0,
                   accum_out=k.route[:, t * 16 + 6:t * 16 + 7],
                   waits=[nmx], inc="sact")
        rs = p.dve("reciprocal", k.route[:, t * 16 + 7:t * 16 + 8],
                   k.route[:, t * 16 + 6:t * 16 + 7], waits=[ex], inc="sdve")
        pr = p.dve("tensor_scalar_mul", k.scr2[:, 128 + t * 8:136 + t * 8],
                   k.scr2[:, t * 8:t * 8 + 8],
                   k.route[:, t * 16 + 7:t * 16 + 8], waits=[ex, rs],
                   inc="sdve")
        oh1 = p.dve("tensor_single_scalar", k.scr2[:, t * 8:t * 8 + 8],
                    k.scr[:, t * 8:t * 8 + 8],
                    k.route[:, t * 16 + 4:t * 16 + 5], ALU.is_ge,
                    waits=[mx, pr], inc="sdve")
        p2t = p.dve("tensor_tensor", k.scr[:, 128 + t * 8:136 + t * 8],
                    k.scr2[:, 128 + t * 8:136 + t * 8],
                    k.scr2[:, t * 8:t * 8 + 8], ALU.subtract,
                    waits=[pr, oh1], inc="sdve")
        p2c = p.dve("reduce_max", k.route[:, t * 16 + 8:t * 16 + 9],
                    k.scr[:, 128 + t * 8:136 + t * 8], waits=[p2t],
                    inc="sdve", axis=AXX)
        oh2 = p.dve("tensor_single_scalar", k.scr[:, 128 + t * 8:136 + t * 8],
                    k.scr[:, 128 + t * 8:136 + t * 8],
                    k.route[:, t * 16 + 8:t * 16 + 9], ALU.is_ge,
                    waits=[p2c], inc="sdve")
        s12 = p.dve("tensor_tensor", k.route[:, t * 16 + 9:t * 16 + 10],
                    k.route[:, t * 16 + 7:t * 16 + 8],
                    k.route[:, t * 16 + 8:t * 16 + 9], ALU.add,
                    waits=[rs, p2c], inc="sdve")
        s12b = p.dve("tensor_scalar_add", k.route[:, t * 16 + 9:t * 16 + 10],
                     k.route[:, t * 16 + 9:t * 16 + 10], 1e-9, waits=[s12],
                     inc="sdve")
        rr = p.dve("reciprocal", k.route[:, t * 16 + 10:t * 16 + 11],
                   k.route[:, t * 16 + 9:t * 16 + 10], waits=[s12b],
                   inc="sdve")
        p.dve("tensor_tensor", k.route[:, t * 16 + 1:t * 16 + 2],
              k.route[:, t * 16 + 7:t * 16 + 8],
              k.route[:, t * 16 + 10:t * 16 + 11], ALU.mult,
              waits=[rr], inc="sdve")
        p.dve("tensor_tensor", k.route[:, t * 16 + 2:t * 16 + 3],
              k.route[:, t * 16 + 8:t * 16 + 9],
              k.route[:, t * 16 + 10:t * 16 + 11], ALU.mult,
              waits=[rr], inc="sdve")
        lgs[t] = p.tok("sdve")
        oh12.append((oh1, oh2))
    for t in range(NTT):
        oh1, oh2 = oh12[t]
        tb1 = k.ps0 if t % 2 == 0 else k.ps1
        w = [oh1] + ([t1s[t - 2]] if t >= 2 else xn_toks)
        tt1 = p.tr(tb1[0:8, 0:128], k.scr2[:, t * 8:t * 8 + 8], k.ident[:, :],
                   waits=w, inc="spe")
        cp1 = p.dve("tensor_copy", oh1f[:, t * 128:(t + 1) * 128],
                    tb1[0:8, 0:128], waits=[tt1], inc="sdve")
        t1s.append(cp1)
        w = [oh2, cp1]
        tt2 = p.tr(tb1[0:8, 0:128], k.scr[:, 128 + t * 8:136 + t * 8],
                   k.ident[:, :], waits=w, inc="spe")
        cp2 = p.dve("tensor_copy", oh2f[:, t * 128:(t + 1) * 128],
                    tb1[0:8, 0:128], waits=[tt2], inc="sdve")
        t2s.append(cp2)
    sel = p.dve("tensor_tensor", keptf, oh1f, oh2f, ALU.add,
                waits=t1s + t2s, inc="sdve")
    sc = p.dve("tensor_tensor_scan", rankf, keptf,
               k.zero8[0:8, 0:1].broadcast_to([8, NT]), 0.0, ALU.add, ALU.add,
               waits=[sel], inc="sdve")
    kp = p.dve("tensor_single_scalar", keptf, rankf, C1 - 0.5, ALU.is_le,
               waits=[sc], inc="sdve")
    slc = p.dve("tensor_scalar_add", rankf, rankf, k.cnt[0:8, 4:5],
                waits=[sc, gtok, kp], inc="sdve")
    a1 = p.dve("tensor_tensor", oh1f, oh1f, keptf, ALU.mult, waits=[kp, slc],
               inc="sdve")
    a2 = p.dve("tensor_tensor", oh2f, oh2f, keptf, ALU.mult, waits=[kp],
               inc="sdve")
    c1 = p.dve("tensor_tensor", keptf, oh1f, rankf, ALU.mult,
               waits=[a1, a2, slc], inc="sdve")
    i1 = e_extract_idx(p, k, oh1f, keptf, [a1, c1], 16, stash_col=12)
    c2 = p.dve("tensor_tensor", keptf, oh2f, rankf, ALU.mult,
               waits=[c1] + i1, inc="sdve")
    i2 = e_extract_idx(p, k, oh2f, keptf, [a2, c2], 32, stash_col=13)
    return i1, i2


def e_l1_attn(p, k, dram, xn_toks, st2_toks, pre):
    xadd_toks = []
    xn_used = []
    aprev = None
    for c in range(NCH):
        for g in range(4):
            off = c * NT + g * 512
            aprev = p.pool("tensor_tensor", k.x[:, off:off + 512],
                           k.x[:, off:off + 512], k.xn[:, off:off + 512],
                           ALU.add, waits=list(pre) + xn_toks + st2_toks,
                           inc="spool")
            xadd_toks.append(aprev)
    glb = p.dma("sp", k.bcols[:, 32:40], dram["l1g_c"], "dlg", waits=pre)
    p.dma("sp", k.bcols[:, 40:48], dram["l1b_c"], "dlg")
    p.dma("pool", k.biasrow[0:1, 0:1024], dram["bq1r"], "dlb", waits=pre)
    p.dma("pool", k.biasrow[0:1, 1024:2048], dram["bk1r"], "dlb")
    p.dma("pool", k.biasrow2[0:1, 0:1024], dram["bv1r"], "dlb")
    p.dma("pool", k.biasrow2[0:1, 1024:2048], dram["bo1r"], "dlb")
    lgtok = p.tok("dlg")
    lbtok = p.tok("dlb")
    w_state = {}
    kv_done = {}
    ln_done = {}
    out_done = {}
    for half in range(2):
        ln_toks = []
        lprev = None
        for s in range(8):
            toff = s * 256 + half * 128
            ti = (s * 256 + half * 128) // 128
            w = [st2_toks[-1], lgtok] + list(pre)
            if lprev is not None:
                w.append(lprev)
            trm = p.tr(k.pq[0:1, 0:128], k.stat[:, 32 + ti:33 + ti],
                       k.ident[:, :], waits=w + (out_done.get(half - 1, [])),
                       inc="spe")
            trr = p.tr(k.pq[0:1, 512:640], k.stat[:, 48 + ti:49 + ti],
                       k.ident[:, :], waits=[trm], inc="spe")
            cm = p.dve("tensor_copy", k.rowbuf[0:1, 0:128], k.pq[0:1, 0:128],
                       waits=[trm, lprev], inc="sdve")
            cr = p.dve("tensor_copy", k.rowbuf[0:1, 128:256],
                       k.pq[0:1, 512:640], waits=[trr], inc="sdve")
            bm = p.mm(k.ps0[:, 0:128], k.ones_row[0:1, 0:128],
                      k.rowbuf[0:1, 0:128], start=True, stop=True,
                      waits=[cm], inc="spe")
            br = p.mm(k.ps1[:, 0:128], k.ones_row[0:1, 0:128],
                      k.rowbuf[0:1, 128:256], start=True, stop=True,
                      waits=[cr], inc="spe")
            xnap = k.xn[:, :].rearrange("p (c n) -> p c n", c=NCH)[
                :, :, toff:toff + 128]
            l1 = p.dve("tensor_tensor", k.scr[:, 0:1024].rearrange(
                "p (c n) -> p c n", c=8), xnap,
                k.ps0[:, 0:128].unsqueeze(1).broadcast_to([128, 8, 128]),
                ALU.subtract, waits=[bm] + xn_toks, inc="sdve")
            l2 = p.dve("tensor_tensor", k.scr[:, 0:1024].rearrange(
                "p (c n) -> p c n", c=8),
                k.scr[:, 0:1024].rearrange("p (c n) -> p c n", c=8),
                k.ps1[:, 0:128].unsqueeze(1).broadcast_to([128, 8, 128]),
                ALU.mult, waits=[br, l1], inc="sdve")
            l3 = p.dve("tensor_tensor", k.scr[:, 0:1024].rearrange(
                "p (c n) -> p c n", c=8),
                k.scr[:, 0:1024].rearrange("p (c n) -> p c n", c=8),
                k.bcols[:, 32:40].unsqueeze(2).broadcast_to([128, 8, 128]),
                ALU.mult, waits=[l2], inc="sdve")
            l4 = p.dve("tensor_tensor", k.bufb[:, s * 1024:(s + 1) * 1024]
                       .rearrange("p (c n) -> p c n", c=8),
                       k.scr[:, 0:1024].rearrange("p (c n) -> p c n", c=8),
                       k.bcols[:, 40:48].unsqueeze(2).broadcast_to(
                           [128, 8, 128]),
                       ALU.add, waits=[l3], inc="sdve")
            lprev = l4
            ln_toks.append(l4)
        ln_done[half] = ln_toks
        kv_toks = []
        for wi, wname, brow in ((0, "wk1", k.biasrow[0:1, 1024:2048]),
                                (1, "wv1", k.biasrow2[0:1, 0:1024])):
            for oc in range(2):
                slot = (wi * 2 + oc) % 2
                wkey = "wkv%d" % slot
                w = [w_state[wkey]] if wkey in w_state else list(pre)
                p.dma("pool", k.bufb[:, 8192 + slot * 4096:
                                     8192 + (slot + 1) * 4096]
                      .rearrange("p (ci f) -> p ci f", ci=8),
                      dram[wname].rearrange("(ci p) f -> p ci f", p=128)
                      [:, :, oc * 512:(oc + 1) * 512], "dwkv%d" % slot,
                      waits=w)
                wt = p.tok("dwkv%d" % slot)
                mmt = None
                ev = None
                for s in range(8):
                    bank = k.ps2 if s % 2 == 0 else k.ps3
                    for ci in range(NCH):
                        w = [wt, ln_toks[s]]
                        if ci == 0 and len(kv_toks) >= 2:
                            w.append(kv_toks[-2])
                        mmt = p.mm(bank[:, :],
                                   k.bufb[:, s * 1024 + ci * 128:
                                          s * 1024 + (ci + 1) * 128],
                                   k.bufb[:, 8192 + slot * 4096 + ci * 512:
                                          8192 + slot * 4096 + (ci + 1) * 512],
                                   start=(ci == 0), stop=False, waits=w)
                    mmt = p.mm(bank[:, :], k.ones_row_bf[0:1, 0:128],
                               brow[0:1, oc * 512:(oc + 1) * 512], start=False,
                               stop=True, waits=[lbtok], inc="spe")
                    ev = p.act("copy",
                               k.bufa[:, wi * 8192 + s * 1024 + oc * 512:
                                      wi * 8192 + s * 1024 + oc * 512 + 512],
                               bank[:, :], waits=[mmt] + list(pre) + st2_toks,
                               inc="sact")
                    kv_toks.append(ev)
                w_state[wkey] = ev
        wq_w = [w_state.get("wkv0"), w_state.get("wkv1")]
        for i in range(2):
            p.dma("pool", k.bufb[:, 8192 + i * 4096:8192 + (i + 1) * 4096]
                  .rearrange("p (ci f) -> p ci f", ci=8),
                  dram["wq1"].rearrange("(ci p) f -> p ci f", p=128)
                  [:, :, i * 512:(i + 1) * 512], "dwq", waits=[wq_w[i]])
        wqtok = p.tok("dwq")
        qev_prev = None
        s8_prev = None
        outs = []
        blk_guard = [None, None]
        pend = None
        for s in range(8):
            qmt = None
            for oc in range(2):
                bank = k.pq[:, oc * 512:(oc + 1) * 512]
                for ci in range(NCH):
                    w = [wqtok, ln_toks[s]]
                    if ci == 0 and qev_prev is not None:
                        w.append(qev_prev)
                    qmt = p.mm(bank,
                               k.bufb[:, s * 1024 + ci * 128:
                                      s * 1024 + (ci + 1) * 128],
                               k.bufb[:, 8192 + oc * 4096 + ci * 512:
                                      8192 + oc * 4096 + (ci + 1) * 512],
                               start=(ci == 0), stop=False, waits=w)
                qmt = p.mm(bank, k.ones_row_bf[0:1, 0:128],
                           k.biasrow[0:1, oc * 512:(oc + 1) * 512],
                           start=False, stop=True, waits=[lbtok], inc="spe")
            qev = p.act("copy", k.stage[:, 0:1024], k.pq[:, :],
                        waits=[qmt] + ([s8_prev] if s8_prev else []),
                        inc="sact")
            qev_prev = qev
            s8c = []
            for hg in range(2):
                prod = p.dve(
                    "tensor_tensor", k.scr[:, 0:2048].bitcast(BF16),
                    k.stage[:, 0:1024].rearrange("p (h d) -> p h d", h=16)
                    [:, hg * 8:(hg + 1) * 8, :].unsqueeze(2)
                    .broadcast_to([128, 8, 8, 64]),
                    k.bufa[:, 0:8192].rearrange("p (t h d) -> p h t d", t=8,
                                                h=16)
                    [:, hg * 8:(hg + 1) * 8, :, :],
                    ALU.mult, waits=[qev] + kv_toks[-1:] + s8c, inc="sdve")
                red = p.dve("tensor_reduce",
                            k.scr2[:, s % 2 * 512 + hg * 64:
                                   s % 2 * 512 + hg * 64 + 64]
                            .rearrange("p (h t) -> p h t", h=8),
                            k.scr[:, 0:2048].bitcast(BF16)
                            .rearrange("p (h t d) -> p h t d", h=8, t=8),
                            AXX, ALU.add,
                            waits=[prod] +
                            ([blk_guard[s % 2]] if blk_guard[s % 2] else []),
                            inc="sdve")
                s8c.append(red)
            s8_prev = s8c[-1]
            sb = k.scr2[:, s % 2 * 512:s % 2 * 512 + 128]
            exd = p.act("activation", sb, sb, AF.Exp, scale=0.125,
                        waits=s8c, inc="sact")
            if pend is not None:
                pend()
                pend = None

            def mk_pend(s=s, sb=sb, exd=exd, qev=qev):
                def fin():
                    smr = p.dve("tensor_reduce",
                                k.scr2[:, s % 2 * 512 + 144:
                                       s % 2 * 512 + 160],
                                sb.rearrange("p (h t) -> p h t", h=16), AXX,
                                ALU.add, waits=[exd], inc="sdve")
                    rcr = p.dve("reciprocal",
                                k.scr2[:, s % 2 * 512 + 160:
                                       s % 2 * 512 + 176],
                                k.scr2[:, s % 2 * 512 + 144:
                                       s % 2 * 512 + 160],
                                waits=[smr], inc="sdve")
                    aw = p.dve("tensor_tensor", sb, sb,
                               k.scr2[:, s % 2 * 512 + 160:
                                      s % 2 * 512 + 176]
                               .rearrange("p (h o) -> p h o", o=1)
                               .broadcast_to([128, 16, 8]), ALU.mult,
                               waits=[rcr], inc="sdve")
                    avs = []
                    for hg in range(2):
                        prod = p.dve(
                            "tensor_tensor",
                            k.rope[:, 0:4096]
                            .rearrange("p (h d t) -> p h d t", h=8, d=64),
                            k.bufa[:, 8192:16384]
                            .rearrange("p (t h d) -> p h d t", t=8, h=16)
                            [:, hg * 8:(hg + 1) * 8, :, :],
                            sb.rearrange("p (h t) -> p h t", h=16)
                            [:, hg * 8:(hg + 1) * 8, :].unsqueeze(2)
                            .broadcast_to([128, 8, 64, 8]),
                            ALU.mult, waits=[aw] + avs, inc="sdve")
                        red = p.dve("tensor_reduce",
                                    k.bufb[:, s * 1024 + hg * 512:
                                           s * 1024 + (hg + 1) * 512]
                                    .rearrange("p (h d) -> p h d", h=8),
                                    k.rope[:, 0:4096]
                                    .rearrange("p (h d t) -> p h d t", h=8,
                                               d=64),
                                    AXX, ALU.add,
                                    waits=[prod, ln_toks[s], qev],
                                    inc="sdve", lowp=True)
                        avs.append(red)
                    blk_guard[s % 2] = avs[-1]
                    outs.append(avs)
                return fin
            pend = mk_pend()
        pend()
        pend = None
        av_prev = outs[-1][-1]
        for i in range(2):
            p.dma("pool", k.bufb[:, 8192 + i * 4096:8192 + (i + 1) * 4096]
                  .rearrange("p (ci f) -> p ci f", ci=8),
                  dram["wo1"].rearrange("(ci p) f -> p ci f", p=128)
                  [:, :, i * 512:(i + 1) * 512], "dwo1",
                  waits=[qev_prev, outs[-1][-1]])
        wotok = p.tok("dwo1")
        trp = [None, None]
        addp = [None, None]
        od = []
        for s in range(8):
            cps = []
            for c in range(NCH):
                bank = k.pt0 if c % 2 == 0 else k.pt1
                w = list(outs[s])
                if trp[c % 2] is not None:
                    w.append(trp[c % 2])
                tt = p.tr(bank[0:128, 0:128],
                          k.bufb[:, s * 1024 + c * 128:s * 1024 + (c + 1) * 128],
                          k.ident_bf[:, :], waits=w, inc="spe")
                cp = p.dve("tensor_copy",
                           k.stage[:, 1024 + c * 128:1024 + (c + 1) * 128],
                           bank[0:128, 0:128],
                           waits=[tt] + ([cps[-1]] if cps else
                                         ([addp[0]] if addp[0] else [])),
                           inc="sdve")
                trp[c % 2] = cp
                cps.append(cp)
            for oc in range(NCH):
                bank = k.ps2 if oc % 2 == 0 else k.ps3
                mmt = None
                for ci in range(NCH):
                    w = [wotok, cps[-1]]
                    if ci == 0 and addp[oc % 2] is not None:
                        w.append(addp[oc % 2])
                    mmt = p.mm(bank[:, 0:128],
                               k.bufb[:, 8192 + (oc // 4) * 4096 +
                                      ci * 512 + (oc % 4) * 128:
                                      8192 + (oc // 4) * 4096 +
                                      ci * 512 + (oc % 4) * 128 + 128],
                               k.stage[:, 1024 + ci * 128:1024 + (ci + 1) * 128],
                               start=(ci == 0), stop=False, waits=w)
                mmt = p.mm(bank[:, 0:128],
                           k.biasrow2[0:1, 1024 + oc * 128:1024 + (oc + 1) * 128],
                           k.ones_row_bf[0:1, 0:128], start=False, stop=True,
                           waits=[lbtok], inc="spe")
                toff = oc * NT + s * 256 + half * 128
                at = p.dve("tensor_tensor", k.x[:, toff:toff + 128],
                           k.x[:, toff:toff + 128], bank[:, 0:128], ALU.add,
                           waits=[mmt] + xadd_toks[-1:], inc="sdve")
                addp[oc % 2] = at
                od.append(at)
                xadd_toks.append(at)
        out_done[half] = od[-2:]
        for i in range(2):
            w_state["wkv%d" % i] = od[-1]
    return xadd_toks


def e_ffn_l1_expert(p, k, dram, disp_dram, eout_dram, e, state):
    xw = list(state.get("xw", []))
    for c in range(NCH):
        p.dma_tr("sp", k.bufa[:, c * 1024:(c + 1) * 1024],
                 disp_dram[e * C1:(e + 1) * C1, c * 128:(c + 1) * 128],
                 "dxe1", waits=xw if c == 0 else ())
    xet = p.tok("dxe1")
    p.dma("pool", k.biasrow[0:1, 0:2048], dram["bg1r"][e:e + 1, :], "db1g",
          waits=[state["bgw"]] if "bgw" in state else ())
    p.dma("pool", k.biasrow2[0:1, 0:2048], dram["bu1r"][e:e + 1, :], "db1g")
    btok = p.tok("db1g")
    h_toks = []
    for gf in range(16):
        slot = gf % 2
        wg = [state["wgc%d" % slot]] if "wgc%d" % slot in state else []
        p.dma("pool", k.bufa[:, 8192 + slot * 2048:8192 + slot * 2048 + 1024]
              .rearrange("p (ci f) -> p ci f", ci=8),
              dram["wg1"][e].rearrange("(ci p) f -> p ci f", p=128)
              [:, :, gf * 128:(gf + 1) * 128], "dwg%d" % slot, waits=wg)
        p.dma("pool", k.bufa[:, 8192 + slot * 2048 + 1024:
                             8192 + slot * 2048 + 2048]
              .rearrange("p (ci f) -> p ci f", ci=8),
              dram["wu1"][e].rearrange("(ci p) f -> p ci f", p=128)
              [:, :, gf * 128:(gf + 1) * 128], "dwg%d" % slot)
        wt = p.tok("dwg%d" % slot)
        last = None
        for sh in range(2):
            bg = k.ps2 if sh == 0 else k.ps3
            bu = k.pq[:, 0:512] if sh == 0 else k.pq[:, 512:1024]
            m1 = m2 = None
            for ci in range(NCH):
                w = [wt, xet]
                if ci == 0 and len(h_toks) >= 2:
                    w.append(h_toks[-2])
                m1 = p.mm(bg[:, :],
                          k.bufa[:, 8192 + slot * 2048 + ci * 128:
                                 8192 + slot * 2048 + (ci + 1) * 128],
                          k.bufa[:, ci * 1024 + sh * 512:
                                 ci * 1024 + (sh + 1) * 512],
                          start=(ci == 0), stop=False, waits=w)
                m2 = p.mm(bu,
                          k.bufa[:, 8192 + slot * 2048 + 1024 + ci * 128:
                                 8192 + slot * 2048 + 1024 + (ci + 1) * 128],
                          k.bufa[:, ci * 1024 + sh * 512:
                                 ci * 1024 + (sh + 1) * 512],
                          start=(ci == 0), stop=False, waits=w)
            m1 = p.mm(bg[:, :], k.biasrow[0:1, gf * 128:(gf + 1) * 128],
                      k.ones_row_bf[0:1, 0:512], start=False, stop=True,
                      waits=[btok], inc="spe")
            m2 = p.mm(bu, k.biasrow2[0:1, gf * 128:(gf + 1) * 128],
                      k.ones_row_bf[0:1, 0:512], start=False, stop=True,
                      waits=[btok], inc="spe")
            hk = "hc%d_%d" % (gf, sh)
            sw = [m1] + ([state.pop("sil%d" % sh)]
                         if "sil%d" % sh in state else [])
            sg = p.act("activation", k.stage[:, sh * 512:(sh + 1) * 512],
                       bg[:, :], AF.Silu, waits=sw, inc="sact")
            state["sil%d" % sh] = sg
            hw = [sg, m2]
            if hk in state:
                hw.append(state.pop(hk))
            ht = p.dve("tensor_tensor",
                       k.xn[:, gf * 1024 + sh * 512:gf * 1024 + (sh + 1) * 512],
                       k.stage[:, sh * 512:(sh + 1) * 512], bu, ALU.mult,
                       waits=hw, inc="sdve")
            h_toks.append(ht)
        state["wgc%d" % slot] = h_toks[-1]
    state["bgw"] = h_toks[-1]
    p.dma("pool", k.biasrow[0:1, 0:1024], dram["bd1r"][e:e + 1, :], "db1d",
          waits=[h_toks[-1]])
    bdt = p.tok("db1d")
    ydones = {}
    wlast = None
    for oc in range(2):
        w = [wlast] if wlast is not None else \
            ([state["wdc"]] if "wdc" in state else [])
        for gg in range(2):
            p.dma("pool", k.bufb[:, gg * 4096:(gg + 1) * 4096]
                  .rearrange("p (gr f) -> p gr f", gr=8),
                  dram["wd1"][e].rearrange("(gh p) f -> p gh f", p=128)
                  [:, gg * 8:(gg + 1) * 8, oc * 512:(oc + 1) * 512],
                  "dwd%d" % gg, waits=w)
        wdt = [p.tok("dwd0"), p.tok("dwd1")]
        for wave in range(2):
            for sq in range(4):
                st = wave * 4 + sq
                bank = [k.ps2[:, :], k.ps3[:, :], k.pq[:, 0:512],
                        k.pq[:, 512:1024]][sq]
                for i in range(16):
                    w2 = wdt + [h_toks[2 * i], h_toks[2 * i + 1]]
                    if i == 0 and "yc%d" % sq in state:
                        w2.append(state.pop("yc%d" % sq))
                    wlast = p.mm(bank,
                                 k.xn[:, i * 1024 + st * 128:
                                      i * 1024 + st * 128 + 128],
                                 k.bufb[:, (i // 8) * 4096 + (i % 8) * 512:
                                        (i // 8) * 4096 + (i % 8) * 512 + 512],
                                 start=(i == 0), stop=False, waits=w2)
                bm = p.mm(bank, k.ones_row_bf[0:1, 0:128],
                          k.biasrow[0:1, oc * 512:(oc + 1) * 512],
                          start=False, stop=True, waits=[bdt], inc="spe")
                wlast = bm
                ev = p.dve("tensor_copy",
                           k.bufa[:, st * 1024 + oc * 512:
                                  st * 1024 + oc * 512 + 512],
                           bank, waits=[bm] +
                           ([state.pop("ye%d_%d" % (st, oc))]
                            if "ye%d_%d" % (st, oc) in state else []),
                           inc="sdve")
                state["yc%d" % sq] = ev
                ydones[(st, oc)] = ev
    state["wdc"] = wlast
    for st in range(8):
        p.dma("act", eout_dram[e * C1 + st * 128:e * C1 + (st + 1) * 128, :],
              k.bufa[:, st * 1024:(st + 1) * 1024], "dye1",
              waits=[ydones[(st, 0)], ydones[(st, 1)]])
    ydall = p.tok("dye1")
    for st in range(8):
        for oc in range(2):
            state["ye%d_%d" % (st, oc)] = ydall
    state["xw"] = [h_toks[-1], ydall]
    state["bgw"] = ydall
    return ydall


def e_l1_experts(p, k, dram, disp_dram, eout_dram, sc_tok, pre):
    guard = pre[-1] if pre else sc_tok
    state = {"xw": [sc_tok] + list(pre),
             "wgc0": guard, "wgc1": guard, "wdc": guard}
    for gf in range(16):
        for sh in range(2):
            state["hc%d_%d" % (gf, sh)] = guard
    last = None
    for e in range(E1):
        last = e_ffn_l1_expert(p, k, dram, disp_dram, eout_dram, e, state)
    return last


def build_progC1():
    nc = bass.Bass("TRN2", target_bir_lowering=False, debug=False)
    dram = declare_const_inputs(nc)
    dram.update(declare_l1_inputs(nc))
    xpartT = nc.dram_tensor("xpartT", [D, NT], F32, kind="ExternalOutput").ap()
    ridx_o = nc.dram_tensor("ridx_o", [128, 64], I32, kind="ExternalOutput").ap()
    route_o = nc.dram_tensor("route_o", [128, 256], F32,
                             kind="ExternalOutput").ap()
    disp = nc.dram_tensor("disp1", [NSLOT1, D], BF16, kind="ExternalOutput").ap()
    p = Prog(nc)
    k = alloc_common(nc)
    ct0 = e_load_consts(p, k, dram)
    e_load_x1(p, k, dram["x1c"])
    xn_toks = e_ln_plain(p, k, ct0 + [p.tok("dx")])
    i1, i2 = e_l1_routing(p, k, dram, xn_toks)
    sc_tok = e_dispatch(p, k, disp, NSLOT1, [16, 32], [i1, i2],
                        [xn_toks[-1], i2[-1]], "dzr", "dsc")
    st2 = e_ln2_stats(p, k, [sc_tok] + xn_toks)
    xadd = e_l1_attn(p, k, dram, xn_toks, st2, [sc_tok])
    for c in range(NCH):
        p.dma("act", xpartT[c * 128:(c + 1) * 128, :],
              k.x[:, c * NT:(c + 1) * NT], "dout", waits=xadd)
    p.dma("act", ridx_o, k.ridx[:, :], "dout", waits=i1 + i2)
    p.dma("act", route_o, k.route[:, :], "dout", waits=xadd)
    p.wait("act", [p.tok("dout"), sc_tok])
    return p.finalize()


def build_progC2():
    nc = bass.Bass("TRN2", target_bir_lowering=False, debug=False)
    dram = declare_const_inputs(nc)
    dram.update(declare_l1_inputs(nc))
    dram["xpartT"] = nc.dram_tensor("xpartT", [D, NT], F32,
                                    kind="ExternalInput").ap()
    dram["ridx_i"] = nc.dram_tensor("ridx_i", [128, 64], I32,
                                    kind="ExternalInput").ap()
    dram["route_i"] = nc.dram_tensor("route_i", [128, 256], F32,
                                     kind="ExternalInput").ap()
    disp = nc.dram_tensor("disp1", [NSLOT1, D], BF16,
                          kind="ExternalInput").ap()
    outT = nc.dram_tensor("outT", [8, D, 256], F32, kind="ExternalOutput").ap()
    eout = nc.dram_tensor("eout1", [NSLOT1, D], BF16, kind="Internal").ap()
    p = Prog(nc)
    k = alloc_common(nc)
    ct0 = e_load_consts(p, k, dram)
    e_load_x(p, k, dram["xpartT"])
    p.dma("sp", k.ridx[:, :], dram["ridx_i"], "dx", waits=ct0)
    p.dma("sp", k.route[:, :], dram["route_i"], "dx", waits=ct0)
    ld = p.tok("dx")
    ldl = [ld] * NTT
    ye_tok = e_l1_experts(p, k, dram, disp, eout, ld, ct0 + [ld])
    add_toks = e_combine(p, k, eout, NSLOT1, 0, [ldl, ldl], [(1, 16), (2, 32)],
                         ye_tok, "dgc", extra_scale_tok=[ld])
    for c in range(NCH):
        for s in range(8):
            p.dma("act", outT[s, c * 128:(c + 1) * 128, :],
                  k.x[:, c * NT + s * 256:c * NT + s * 256 + 256], "dout",
                  waits=add_toks)
    p.wait("act", [p.tok("dout")])
    return p.finalize()



def e_xn_tm(p, k, pre):
    toks = []
    bank_last = [None, None]
    for t in range(NTT):
        cps = []
        for c in range(NCH):
            bank = k.pt0 if c % 2 == 0 else k.pt1
            w = list(pre)
            if bank_last[c % 2] is not None:
                w.append(bank_last[c % 2])
            tt = p.tr(bank[0:128, 0:128],
                      k.xn[:, c * NT + t * 128:c * NT + (t + 1) * 128],
                      k.ident_bf[:, :], waits=w, inc="spe")
            cp = p.dve("tensor_copy",
                       k.bufb[:, t * 1024 + c * 128:t * 1024 + (c + 1) * 128],
                       bank[0:128, 0:128], waits=[tt], inc="sdve")
            bank_last[c % 2] = cp
            cps.append(cp)
        toks.append(cps[-1])
    return toks


def e_moe_mm_l0(p, k, dram, xntm_toks, pre):
    x_toks = []
    state = {}
    guard = pre[-1]
    for e in range(E0):
        xev = []
        for wave in range(2):
            mmt = None
            slot_last = state.get("slot_last", [None] * 4)
            for t in range(NTT):
                slot = t % 4
                w = [guard]
                if slot_last[slot] is not None:
                    w.append(slot_last[slot])
                oh = p.dve("tensor_scalar",
                           k.stage[:, slot * C0:(slot + 1) * C0],
                           k.iota512[:, 0:C0],
                           k.route[:, t * 16 + 11:t * 16 + 12],
                           -float(e * C0), ALU.subtract, ALU.is_equal,
                           waits=w, inc="sdve")
                for cw in range(4):
                    c = wave * 4 + cw
                    bank = [k.ps0, k.ps1, k.ps2, k.ps3][cw]
                    w = [oh, xntm_toks[t]]
                    if t == 0:
                        w.append(state.get("xdw%d" % cw))
                    mmt = p.mm(bank[:, 0:C0],
                               k.bufb[:, t * 1024 + c * 128:
                                      t * 1024 + (c + 1) * 128],
                               k.stage[:, slot * C0:(slot + 1) * C0],
                               start=(t == 0), stop=(t == NTT - 1), waits=w,
                               inc="spe" if cw == 3 else None)
                    if cw == 3:
                        slot_last[slot] = mmt
            state["slot_last"] = slot_last
            for cw in range(4):
                c = wave * 4 + cw
                bank = [k.ps0, k.ps1, k.ps2, k.ps3][cw]
                ev = p.dve("tensor_copy", k.bufa[:, c * C0:(c + 1) * C0],
                           bank[:, 0:C0],
                           waits=[mmt, state.pop("xfw%d" % c, None)],
                           inc="sdve")
                state["xdw%d" % cw] = ev
                xev.append(ev)
        xet = xev[-1]
        b1t = p.dma("sp", k.bcols[:, 0:32], dram["b1c"][e], "db1",
                    waits=[state["b1w"]] if "b1w" in state else [guard])
        b1tok = p.tok("db1")
        relu_toks = []
        for hc in range(32):
            g, sub = divmod(hc, 2)
            slot = g % 4
            if sub == 0:
                wkey = "w1c%d" % slot
                w = [state[wkey]] if wkey in state else [guard]
                p.dma("pool", k.bufa[:, 6144 + slot * 2048:8192 + slot * 2048]
                      .rearrange("p (ci f) -> p ci f", ci=8),
                      dram["w1"][e].rearrange("(ci p) f -> p ci f", p=128)
                      [:, :, g * 256:(g + 1) * 256],
                      "dw1%d" % slot, waits=w)
                state["w1t%d" % slot] = p.tok("dw1%d" % slot)
            wbase = 6144 + slot * 2048
            bank = k.ps0 if hc % 2 == 0 else k.ps1
            mmt = None
            for c in range(NCH):
                w = [state["w1t%d" % slot], xet]
                if c == 0 and len(relu_toks) >= 2:
                    w.append(relu_toks[-2])
                mmt = p.mm(bank[:, 0:C0],
                           k.bufa[:, wbase + c * 256 + sub * 128:
                                  wbase + c * 256 + sub * 128 + 128],
                           k.bufa[:, c * C0:(c + 1) * C0],
                           start=(c == 0), stop=(c == NCH - 1), waits=w,
                           inc="spe" if c == NCH - 1 else None)
            state["w1c%d" % slot] = mmt
            w = [mmt, b1tok]
            hk = "hcons%d" % hc
            if hk in state:
                w.append(state.pop(hk))
            else:
                w.append(guard)
            rt = p.act("activation", k.xn[:, hc * C0:(hc + 1) * C0],
                       bank[:, 0:C0], AF.Relu, bias=k.bcols[:, hc:hc + 1],
                       scale=1.0, waits=w, inc="sact")
            relu_toks.append(rt)
        state["b1w"] = relu_toks[-1]
        for c in range(NCH):
            state["xfw%d" % c] = relu_toks[-1]
        b2t = p.dma("pool", k.biasrow[0:1, 0:1024], dram["b2r"][e:e + 1, :],
                    "db2", waits=[state["b2w"]] if "b2w" in state else [guard])
        b2tok = p.tok("db2")
        ydones = {}
        NST0 = C0 // 128

        def w2ap(slot, c0, c1):
            if slot < 2:
                return k.rope[:, slot * 2048 + c0:slot * 2048 + c1]
            return k.xn[:, 12288 + (slot - 2) * 2048 + c0:
                        12288 + (slot - 2) * 2048 + c1]
        for oc in range(2):
            last4 = [None] * NST0
            for hg in range(8):
                slot = (oc * 8 + hg) % 4
                wkey = "w2c%d" % slot
                w = [state[wkey]] if wkey in state else [guard]
                p.dma("pool", w2ap(slot, 0, 2048)
                      .rearrange("p (hh f) -> p hh f", hh=4),
                      dram["w2"][e].rearrange("(hh p) f -> p hh f", p=128)
                      [:, hg * 4:(hg + 1) * 4, oc * 512:(oc + 1) * 512],
                      "dw2%d" % slot, waits=w)
                w2tok = p.tok("dw2%d" % slot)
                for st in range(NST0):
                    bank = [k.ps2[:, :], k.ps3[:, :], k.pq[:, 0:512]][st]
                    for i in range(4):
                        hc = hg * 4 + i
                        w = [w2tok, relu_toks[hc]]
                        if hg == 0 and i == 0 and "ycons%d" % st in state:
                            w.append(state.pop("ycons%d" % st))
                        last4[st] = p.mm(
                            bank,
                            k.xn[:, hc * C0 + st * 128:hc * C0 + st * 128 + 128],
                            w2ap(slot, i * 512, (i + 1) * 512),
                            start=(hg == 0 and i == 0), stop=False, waits=w,
                            inc="spe" if (st == NST0 - 1 and i == 3) else None)
                state[wkey] = last4[NST0 - 1]
            for st in range(NST0):
                bank = [k.ps2[:, :], k.ps3[:, :], k.pq[:, 0:512]][st]
                bm = p.mm(bank, k.ones_row_bf[0:1, 0:128],
                          k.biasrow[0:1, oc * 512:(oc + 1) * 512],
                          start=False, stop=True, waits=[b2tok], inc="spe")
                ev = p.dve("tensor_copy",
                           k.bufa[:, 3072 + st * 1024 + oc * 512:
                                  3072 + st * 1024 + (oc + 1) * 512],
                           bank, waits=[bm] +
                           ([state.pop("ye%d_%d" % (st, oc))]
                            if "ye%d_%d" % (st, oc) in state else []),
                           inc="sdve")
                state["ycons%d" % st] = ev
                ydones[(st, oc)] = ev
        ytoks = list(ydones.values())
        for tg in range(4):
            trs = []
            for j in range(4):
                t = tg * 4 + j
                w = ytoks if j == 0 else [trs[-1]]
                if "rowc" in state and j == 0:
                    w = w + [state.pop("rowc")]
                t1 = p.tr(k.pt0[0:1, j * 128:(j + 1) * 128]
                          if False else k.ps0[0:1, j * 128:(j + 1) * 128],
                          k.route[:, t * 16 + 11:t * 16 + 12],
                          k.ident[:, :], waits=w, inc="spe")
                t2 = p.tr(k.ps1[0:1, j * 128:(j + 1) * 128],
                          k.route[:, t * 16 + 0:t * 16 + 1],
                          k.ident[:, :], waits=[t1], inc="spe")
                trs.append(t2)
            r1 = p.dve("tensor_copy", k.rowbuf[0:1, 0:512], k.ps0[0:1, 0:512],
                       waits=[trs[-1]] + ([state.pop("rowr")]
                                          if "rowr" in state else []),
                       inc="sdve")
            r2 = p.dve("tensor_copy", k.rowbuf[0:1, 512:1024],
                       k.ps1[0:1, 0:512], waits=[trs[-1], r1], inc="sdve")
            state["rowc"] = r2
            m1 = p.mm(k.ps0[:, :], k.ones_row[0:1, 0:128],
                      k.rowbuf[0:1, 0:512], start=True, stop=False,
                      waits=[r1], inc="spe")
            m1 = p.mm(k.ps0[:, :], k.niota_row[0:1, 0:128],
                      k.ones_row[0:1, 0:512], start=False, stop=True,
                      waits=[], inc="spe")
            mp = p.mm(k.ps1[:, :], k.ones_row[0:1, 0:128],
                      k.rowbuf[0:1, 512:1024], start=True, stop=True,
                      waits=[r2], inc="spe")
            state["rowr"] = mp
            ohts = []
            for st in range(C0 // 128):
                o1 = p.dve("tensor_single_scalar",
                           k.stage[:, st * 512:(st + 1) * 512], k.ps0[:, :],
                           float(e * C0 + st * 128), ALU.is_equal,
                           waits=[m1] + ([state.pop("cmb%d" % st)]
                                         if "cmb%d" % st in state else []),
                           inc="sdve")
                o2 = p.dve("tensor_tensor",
                           k.stage[:, st * 512:(st + 1) * 512],
                           k.stage[:, st * 512:(st + 1) * 512], k.ps1[:, :],
                           ALU.mult, waits=[o1, mp], inc="sdve")
                ohts.append(o2)
            state["rowr"] = ohts[-1]
            state["rowc"] = ohts[-1]
            for c in range(NCH):
                bank = k.ps2 if c % 2 == 0 else k.ps3
                mmt = None
                nst = C0 // 128
                for st in range(nst):
                    w = [ohts[st], ydones[(st, c // 4)]]
                    if st == 0 and "cadd%d" % (c % 2) in state:
                        w.append(state.pop("cadd%d" % (c % 2)))
                    mmt = p.mm(bank[:, :],
                               k.bufa[:, 3072 + st * 1024 + c * 128:
                                      3072 + st * 1024 + (c + 1) * 128],
                               k.stage[:, st * 512:(st + 1) * 512],
                               start=(st == 0), stop=(st == nst - 1), waits=w,
                               inc="spe" if st == nst - 1 else None)
                at = p.dve("tensor_tensor",
                           k.x[:, c * NT + tg * 512:c * NT + (tg + 1) * 512],
                           k.x[:, c * NT + tg * 512:c * NT + (tg + 1) * 512],
                           bank[:, :], ALU.add,
                           waits=[mmt] + list(pre), inc="sdve")
                state["cadd%d" % (c % 2)] = at
                x_toks.append(at)
            for st in range(C0 // 128):
                state["cmb%d" % st] = x_toks[-1]
        for cw in range(4):
            state["xdw%d" % cw] = x_toks[-1]
    return x_toks


def e_ffn_l1_expert_mm(p, k, dram, eout_dram, xntm_toks, e, state):
    guard = state.get("guard")
    xev = []
    for sg in range(2):
        ohs = []
        for t in range(NTT):
            w = [guard] if guard else []
            ok = "ohb%d" % t
            if ok in state:
                w.append(state.pop(ok))
            SG1 = C1 // 2
            o1 = p.dve("tensor_scalar",
                       k.bufa[:, 6144 + t * SG1:6144 + (t + 1) * SG1],
                       k.iota512[:, 0:SG1],
                       k.route[:, t * 16 + 12:t * 16 + 13],
                       -float(e * C1 + sg * SG1),
                       ALU.subtract, ALU.is_equal, waits=w, inc="sdve")
            pp = t % 2
            t2w = list(w)
            if "ohtmp%d" % pp in state:
                t2w.append(state.pop("ohtmp%d" % pp))
            t2 = p.dve("tensor_scalar",
                       k.stage[:, 1024 + pp * SG1:1024 + (pp + 1) * SG1],
                       k.iota512[:, 0:SG1],
                       k.route[:, t * 16 + 13:t * 16 + 14],
                       -float(e * C1 + sg * SG1),
                       ALU.subtract, ALU.is_equal, waits=t2w, inc="sdve")
            oh = p.dve("tensor_tensor",
                       k.bufa[:, 6144 + t * SG1:6144 + (t + 1) * SG1],
                       k.bufa[:, 6144 + t * SG1:6144 + (t + 1) * SG1],
                       k.stage[:, 1024 + pp * SG1:1024 + (pp + 1) * SG1],
                       ALU.add, waits=[o1, t2], inc="sdve")
            state["ohtmp%d" % pp] = oh
            ohs.append(oh)
        SG1 = C1 // 2
        for wave in range(2):
            mmt = None
            for t in range(NTT):
                for cw in range(4):
                    c = wave * 4 + cw
                    bank = [k.ps0, k.ps1, k.ps2, k.ps3][cw]
                    w = [ohs[t], xntm_toks[t]]
                    if t == 0:
                        w.append(state.get("xdw%d" % cw))
                    mmt = p.mm(bank[:, 0:SG1],
                               k.bufb[:, t * 1024 + c * 128:
                                      t * 1024 + (c + 1) * 128],
                               k.bufa[:, 6144 + t * SG1:6144 + (t + 1) * SG1],
                               start=(t == 0), stop=(t == NTT - 1), waits=w,
                               inc="spe" if cw == 3 else None)
            for cw in range(4):
                c = wave * 4 + cw
                bank = [k.ps0, k.ps1, k.ps2, k.ps3][cw]
                ev = p.dve("tensor_copy",
                           k.bufa[:, c * C1 + sg * SG1:
                                  c * C1 + (sg + 1) * SG1],
                           bank[:, 0:SG1],
                           waits=[mmt, state.pop("xfw%d_%d" % (c, sg), None)],
                           inc="sdve")
                state["xdw%d" % cw] = ev
                xev.append(ev)
    xet = xev[-1]
    for t in range(NTT):
        state["ohb%d" % t] = xet
    p.dma("pool", k.biasrow[0:1, 0:2048], dram["bg1r"][e:e + 1, :], "db1g",
          waits=[state["bgw"]] if "bgw" in state else [guard])
    p.dma("pool", k.biasrow2[0:1, 0:2048], dram["bu1r"][e:e + 1, :], "db1g")
    btok = p.tok("db1g")
    h_toks = []

    def wgap(slot, c0, c1):
        if slot < 2:
            return k.bufa[:, 12288 + slot * 2048 + c0:
                          12288 + slot * 2048 + c1]
        return k.xn[:, 12288 + (slot - 2) * 2048 + c0:
                    12288 + (slot - 2) * 2048 + c1]
    for gf in range(16):
        slot = gf % 4
        wg = ([state["wgc%d" % slot]] if "wgc%d" % slot in state else []) + [xet]
        SG1 = C1 // 2
        p.dma("pool", wgap(slot, 0, 1024)
              .rearrange("p (ci f) -> p ci f", ci=8),
              dram["wg1"][e].rearrange("(ci p) f -> p ci f", p=128)
              [:, :, gf * 128:(gf + 1) * 128], "dwg%d" % slot, waits=wg)
        p.dma("pool", wgap(slot, 1024, 2048)
              .rearrange("p (ci f) -> p ci f", ci=8),
              dram["wu1"][e].rearrange("(ci p) f -> p ci f", p=128)
              [:, :, gf * 128:(gf + 1) * 128], "dwg%d" % slot)
        wt = p.tok("dwg%d" % slot)
        for sh in range(2):
            bg = k.ps2[:, 0:SG1] if sh == 0 else k.ps3[:, 0:SG1]
            bu = k.pq[:, 0:SG1] if sh == 0 else k.pq[:, 512:512 + SG1]
            m1 = m2 = None
            for ci in range(NCH):
                w = [wt, xet]
                if ci == 0 and len(h_toks) >= 2:
                    w.append(h_toks[-2])
                m1 = p.mm(bg,
                          wgap(slot, ci * 128, (ci + 1) * 128),
                          k.bufa[:, ci * C1 + sh * SG1:
                                 ci * C1 + (sh + 1) * SG1],
                          start=(ci == 0), stop=False, waits=w)
                m2 = p.mm(bu,
                          wgap(slot, 1024 + ci * 128, 1024 + (ci + 1) * 128),
                          k.bufa[:, ci * C1 + sh * SG1:
                                 ci * C1 + (sh + 1) * SG1],
                          start=(ci == 0), stop=False, waits=w)
            m1 = p.mm(bg, k.biasrow[0:1, gf * 128:(gf + 1) * 128],
                      k.ones_row_bf[0:1, 0:SG1], start=False, stop=True,
                      waits=[btok], inc="spe")
            m2 = p.mm(bu, k.biasrow2[0:1, gf * 128:(gf + 1) * 128],
                      k.ones_row_bf[0:1, 0:SG1], start=False, stop=True,
                      waits=[btok], inc="spe")
            sw = [m1] + ([state.pop("sil%d" % sh)]
                         if "sil%d" % sh in state else [])
            sg_ = p.act("activation", k.stage[:, sh * SG1:(sh + 1) * SG1],
                        bg, AF.Silu, waits=sw, inc="sact")
            state["sil%d" % sh] = sg_
            hw = [sg_, m2]
            hk = "hc%d_%d" % (gf, sh)
            if hk in state:
                hw.append(state.pop(hk))
            ht = p.dve("tensor_tensor",
                       k.xn[:, gf * C1 + sh * SG1:gf * C1 + (sh + 1) * SG1],
                       k.stage[:, sh * SG1:(sh + 1) * SG1], bu, ALU.mult,
                       waits=hw, inc="sdve")
            h_toks.append(ht)
        state["wgc%d" % slot] = h_toks[-1]
    state["bgw"] = h_toks[-1]
    p.dma("pool", k.biasrow[0:1, 0:1024], dram["bd1r"][e:e + 1, :], "db1d",
          waits=[h_toks[-1]])
    bdt = p.tok("db1d")
    def wdap(slot, c0, c1):
        if slot < 2:
            return k.rope[:, slot * 2048 + c0:slot * 2048 + c1]
        return k.scr[:, 1024:2048].bitcast(BF16)[:, c0:c1]
    for wave in range(2):
        ydones = {}
        for oc in range(2):
            wlast = None
            for ggg in range(4):
                slot = ((wave * 2 + oc) * 4 + ggg) % 3
                wkey = "wdc%d" % slot
                w = [state[wkey]] if wkey in state else [guard]
                p.dma("pool", wdap(slot, 0, 2048)
                      .rearrange("p (gr f) -> p gr f", gr=4),
                      dram["wd1"][e].rearrange("(gh p) f -> p gh f", p=128)
                      [:, ggg * 4:(ggg + 1) * 4, oc * 512:(oc + 1) * 512],
                      "dwd%d" % slot, waits=w)
                wdt = p.tok("dwd%d" % slot)
                for sq in range(3):
                    st = wave * 3 + sq
                    bank = [k.ps2[:, :], k.ps3[:, :], k.pq[:, 0:512]][sq]
                    for i in range(4):
                        gf2 = ggg * 4 + i
                        w2 = [wdt, h_toks[2 * gf2], h_toks[2 * gf2 + 1]]
                        if ggg == 0 and i == 0 and "yc%d" % sq in state:
                            w2.append(state.pop("yc%d" % sq))
                        wlast = p.mm(bank,
                                     k.xn[:, gf2 * C1 + st * 128:
                                          gf2 * C1 + st * 128 + 128],
                                     wdap(slot, i * 512, (i + 1) * 512),
                                     start=(ggg == 0 and i == 0), stop=False,
                                     waits=w2,
                                     inc="spe" if (sq == 2 and i == 3)
                                     else None)
                state[wkey] = wlast
            for sq in range(3):
                st = wave * 3 + sq
                bank = [k.ps2[:, :], k.ps3[:, :], k.pq[:, 0:512]][sq]
                bm = p.mm(bank, k.ones_row_bf[0:1, 0:128],
                          k.biasrow[0:1, oc * 512:(oc + 1) * 512],
                          start=False, stop=True, waits=[bdt], inc="spe")
                ev = p.dve("tensor_copy",
                           k.bufa[:, 6144 + sq * 1024 + oc * 512:
                                  6144 + sq * 1024 + oc * 512 + 512],
                           bank, waits=[bm] +
                           ([state.pop("ye%d" % sq)]
                            if "ye%d" % sq in state else []),
                           inc="sdve")
                state["yc%d" % sq] = ev
                ydones[(sq, oc)] = ev
        for sq in range(3):
            st = wave * 3 + sq
            p.dma("act",
                  eout_dram[e * C1 + st * 128:e * C1 + (st + 1) * 128, :],
                  k.bufa[:, 6144 + sq * 1024:6144 + (sq + 1) * 1024],
                  "dye1", waits=[ydones[(sq, 0)], ydones[(sq, 1)]])
        ydall = p.tok("dye1")
        for sq in range(3):
            state["ye%d" % sq] = ydall
    for c in range(NCH):
        for sg in range(2):
            state["xfw%d_%d" % (c, sg)] = h_toks[-1]
    state["guard"] = p.tok("dye1")
    state["bgw"] = p.tok("dye1")
    return p.tok("dye1")


def e_l1_experts_mm(p, k, dram, eout_dram, xntm_toks, pre):
    state = {"guard": pre[-1]}
    last = None
    for e in range(E1):
        last = e_ffn_l1_expert_mm(p, k, dram, eout_dram, xntm_toks, e, state)
    return last


def e_counts_cc(p, k, cc_in, cc_out, ctok):
    d1 = p.dma("act", cc_in, k.cnt[0:8, 0:1], "dcc1", waits=[ctok])
    cc = p.pool("collective_compute", "AllGather", ALU.bypass,
                replica_groups=[list(range(NCORES))],
                ins=[cc_in.opt()], outs=[cc_out.opt()],
                waits=[p.tok("dcc1")], inc="ccs", step=1)
    ld = p.dma("sp", k.cnt[0:8, 8:16], cc_out.rearrange("(b e) -> b e", e=8),
               "dca", waits=[cc])
    return p.tok("dca")


def e_a2a_reshard(p, k, a_io, x_toks, xfin=None):
    ccs = []
    for h in range(2):
        a_in, a_out = a_io[h]
        for c in range(4 * h, 4 * h + 4):
            for sl in range(8):
                if xfin is not None:
                    w = [xfin[(sl // 2) * 8 + c]]
                else:
                    w = x_toks
                p.dma("act", a_in[sl, (c - 4 * h) * 128:
                                  (c - 4 * h + 1) * 128, :],
                      k.x[:, c * NT + sl * 256:c * NT + sl * 256 + 256],
                      "da2a%d" % h, waits=w)
        cc = p.pool("collective_compute", "AllToAll", ALU.bypass,
                    replica_groups=[list(range(NCORES))],
                    ins=[a_in.opt()], outs=[a_out.opt()],
                    waits=[p.tok("da2a%d" % h)], inc="ccs", step=1)
        ccs.append(cc)
    toks = []
    for c in range(NCH):
        h = c // 4
        a_in, a_out = a_io[h]
        for sl in range(8):
            p.dma("sp", k.x[:, c * NT + sl * 256:c * NT + sl * 256 + 256],
                  a_out[sl, (c - 4 * h) * 128:(c - 4 * h + 1) * 128, :],
                  "dx2_%d" % c, waits=[ccs[h]])
        toks.append(p.tok("dx2_%d" % c))
    return toks


def build_progA():
    nc = bass.Bass("TRN2", target_bir_lowering=False, debug=False)
    dram = declare_const_inputs(nc)
    dram["xT"] = nc.dram_tensor("xT", [D, NT], F32, kind="ExternalInput").ap()
    dram["gate_t"] = nc.dram_tensor("gate_t", [128, 64], F32,
                                    kind="ExternalInput").ap()
    cnts = nc.dram_tensor("cnts", [8, 1], F32, kind="ExternalOutput").ap()
    p = Prog(nc)
    k = alloc_common(nc)
    ct0 = e_load_consts(p, k, dram)
    xt = e_load_x(p, k, dram["xT"])
    xn_toks = e_ln_plain(p, k, ct0 + xt)
    ctok, _ = e_l0_counts(p, k, dram, xn_toks)
    st = p.dma("act", cnts, k.cnt[0:8, 0:1], "dout", waits=[ctok])
    p.wait("act", [st])
    return p.finalize()


def declare_l0_attn_inputs(nc):
    d = {}
    d["wqa"] = nc.dram_tensor("wqa", [128, 512], F32, kind="ExternalInput").ap()
    d["wkva"] = nc.dram_tensor("wkva", [128, 256], F32,
                               kind="ExternalInput").ap()
    d["wqb2"] = nc.dram_tensor("wqb2", [QL, 2048], F32,
                               kind="ExternalInput").ap()
    d["wkvb3"] = nc.dram_tensor("wkvb3", [KVL, 768], F32,
                                kind="ExternalInput").ap()
    d["wout"] = nc.dram_tensor("wout", [D, D], F32, kind="ExternalInput").ap()
    d["bout_r"] = nc.dram_tensor("bout_r", [1, D], F32,
                                 kind="ExternalInput").ap()
    return d


def e_l0_attn(p, k, dram, xn_toks, pre):
    VT0 = 4096
    QK0 = 8256
    TC0 = 10304
    PT0 = 12352
    WQ0 = 13376
    WK0 = 15424
    w = list(pre)
    p.dma("pool", k.stage[:, 0:512], dram["wqa"], "dwa", waits=w)
    p.dma("pool", k.stage[:, 512:768], dram["wkva"], "dwa")
    p.dma("pool", k.bufa[0:QL, WQ0:WQ0 + 2048], dram["wqb2"], "dwa", waits=w)
    p.dma("pool", k.bufa[0:KVL, WK0:WK0 + 768], dram["wkvb3"], "dwa", waits=w)
    wtok = p.tok("dwa")
    vm = p.dve("memset", k.bufa[:, VT0:VT0 + 4160], 1.0, waits=pre,
               inc="sdve")
    kl_toks = []
    ev_prev = None
    for tg in range(4):
        m2 = None
        for c in range(NCH):
            w = [wtok] + xn_toks
            if c == 0 and ev_prev is not None:
                w.append(ev_prev)
            m2 = p.mm(k.ps1[0:KVL, :],
                      k.stage[:, 512 + c * 32:512 + (c + 1) * 32],
                      k.xn[:, c * NT + tg * 512:c * NT + tg * 512 + 512],
                      start=(c == 0), stop=(c == NCH - 1), waits=w,
                      inc="spe" if c == NCH - 1 else None)
        e2 = p.dve("tensor_copy",
                   k.bufa[0:KVL, QK0 + tg * 512:QK0 + tg * 512 + 512],
                   k.ps1[0:KVL, :], waits=[m2, vm], inc="sdve")
        ev_prev = e2
        kl_toks.append(e2)
    kv_toks = []
    vtr_toks = []
    kvread = None
    for j in range(2):
        vev = []
        rp_prev = None
        for tg in range(4):
            w = [kl_toks[-1]] + ([rp_prev] if rp_prev else [])
            mk = p.mm(k.ps0[:, :], k.bufa[0:KVL, WK0 + j * 128:WK0 + (j + 1) * 128],
                      k.bufa[0:KVL, QK0 + tg * 512:QK0 + tg * 512 + 512],
                      start=True, stop=True, waits=w, inc="spe")
            ms = p.mm(k.ps1[:, :],
                      k.bufa[0:KVL, WK0 + 256 + j * 128:WK0 + 256 + (j + 1) * 128],
                      k.bufa[0:KVL, QK0 + tg * 512:QK0 + tg * 512 + 512],
                      start=True, stop=True, waits=w, inc="spe")
            mv = p.mm(k.ps2[:, :],
                      k.bufa[0:KVL, WK0 + 512 + j * 128:WK0 + 512 + (j + 1) * 128],
                      k.bufa[0:KVL, QK0 + tg * 512:QK0 + tg * 512 + 512],
                      start=True, stop=True,
                      waits=w + ([vev[-1]] if vev else []), inc="spe")
            kvread = mv
            t1 = p.dve("tensor_tensor", k.scr[:, 0:512], k.ps0[:, :],
                       k.rope[:, tg * 512:(tg + 1) * 512], ALU.mult,
                       waits=[mk, rp_prev], inc="sdve")
            t2 = p.dve("tensor_tensor", k.scr2[:, 0:512], k.ps1[:, :],
                       k.rope[:, NT + tg * 512:NT + (tg + 1) * 512], ALU.mult,
                       waits=[ms, t1], inc="sdve")
            t3 = p.dve("tensor_tensor",
                       k.bufa[:, j * 2048 + tg * 512:j * 2048 + (tg + 1) * 512],
                       k.scr[:, 0:512], k.scr2[:, 0:512], ALU.add,
                       waits=[t2], inc="sdve")
            rp_prev = t3
            kv_toks.append(t3)
            ve = p.dve("tensor_copy", k.bufa[:, TC0 + tg * 512:TC0 + (tg + 1) * 512],
                       k.ps2[:, :], waits=[mv] + ([vev[-1]] if vev else []),
                       inc="sdve")
            vev.append(ve)
        tprev = [None, None]
        for kt in range(16):
            for half in range(2):
                kvh = 2 * j + half
                bank = k.pt0 if (kt + half) % 2 == 0 else k.pt1
                w = [vev[kt // 4], vm]
                if tprev[(kt + half) % 2] is not None:
                    w.append(tprev[(kt + half) % 2])
                tt = p.tr(bank[0:128, 0:64],
                          k.bufa[half * 64:half * 64 + 64,
                                 TC0 + kt * 128:TC0 + (kt + 1) * 128],
                          k.ident_bf[half * 64:half * 64 + 64,
                                     half * 64:half * 64 + 64],
                          waits=w, inc="spe")
                cc = p.dve("tensor_copy",
                           k.bufa[:, VT0 + (kvh * 16 + kt) * 65:
                                  VT0 + (kvh * 16 + kt) * 65 + 64],
                           bank[0:128, 0:64], waits=[tt], inc="sdve")
                tprev[(kt + half) % 2] = cc
                vtr_toks.append(cc)
    ql_toks = []
    ev_prev = None
    for tg in range(4):
        m1 = None
        for c in range(NCH):
            w = [wtok, kvread] + xn_toks
            if c == 0 and ev_prev is not None:
                w.append(ev_prev)
            m1 = p.mm(k.ps0[0:QL, :], k.stage[:, c * 64:(c + 1) * 64],
                      k.xn[:, c * NT + tg * 512:c * NT + tg * 512 + 512],
                      start=(c == 0), stop=(c == NCH - 1),
                      waits=w + [kv_toks[-1]],
                      inc="spe" if c == NCH - 1 else None)
        e1 = p.dve("tensor_copy", k.bufa[0:QL, QK0 + tg * 512:QK0 + tg * 512 + 512],
                   k.ps0[0:QL, :], waits=[m1, vtr_toks[-1]], inc="sdve")
        ev_prev = e1
        ql_toks.append(e1)
    xn_done = p.tok("spe")
    xadd_toks = []
    qr_prev = None
    pv_prev = [None, None]
    sc_prev = [None, None]
    exp_prev = [None, None]
    bc_prev = None
    sh_prev = None
    zlast = None
    fin_q = []
    for c in range(NCH):
        qr_toks = []
        for tg in range(4):
            w = [wtok, ql_toks[-1], vtr_toks[-1]]
            if qr_prev is not None:
                w.append(qr_prev)
            if zlast is not None:
                w.append(zlast)
            mq = p.mm(k.ps0[:, :], k.bufa[0:QL, WQ0 + c * 128:WQ0 + (c + 1) * 128],
                      k.bufa[0:QL, QK0 + tg * 512:QK0 + tg * 512 + 512],
                      start=True, stop=True, waits=w, inc="spe")
            ms = p.mm(k.ps1[:, :],
                      k.bufa[0:QL, WQ0 + 1024 + c * 128:WQ0 + 1024 + (c + 1) * 128],
                      k.bufa[0:QL, QK0 + tg * 512:QK0 + tg * 512 + 512],
                      start=True, stop=True, waits=w, inc="spe")
            t1 = p.dve("tensor_tensor", k.scr[:, 0:512], k.ps0[:, :],
                       k.rope[:, tg * 512:(tg + 1) * 512], ALU.mult,
                       waits=[mq] + ([qr_prev] if qr_prev else []) +
                       ([zlast] if zlast else []), inc="sdve")
            t2 = p.dve("tensor_tensor", k.scr2[0:128, 0:512], k.ps1[:, :],
                       k.rope[:, NT + tg * 512:NT + (tg + 1) * 512], ALU.mult,
                       waits=[ms, t1], inc="sdve")
            t3 = p.dve("tensor_tensor",
                       k.bufa[:, TC0 + tg * 512:TC0 + (tg + 1) * 512],
                       k.scr[:, 0:512], k.scr2[0:128, 0:512], ALU.add,
                       waits=[t2, vtr_toks[-1]], inc="sdve")
            qr_toks.append(t3)
            qr_prev = t3
        for hh in range(2):
            h = HPAIR[c][hh]
            kvh = h // 4
            j, rbk = kvh // 2, (kvh % 2) * 64
            rbq = hh * 64
            assert rbk == rbq
            for Q in range(4):
                par = (c * 8 + hh * 4 + Q) % 2
                pvbank = k.pq[0:65, par * 512:par * 512 + 512]
                nkt = 4 * Q + 4
                pv = None
                exs = [None] * nkt

                def emit_score(kt):
                    sbank = k.ps2 if kt % 2 == 0 else k.ps3
                    w = [qr_toks[Q], kv_toks[-1]]
                    if exp_prev[kt % 2] is not None:
                        w.append(exp_prev[kt % 2])
                    msc = p.mm(sbank[:, :],
                               k.bufa[rbk:rbk + 64,
                                      j * 2048 + kt * 128:j * 2048 + (kt + 1) * 128],
                               k.bufa[rbq:rbq + 64,
                                      TC0 + Q * 512:TC0 + (Q + 1) * 512],
                               start=True, stop=True, waits=w, inc="spe")
                    pslot = k.bufa[:, PT0 + (kt % 2) * 512:
                                   PT0 + (kt % 2) * 512 + 512]
                    w2 = [msc]
                    if pv_prev[kt % 2] is not None:
                        w2.append(pv_prev[kt % 2])
                    ex = p.act("activation", pslot, sbank[:, :], AF.Exp,
                               scale=0.125, waits=w2, inc="sact")
                    if kt >= 4 * Q:
                        jm = kt - 4 * Q
                        ex = p.dve("tensor_tensor", pslot, pslot,
                                   k.masks[:, 384 - 128 * jm:896 - 128 * jm],
                                   ALU.mult, waits=[ex], inc="sdve")
                    exp_prev[kt % 2] = ex
                    exs[kt] = ex

                emit_score(0)
                for kt in range(nkt):
                    if kt + 1 < nkt:
                        emit_score(kt + 1)
                    if kt == 0 and len(fin_q) >= 2:
                        fin_q.pop(0)()
                    pslot = k.bufa[:, PT0 + (kt % 2) * 512:
                                   PT0 + (kt % 2) * 512 + 512]
                    w = [exs[kt]]
                    if kt == 0:
                        w.append(sc_prev[par])
                    pv = p.mm(pvbank,
                              k.bufa[:, VT0 + (kvh * 16 + kt) * 65:
                                     VT0 + (kvh * 16 + kt) * 65 + 65],
                              pslot, start=(kt == 0), stop=(kt == nkt - 1),
                              waits=w, inc="spe" if kt == nkt - 1 else None)
                    pv_prev[kt % 2] = pv
                rrow = (k.scr2[64:65, 512:1024] if par == 0
                        else k.scr[64:65, 1024:1536])
                rc = p.dve("reciprocal", rrow,
                           k.pq[64:65, par * 512:par * 512 + 512],
                           waits=[pv] + ([bc_prev] if bc_prev else []),
                           inc="sdve")

                def mk_fin(rc=rc, pv=pv, par=par, c=c, hh=hh, Q=Q,
                           rrow=rrow, qlast=qr_toks[-1]):
                    def fin():
                        nonlocal bc_prev, sh_prev, zlast
                        bc = p.mm(k.ps0[0:64, :], k.onesf[64:65, 0:64],
                                  rrow, start=True,
                                  stop=True, waits=[rc, qlast], inc="spe")
                        bc_prev = bc
                        zc = p.dve("tensor_copy", k.scr[0:64, 512:1024],
                                   k.pq[0:64, par * 512:par * 512 + 512],
                                   waits=[pv, bc], inc="sdve")
                        sc_prev[par] = zc
                        if hh == 0:
                            zn = p.dve("tensor_tensor",
                                       k.bufb[0:64, c * 2048 + Q * 512:
                                              c * 2048 + (Q + 1) * 512],
                                       k.scr[0:64, 512:1024], k.ps0[0:64, :],
                                       ALU.mult, waits=[bc, zc], inc="sdve")
                            zlast = zn
                            xadd_toks.append(zn)
                        else:
                            zn = p.dve("tensor_tensor", k.stage[0:64, 768:1280],
                                       k.scr[0:64, 512:1024], k.ps0[0:64, :],
                                       ALU.mult,
                                       waits=[bc, zc] +
                                       ([sh_prev] if sh_prev else []),
                                       inc="sdve")
                            sh = p.mm(k.ps1[64:128, :], k.ident_bf[0:64, 0:64],
                                      k.stage[0:64, 768:1280], start=True,
                                      stop=True, waits=[zn], inc="spe")
                            cp = p.dve("tensor_copy",
                                       k.bufb[64:128, c * 2048 + Q * 512:
                                              c * 2048 + (Q + 1) * 512],
                                       k.ps1[64:128, :], waits=[sh],
                                       inc="sdve")
                            sh_prev = cp
                            zlast = cp
                            xadd_toks.append(cp)
                    return fin

                fin_q.append(mk_fin())
        while fin_q:
            fin_q.pop(0)()
    az_toks = list(xadd_toks)
    xadd_toks = []
    p.dma("pool", k.biasrow2[0:1, 0:1024], dram["bout_r"], "dwo",
          waits=[az_toks[-1]])
    wo_state = {}
    add_prev = [None, None]
    for oc in range(NCH):
        slot = oc % 2
        w = [wo_state.get(slot) or az_toks[-1]]
        p.dma("pool", k.bufa[:, slot * 2048:slot * 2048 + 1024]
              .rearrange("p (ci f) -> p ci f", ci=8),
              dram["wout"].rearrange("(ci p) f -> p ci f", p=128)
              [:, :, oc * 128:(oc + 1) * 128], "dwo%d" % slot, waits=w)
        wtok2 = p.tok("dwo%d" % slot)
        for tg in range(4):
            bank = k.ps2 if tg % 2 == 0 else k.ps3
            mmt = None
            for ci in range(NCH):
                w = [wtok2, az_toks[-1]]
                if ci == 0 and add_prev[tg % 2] is not None:
                    w.append(add_prev[tg % 2])
                mmt = p.mm(bank[:, :],
                           k.bufa[:, slot * 2048 + ci * 128:
                                  slot * 2048 + (ci + 1) * 128],
                           k.bufb[:, ci * 2048 + tg * 512:
                                  ci * 2048 + (tg + 1) * 512],
                           start=(ci == 0), stop=False, waits=w)
            mmt = p.mm(bank[:, :], k.biasrow2[0:1, oc * 128:(oc + 1) * 128],
                       k.ones_row_bf[0:1, 0:512], start=False, stop=True,
                       waits=[p.tok("dwo")], inc="spe")
            at = p.dve("tensor_tensor",
                       k.x[:, oc * NT + tg * 512:oc * NT + (tg + 1) * 512],
                       k.x[:, oc * NT + tg * 512:oc * NT + (tg + 1) * 512],
                       bank[:, :], ALU.add, waits=[mmt], inc="sdve")
            add_prev[tg % 2] = at
            xadd_toks.append(at)
        wo_state[slot] = xadd_toks[-1]
    return xadd_toks, xn_done


def declare_l0_weight_inputs(nc, with_counts=True):
    d = {}
    d["gate_t"] = nc.dram_tensor("gate_t", [128, 64], F32,
                                 kind="ExternalInput").ap()
    if with_counts:
        d["cnts_all"] = nc.dram_tensor("cnts_all", [8, 8], F32,
                                       kind="ExternalInput").ap()
    d["mymask"] = nc.dram_tensor("mymask", [8, 1], F32,
                                 kind="ExternalInput").ap()
    d["c_ec0"] = nc.dram_tensor("c_ec0", [8, 1], F32,
                                kind="ExternalInput").ap()
    d["w1"] = nc.dram_tensor("w1", [E0, D, DFF0], F32,
                             kind="ExternalInput").ap()
    d["b1c"] = nc.dram_tensor("b1c", [E0, 128, 32], F32,
                              kind="ExternalInput").ap()
    d["w2"] = nc.dram_tensor("w2", [E0, DFF0, D], F32,
                             kind="ExternalInput").ap()
    d["b2r"] = nc.dram_tensor("b2r", [E0, D], F32,
                              kind="ExternalInput").ap()
    return d


def build_progB(with_attn=False):
    nc = bass.Bass("TRN2", target_bir_lowering=False, debug=False)
    dram = declare_const_inputs(nc)
    dram["xT"] = nc.dram_tensor("xT", [D, NT], F32, kind="ExternalInput").ap()
    dram.update(declare_l0_weight_inputs(nc))
    if with_attn:
        dram.update(declare_l0_attn_inputs(nc))
    x1T = nc.dram_tensor("x1T", [D, NT], F32, kind="ExternalOutput").ap()
    disp = nc.dram_tensor("disp0", [NSLOT0, D], BF16, kind="Internal").ap()
    eout = nc.dram_tensor("eout0", [NSLOT0, D], BF16, kind="Internal").ap()
    p = Prog(nc)
    k = alloc_common(nc)
    ct0 = e_load_consts(p, k, dram)
    xt_toks = e_load_x(p, k, dram["xT"])
    gt0 = []
    xn_toks = e_ln_plain(p, k, ct0, chunk_toks=xt_toks, gt_out=gt0)
    ctok, oh_toks = e_l0_counts(p, k, dram, xn_toks, gtoks=gt0)
    idx_toks = e_l0_ranks(p, k, dram, ctok, oh_toks)
    sc_tok = e_dispatch(p, k, disp, NSLOT0, [0], [idx_toks],
                        [xn_toks[-1]], "dzr", "dsc")
    pre = [xn_toks[-1]]
    extra = []
    if with_attn:
        xadd, xn_done = e_l0_attn(p, k, dram, xn_toks, [sc_tok])
        pre = [xn_toks[-1], xn_done, xadd[-1]]
        extra = [xadd[-1]]
    ye_tok = e_l0_experts(p, k, dram, disp, eout, sc_tok, pre)
    add_toks = e_combine(p, k, eout, NSLOT0, 0, [idx_toks], [(0, 0)],
                         ye_tok, "dgc", extra_scale_tok=extra)
    for c in range(NCH):
        p.dma("act", x1T[c * 128:(c + 1) * 128, :],
              k.x[:, c * NT:(c + 1) * NT], "dout", waits=add_toks)
    p.wait("act", [p.tok("dout")])
    return p.finalize()


def build_fused():
    from concourse.aot_env import allow_ici_all_to_all
    nc = bass.Bass("TRN2", target_bir_lowering=False, debug=False,
                   num_devices=NCORES)
    dram = declare_const_inputs(nc)
    dram["xT"] = nc.dram_tensor("xT", [D, NT], F32, kind="ExternalInput").ap()
    dram.update(declare_l0_attn_inputs(nc))
    dram.update(declare_l0_weight_inputs(nc, with_counts=False))
    dram.update(declare_l1_inputs(nc, with_x=False))
    outT = nc.dram_tensor("outT", [8, D, 256], F32, kind="ExternalOutput").ap()
    eout = nc.dram_tensor("eout1", [NSLOT1, D], BF16, kind="Internal").ap()
    cc_in = nc.dram_tensor("cc_in", [8], F32, kind="Internal").ap()
    cc_out = nc.dram_tensor("cc_out", [64], F32, kind="Internal").ap()
    a_io = [(nc.dram_tensor("a2a_in%d" % h, [8, 512, 256], F32,
                            kind="Internal").ap(),
             nc.dram_tensor("a2a_out%d" % h, [8, 512, 256], F32,
                            kind="Internal").ap()) for h in range(2)]
    p = Prog(nc)
    k = alloc_common(nc)
    ct0 = e_load_consts(p, k, dram)
    xt_toks = e_load_x(p, k, dram["xT"])
    gt0 = []
    xn_toks = e_ln_plain(p, k, ct0, chunk_toks=xt_toks, gt_out=gt0)
    ctok, oh_toks = e_l0_counts(p, k, dram, xn_toks, gtoks=gt0)
    cnt_tok = e_counts_cc(p, k, cc_in, cc_out, ctok)
    idx_toks = e_l0_ranks(p, k, dram, ctok, oh_toks, counts_tok=cnt_tok)
    xadd0, xn_done = e_l0_attn(p, k, dram, xn_toks, [idx_toks[-1]])
    xntm_toks = e_xn_tm(p, k, [xadd0[-1], xn_done])
    moe_toks = e_moe_mm_l0(p, k, dram, xntm_toks, [xadd0[-1]])
    a2a_toks = e_a2a_reshard(p, k, a_io, [moe_toks[-1]],
                             xfin=moe_toks[-32:])
    gt2 = []
    xn2 = e_ln_plain(p, k, [], chunk_toks=a2a_toks, gt_out=gt2)
    i1, i2 = e_l1_routing(p, k, dram, xn2, gtoks=gt2)
    st2 = e_ln2_stats(p, k, xn2 + list(gt2[3]))
    xadd1 = e_l1_attn(p, k, dram, xn2, st2, [i2[-1]])
    xntm2 = e_xn_tm(p, k, [xadd1[-1]])
    ye = e_l1_experts_mm(p, k, dram, eout, xntm2, [xadd1[-1], xntm2[-1]])
    add_toks = e_combine(p, k, eout, NSLOT1, 0, [i1, i2], [(1, 16), (2, 32)],
                         ye, "dgc", extra_scale_tok=[xadd1[-1]])
    for s in range(8):
        for c in range(NCH):
            p.dma("act", outT[s, c * 128:(c + 1) * 128, :],
                  k.x[:, c * NT + s * 256:c * NT + s * 256 + 256], "dout",
                  waits=[add_toks[(2 * s) * 8 + c],
                         add_toks[(2 * s + 1) * 8 + c]])
    p.wait("act", [p.tok("dout")])
    with allow_ici_all_to_all():
        return p.finalize()


_KCOLS = (np.arange(256) // 64) * 128 + (np.arange(256) % 64) * 2


def host_prep(inputs):
    perm = np.array(ATTN_PERM)
    Wqb_p = np.asarray(inputs["L0_attn_Wqb"])[:, perm]
    Wkvb = np.asarray(inputs["L0_attn_Wkvb"])
    Wk = Wkvb[:, _KCOLS]
    Wv = Wkvb[:, _KCOLS + 1]
    shared = dict(host_constants())
    shared.update({
        "wqa": _chunked(np.asarray(inputs["L0_attn_Wqa"])),
        "wkva": _chunked(np.asarray(inputs["L0_attn_Wkva"])),
        "wqb2": np.ascontiguousarray(
            np.concatenate([Wqb_p, _swap_cols(Wqb_p, 64)], 1)),
        "wkvb3": np.ascontiguousarray(
            np.concatenate([Wk, _swap_cols(Wk, 64), Wv], 1)),
        "wout": np.ascontiguousarray(np.asarray(inputs["L0_attn_Wout"])[perm, :]),
        "bout_r": np.ascontiguousarray(np.asarray(inputs["L0_attn_bout"])[None]),
        "gate_t": _chunked(np.ascontiguousarray(
            np.asarray(inputs["L0_ffn_gate_w"]).T)),
        "w1": np.asarray(inputs["L0_ffn_w1"]),
        "b1c": np.ascontiguousarray(
            np.asarray(inputs["L0_ffn_b1"]).reshape(8, 32, 128)
            .transpose(0, 2, 1)),
        "w2": np.asarray(inputs["L0_ffn_w2"]),
        "b2r": np.asarray(inputs["L0_ffn_b2"]),
        "c_ec0": (np.arange(8, dtype=np.float32) * C0 - 1).reshape(8, 1),
        "c_ec1": (np.arange(8, dtype=np.float32) * C1 - 1).reshape(8, 1),
        "rw_t": _chunked(np.ascontiguousarray(
            np.asarray(inputs["L1_ffn_router_w"]).T)),
        "l1g_c": np.ascontiguousarray(
            np.asarray(inputs["L1_attn_ln_g"]).reshape(8, 128).T),
        "l1b_c": np.ascontiguousarray(
            np.asarray(inputs["L1_attn_ln_b"]).reshape(8, 128).T),
        "wq1": np.asarray(inputs["L1_attn_Wq"]),
        "wk1": np.asarray(inputs["L1_attn_Wk"]),
        "wv1": np.asarray(inputs["L1_attn_Wv"]),
        "wo1": np.asarray(inputs["L1_attn_Wo"]),
        "bq1r": np.ascontiguousarray(np.asarray(inputs["L1_attn_bq"])[None]),
        "bk1r": np.ascontiguousarray(np.asarray(inputs["L1_attn_bk"])[None]),
        "bv1r": np.ascontiguousarray(np.asarray(inputs["L1_attn_bv"])[None]),
        "bo1r": np.ascontiguousarray(np.asarray(inputs["L1_attn_bo"])[None]),
        "wg1": np.asarray(inputs["L1_ffn_gate_w"]),
        "wu1": np.asarray(inputs["L1_ffn_up_w"]),
        "bg1r": np.asarray(inputs["L1_ffn_gate_b"]),
        "bu1r": np.asarray(inputs["L1_ffn_up_b"]),
        "wd1": np.asarray(inputs["L1_ffn_down_w"]),
        "bd1r": np.asarray(inputs["L1_ffn_down_b"]),
    })
    return shared


def _pick(shared, names):
    return {n: shared[n] for n in names if n in shared}


def _in_names(nc):
    import concourse.mybir as _mb
    out = []
    for alloc in nc.m.functions[0].allocations:
        if isinstance(alloc, _mb.MemoryLocationSet) and \
                alloc.kind == "ExternalInput":
            nm = alloc.memorylocations[0].name
            if nm != "partition_id":
                out.append(nm)
    return out


USE_FUSED = True


def kernel(**inputs):
    if USE_FUSED:
        return kernel_fused(**inputs)
    return kernel_split(**inputs)


def kernel_fused(**inputs):
    shared = host_prep(inputs)
    x = np.asarray(inputs["L0_attn_x"])
    nc = build_fused()
    names = _in_names(nc)
    maps = []
    for b in range(NCORES):
        m = dict(_pick(shared, names))
        m["xT"] = np.ascontiguousarray(x[b].T)
        mask = np.zeros((8, 1), np.float32)
        mask[:b] = 1.0
        m["mymask"] = mask
        maps.append(m)
    res = run_bass_kernel_spmd(nc, maps, core_ids=list(range(NCORES)))
    out = np.empty((B, S, D), np.float32)
    for c in range(NCORES):
        out[:, c * 256:(c + 1) * 256, :] = \
            np.asarray(res.results[c]["outT"]).reshape(8, D, 256) \
            .transpose(0, 2, 1)
    return out


def kernel_split(**inputs):
    shared = host_prep(inputs)
    x = np.asarray(inputs["L0_attn_x"])
    xT = [np.ascontiguousarray(x[b].T) for b in range(NCORES)]
    masks = [np.zeros((8, 1), np.float32) for _ in range(NCORES)]
    for b in range(NCORES):
        masks[b][:b] = 1.0

    ncA = build_progA()
    namesA = _in_names(ncA)
    mapsA = []
    for b in range(NCORES):
        m = dict(_pick(shared, namesA))
        m["xT"] = xT[b]
        mapsA.append(m)
    resA = run_bass_kernel_spmd(ncA, mapsA, core_ids=list(range(NCORES)))
    cnts_all = np.ascontiguousarray(
        np.concatenate([resA.results[b]["cnts"].reshape(1, 8)
                        for b in range(NCORES)], 0))

    ncB = build_progB(with_attn=True)
    namesB = _in_names(ncB)
    mapsB = []
    for b in range(NCORES):
        m = dict(_pick(shared, namesB))
        m["xT"] = xT[b]
        m["cnts_all"] = cnts_all
        m["mymask"] = masks[b]
        mapsB.append(m)
    resB = run_bass_kernel_spmd(ncB, mapsB, core_ids=list(range(NCORES)))
    x1T = [resB.results[b]["x1T"] for b in range(NCORES)]

    x1c = [np.ascontiguousarray(
        np.stack([x1T[s][:, c * 256:(c + 1) * 256] for s in range(8)]))
        for c in range(NCORES)]

    ncC1 = build_progC1()
    namesC1 = _in_names(ncC1)
    mapsC1 = []
    for c in range(NCORES):
        m = dict(_pick(shared, namesC1))
        m["x1c"] = x1c[c]
        mapsC1.append(m)
    resC1 = run_bass_kernel_spmd(ncC1, mapsC1, core_ids=list(range(NCORES)))

    ncC2 = build_progC2()
    namesC2 = _in_names(ncC2)
    mapsC2 = []
    for c in range(NCORES):
        m = dict(_pick(shared, namesC2))
        m["x1c"] = x1c[c]
        m["xpartT"] = resC1.results[c]["xpartT"]
        m["ridx_i"] = resC1.results[c]["ridx_o"]
        m["route_i"] = resC1.results[c]["route_o"]
        m["disp1"] = resC1.results[c]["disp1"]
        mapsC2.append(m)
    resC2 = run_bass_kernel_spmd(ncC2, mapsC2, core_ids=list(range(NCORES)))

    out = np.empty((B, S, D), np.float32)
    for c in range(NCORES):
        out[:, c * 256:(c + 1) * 256, :] = \
            np.asarray(resC2.results[c]["outT"]).reshape(8, D, 256) \
            .transpose(0, 2, 1)
    return out
